# version 07
# baseline (speedup 1.0000x reference)
import numpy as np
from contextlib import ExitStack

import concourse.bass as bass
import concourse.mybir as mybir
from concourse.bass_utils import run_bass_kernel_spmd

F32 = mybir.dt.float32
BF16 = mybir.dt.bfloat16
AF = mybir.ActivationFunctionType
OP = mybir.AluOpType
AX = mybir.AxisListType

N_CORES = 8
P = 128
D = 2048
INNER = 8192
TOK = 1024
NT = TOK // P
H = 16
DK = 128
EPS_LN = 1e-5
EPS_Q = 1e-5
EPS_W = 1e-5
C32 = float(1.5 * 2.0 ** 23)
C64 = float(1.5 * 2.0 ** 52)
MS1 = 256
NSPAN1 = INNER // MS1
NSLAB2 = D // P

SEMS = ["xA", "xB", "wA", "wB", "mA", "mB", "s1A", "s1B", "s2A", "s2B",
        "xpA", "xpB", "txA", "txB", "toA", "toB", "t1A", "t1B", "t2A", "t2B",
        "w1A", "w1B", "w2A", "w2B", "sp",
        "ob0", "ob1", "ob2", "ob3",
        "st0", "st1", "st2", "st3", "v", "vv", "sc", "g", "t"]


class Plan:
    def __init__(self):
        self.ops = []
        self.sems = {}

    def sem(self, name):
        self.sems.setdefault(name, 0)

    def inc(self, name, n=1):
        self.sems[name] = self.sems.get(name, 0) + n
        return (name, n)

    def val(self, name):
        return self.sems.get(name, 0)

    def op(self, engine, fn, waits=(), incs=()):
        self.ops.append((engine, fn,
                         [w for w in waits if w is not None and w[1] > 0],
                         list(incs)))

    def emit(self, engine_name, eng, handles):
        for e, fn, waits, incs in self.ops:
            if e != engine_name:
                continue
            for (s, v) in waits:
                eng.wait_ge(handles[s], v)
            ins = fn(eng)
            for (s, amt) in incs:
                ins.then_inc(handles[s], amt)


def build_program(tap=None):
    nc = bass.Bass("TRN2", target_bir_lowering=False,
                   detect_race_conditions=False,
                   dynamic_dma_scratch_size=2048)

    x_in = nc.dram_tensor("x_in", [TOK, D], F32, kind="ExternalInput").ap()
    w_attn = {
        "q": nc.dram_tensor("wq_in", [D, D], F32, kind="ExternalInput").ap(),
        "k": nc.dram_tensor("wk_in", [D, D], F32, kind="ExternalInput").ap(),
        "v": nc.dram_tensor("wv_in", [D, D], F32, kind="ExternalInput").ap(),
        "o": nc.dram_tensor("wo_in", [D, D], F32, kind="ExternalInput").ap(),
    }
    w1_in = nc.dram_tensor("w1_in", [INNER, D], F32, kind="ExternalInput").ap()
    w2_in = nc.dram_tensor("w2_in", [D, INNER], F32, kind="ExternalInput").ap()
    out_hbm = nc.dram_tensor("out", [TOK, D], F32, kind="ExternalOutput").ap()
    xp_hbm = nc.dram_tensor("xp_spill", [NSLAB2, NT, P, DK], BF16,
                            kind="Internal").ap()

    pl = Plan()
    for s in SEMS:
        pl.sem(s)

    with ExitStack() as ctx:
        def sbt(name, shape, dt=BF16):
            return ctx.enter_context(nc.sbuf_tensor(name, shape, dt))

        AR14 = sbt("AR14", [P, 4, NT, D])
        AR1 = AR14[:, 0]
        AR2 = AR14[:, 1]
        AR3 = AR14[:, 2]
        AR4 = AR14[:, 3]
        AR8 = sbt("AR8", [P, 28672])
        F1 = sbt("F1", [P, D], F32)
        F2 = sbt("F2", [P, D], F32)
        F3 = sbt("F3", [P, D], F32)
        F5 = sbt("F5", [P, NT, H, H], F32)
        OSB = sbt("OSB", [P, 2, 256], F32)
        COLS = sbt("COLS", [P, 16], F32)
        TC = sbt("TC", [P, NT, 12], F32)
        MEANP = sbt("MEANP", [P, 128], F32)
        JNK = sbt("JNK", [P, 256], BF16)
        ONESF = sbt("ONESF", [P, 128], F32)

        PS = [ctx.enter_context(nc.psum_tensor(f"ps{i}", [P, 512], F32))
              for i in range(8)]

        handles = {s: ctx.enter_context(nc.semaphore(s)) for s in pl.sems}
        block = ctx.enter_context(nc.Block())

        cC, cProbe, cFlag, cTmp = 0, 1, 2, 3
        cM1, cSw1, cM1o127 = 4, 5, 6
        cM2, cSw2, cM2o127 = 7, 8, 9
        cT2, cCp1, cCm1, cZero = 10, 11, 12, 13
        cOne, cNegC = 14, 15

        def col(i):
            return COLS[:, i:i + 1]

        def tc(t, i):
            return TC[:, t, i:i + 1]

        XNT = AR1[:].rearrange("p t (c f) -> p t c f", c=16)
        XT1 = XNT
        q_bf = AR2
        k_bf = AR3
        v_bf = AR2
        o_bf = AR3

        def wqtr(i):
            return AR4[:, 4 * (i % 2):4 * (i % 2) + 4, :].rearrange(
                "p t f -> p (t f)").rearrange("p (c g) -> p c g", c=16)

        def ar8(lo, hi):
            return AR8[:, lo:hi]

        xn_buf = [ar8(0, 2048), ar8(2048, 4096)]
        tmp_buf = [ar8(4096, 6144), ar8(6144, 8192)]
        junk1k = ar8(8192, 9216)
        junkP0 = ar8(9216, 11264)

        def xp_t(t):
            return ar8(t * 2048, (t + 1) * 2048)

        XPV = AR8[:, 0:16384].rearrange("p (t f) -> p t f", t=NT)
        xint_buf = [ar8(16384, 18432), ar8(18432, 20480)]
        junkP5 = ar8(20480, 22528)
        W1Q = ar8(16384, 20480).rearrange("p (s f) -> p s f", s=2)
        W1T = [ar8(20480, 24576).rearrange("p (c f) -> p c f", c=16),
               ar8(24576, 28672).rearrange("p (c f) -> p c f", c=16)]

        def h2_tile(t):
            if t < 2:
                return AR2[:, 4 * t:4 * (t + 1), :].rearrange(
                    "p a b -> p (a b)")
            if t < 4:
                return AR3[:, 4 * (t - 2):4 * (t - 1), :].rearrange(
                    "p a b -> p (a b)")
            if t < 6:
                return AR4[:, 4 * (t - 4):4 * (t - 3), :].rearrange(
                    "p a b -> p (a b)")
            return ar8(8192 * (t - 6), 8192 * (t - 5))

        def xt2_tile(t):
            if t < 2:
                base = AR1[:, 4 * t:4 * (t + 1), :].rearrange(
                    "p a b -> p (a b)")
            else:
                base = h2_tile(t - 2)
            return base.rearrange("p (c f) -> p c f", c=64)

        W2T = [ar8(0, 8192).rearrange("p (c f) -> p c f", c=64),
               ar8(8192, 16384).rearrange("p (c f) -> p c f", c=64)]
        PQ = [ar8(16384, 18432), ar8(18432, 20480)]
        XPS = [ar8(20480, 21504).rearrange("p (t f) -> p t f", t=NT),
               ar8(21504, 22528).rearrange("p (t f) -> p t f", t=NT)]
        XT2V = AR14[:].rearrange("p a t d -> p (a t d)").rearrange(
            "p (T c f) -> p T c f", T=NT, c=64)
        MSTG = ar8(22528, 24576).rearrange("p (s c f) -> p s c f", s=4, c=4)
        TSTG = ar8(24576, 26624).rearrange("p (s c f) -> p s c f", s=4, c=4)

        def gap(eng="vector"):
            pl.op(eng, lambda e: e.memset(col(cT2), 0.0))

        def vchain(fn, extra_waits=(), incs=()):
            pl.op("vector", fn,
                  waits=[("vv", pl.val("vv"))] + list(extra_waits),
                  incs=[pl.inc("vv")])
            if incs:
                pl.op("vector", lambda v: v.memset(col(cT2), 0.0),
                      waits=[("vv", pl.val("vv"))], incs=list(incs))

        def dma(sem, fn, waits=(), eng="sync"):
            pl.op(eng, fn, waits=waits, incs=[pl.inc(sem, 16)])
            return (sem, pl.val(sem))

        def tappoint(name, rows_src):
            if tap != name:
                return
            for t, (ap, w) in enumerate(rows_src):
                W = min(ap.free_size(), 2048)
                pl.op("vector", lambda v, ap=ap, W=W: v.tensor_copy(
                    F3[:, 0:W], ap),
                    waits=list(w) + [("st0", pl.val("st0"))],
                    incs=[pl.inc("v")])
                vv_ = pl.val("v")
                dma("st0", lambda s, t=t, W=W: s.dma_start(
                    out=out_hbm[t * P:(t + 1) * P, 0:W], in_=F3[:, 0:W]),
                    waits=[("v", vv_)])
            pl.op("sync", lambda s: s.nop(),
                  waits=[("st0", pl.val("st0"))])
            pl._tap_cut = len(pl.ops)

        vchain(lambda v: v.memset(ONESF[:], 1.0))
        vchain(lambda v: v.memset(col(cProbe), 0.25))
        vchain(lambda v: v.tensor_scalar(
            col(cProbe), col(cProbe), C32, C32, OP.add, OP.subtract))
        vchain(lambda v: v.tensor_scalar(
            col(cFlag), col(cProbe), 0.1, None, OP.is_gt))
        vchain(lambda v: v.tensor_scalar(
            col(cC), col(cFlag), C64, None, OP.mult))
        vchain(lambda v: v.tensor_scalar(
            col(cTmp), col(cFlag), -C32, C32, OP.mult, OP.add))
        vchain(lambda v: v.tensor_tensor(
            col(cC), col(cC), col(cTmp), OP.add))
        vchain(lambda v: v.tensor_scalar(
            col(cCp1), col(cC), 1.0, None, OP.add))
        vchain(lambda v: v.tensor_scalar(
            col(cCm1), col(cC), -1.0, None, OP.add))
        vchain(lambda v: v.memset(col(cZero), 0.0))
        vchain(lambda v: v.memset(col(cOne), 1.0))
        vchain(lambda v: v.tensor_scalar(
            col(cNegC), col(cC), -1.0, None, OP.mult))
        vchain(lambda v: v.memset(JNK[:], 0.0), incs=[pl.inc("v")])
        vC = pl.val("v")

        def xstage(t):
            return F1 if t % 2 == 0 else F2

        def xsem(t):
            return "xA" if t % 2 == 0 else "xB"

        def txsem(t):
            return "txA" if t % 2 == 0 else "txB"

        v_xn = {}
        tr_xnt = {}
        x_ld = {}
        for t in range(NT):
            w = [("v", v_xn[t - 2])] if (t - 2) in v_xn else []
            x_ld[t] = dma(xsem(t), lambda s, t=t: s.dma_start(
                out=xstage(t)[:], in_=x_in[t * P:(t + 1) * P, :]), waits=w)
            vchain(lambda v, t=t: v.tensor_reduce(
                tc(t, 0), xstage(t)[:], AX.X, OP.add),
                extra_waits=[x_ld[t], ("v", vC)])
            vchain(lambda v, t=t: v.scalar_tensor_tensor(
                junkP0, xstage(t)[:], 1.0, xstage(t)[:], OP.mult, OP.mult,
                accum_out=tc(t, 3)))
            vchain(lambda v, t=t: v.tensor_scalar(
                tc(t, 0), tc(t, 0), 1.0 / D, None, OP.mult))
            vchain(lambda v, t=t: v.tensor_scalar(
                tc(t, 3), tc(t, 3), 1.0 / D, None, OP.mult))
            vchain(lambda v, t=t: v.tensor_tensor(
                tc(t, 2), tc(t, 0), tc(t, 0), OP.mult))
            vchain(lambda v, t=t: v.tensor_tensor(
                tc(t, 3), tc(t, 3), tc(t, 2), OP.subtract))
            vchain(lambda v, t=t: v.tensor_scalar(
                tc(t, 3), tc(t, 3), EPS_LN, None, OP.add))
            vchain(lambda v, t=t: v.reciprocal(tc(t, 3), tc(t, 3)),
                   incs=[pl.inc("v")])
            vr = pl.val("v")
            pl.op("scalar", lambda s, t=t: s.activation(
                tc(t, 1), tc(t, 3), AF.Sqrt), waits=[("v", vr)],
                incs=[pl.inc("sc")])
            scr = pl.val("sc")
            vchain(lambda v, t=t: v.tensor_tensor(
                tc(t, 2), tc(t, 0), tc(t, 1), OP.mult),
                extra_waits=[("sc", scr)])
            wtr = [tr_xnt[t - 2]] if (t - 2) in tr_xnt else []
            vchain(lambda v, t=t: v.tensor_scalar(
                xn_buf[t % 2], xstage(t)[:], tc(t, 1), tc(t, 2),
                OP.mult, OP.subtract), extra_waits=wtr, incs=[pl.inc("v")])
            v_xn[t] = pl.val("v")
            tr_xnt[t] = dma(txsem(t), lambda s, t=t: s.dma_start_transpose(
                XNT[:, t], xn_buf[t % 2].rearrange("p (c f) -> p c f", c=16)),
                waits=[("v", v_xn[t])])

        tappoint("xnt", [(XNT[:, t].rearrange("p c f -> p (c f)"),
                          [tr_xnt[t]]) for t in range(NT)])

        mean_state = {"w1": 0, "w2": 0}
        mean_sc = {}

        junk_f5 = F5[:].rearrange("p t h k -> p (t h k)")[:, 0:1024]

        def emit_mean_piece(tag, whbm, stage_buf, halves_per_row, first_wait):
            i = mean_state[tag]
            mean_state[tag] += 1
            row0 = i // halves_per_row
            cof = (i % halves_per_row) * 1024
            half = stage_buf[:, (i % 2) * 1024:(i % 2) * 1024 + 1024]
            junk = junk1k if tag == "w1" else junk_f5
            sem = "mA" if i % 2 == 0 else "mB"
            wv = []
            if (tag, i - 2) in mean_sc:
                wv.append(("sc", mean_sc[(tag, i - 2)]))
            elif i < 2 and first_wait is not None:
                wv.append(first_wait)
            ld = dma(sem, lambda s, whbm=whbm, row0=row0, cof=cof, half=half:
                     s.dma_start(out=half,
                                 in_=whbm[row0 * P:(row0 + 1) * P,
                                          cof:cof + 1024]), waits=wv)
            pl.op("scalar", lambda s, half=half, i=i, junk=junk: s.activation(
                junk, half, AF.Abs, accum_out=MEANP[:, i:i + 1]),
                waits=[ld], incs=[pl.inc("sc")])
            mean_sc[(tag, i)] = pl.val("sc")

        def finalize_mean(tag, cM, cSw, cMo127, extra_mm_waits=()):
            vchain(lambda v, cM=cM: v.tensor_reduce(
                col(cM), MEANP[:, 0:128], AX.X, OP.add),
                extra_waits=[("sc", mean_sc[(tag, 127)])],
                incs=[pl.inc("v")])
            vv_ = pl.val("v")
            pl.op("tensor", lambda e, cM=cM: e.matmul(
                PS[0][:, 0:1], ONESF[:], col(cM), start=True, stop=True),
                waits=[("v", vv_)] + list(extra_mm_waits),
                incs=[pl.inc("t")])
            tv = pl.val("t")
            vchain(lambda v, cM=cM: v.tensor_copy(
                col(cM), PS[0][:, 0:1]), extra_waits=[("t", tv)])
            vchain(lambda v, cM=cM: v.tensor_scalar(
                col(cM), col(cM), 1.0 / (INNER * D), None, OP.mult))
            vchain(lambda v, cM=cM: v.tensor_scalar(
                col(cM), col(cM), EPS_W, None, OP.max))
            vchain(lambda v, cM=cM, cSw=cSw: v.reciprocal(
                col(cSw), col(cM)))
            vchain(lambda v, cM=cM, cMo127=cMo127: v.tensor_scalar(
                col(cMo127), col(cM), 1.0 / 127.0, None, OP.mult))
            vchain(lambda v: v.memset(JNK[:], 0.0), incs=[pl.inc("v")])
            return pl.val("v")

        cvt_seq = []
        qtr_mm_done = {}
        sc_evac = {}
        t_mm = {}
        bank_free = {}
        g_scores = {}

        def attn_weight_phase(wname, stat_tr, dst, wi,
                              extra_evac_wait=None, bg=None):
            whbm = w_attn[wname]
            for qtr in range(4):
                gq = wi * 4 + qtr
                buf = wqtr(gq % 2)
                for i8 in range(8):
                    slot = i8 % 2
                    fsl = F3[:, slot * 1024:(slot + 1) * 1024]
                    sem = "wA" if slot == 0 else "wB"
                    wv = []
                    if len(cvt_seq) >= 2:
                        wv.append(("sc", cvt_seq[-2]))
                    if i8 == 0 and (gq - 2) in qtr_mm_done:
                        wv.append(("t", qtr_mm_done[gq - 2]))
                    src = whbm[i8 * 256:(i8 + 1) * 256,
                               qtr * 512:(qtr + 1) * 512]
                    ld = dma(sem, lambda s, src=src, fsl=fsl: s.dma_start(
                        out=fsl.rearrange("p (a g) -> p a g", a=2),
                        in_=src.rearrange("(a p) g -> p a g", p=128)),
                        waits=wv)
                    pl.op("scalar", lambda s, buf=buf, i8=i8, fsl=fsl:
                          s.copy(buf[:, 2 * i8:2 * i8 + 2, :].rearrange(
                              "p a g -> p (a g)"), fsl),
                          waits=[ld], incs=[pl.inc("sc")])
                    cvt_seq.append(pl.val("sc"))
                cvt_all = pl.val("sc")
                if bg is not None:
                    bg()
                for t in range(NT):
                    wts = [("sc", cvt_all), stat_tr[t]]
                    if t in bank_free:
                        wts.append(bank_free[t])
                    for c in range(16):
                        pl.op("tensor", lambda e, t=t, c=c, buf=buf:
                              e.matmul(PS[t][:], XNT[:, t, c, :],
                                       buf[:, c, :],
                                       start=(c == 0), stop=(c == 15)),
                              waits=wts if c == 0 else [],
                              incs=[pl.inc("t")] if c == 15 else [])
                    t_mm[(wname, qtr, t)] = pl.val("t")
                    wev = [("t", t_mm[(wname, qtr, t)])]
                    if extra_evac_wait is not None:
                        wev.extend(extra_evac_wait(t))
                    pl.op("scalar", lambda s, t=t, qtr=qtr, dst=dst:
                          s.copy(dst[:, t, qtr * 512:(qtr + 1) * 512],
                                 PS[t][:]),
                          waits=wev, incs=[pl.inc("sc")])
                    bank_free[t] = ("sc", pl.val("sc"))
                    if qtr == 3:
                        sc_evac[(wname, t)] = pl.val("sc")
                qtr_mm_done[gq] = pl.val("t")

        def mean_w1_bg():
            for _ in range(12):
                if mean_state["w1"] < 128:
                    emit_mean_piece("w1", w1_in, F2, 2, ("v", v_xn[NT - 1]))

        bg_count = [0]

        def bg_hook():
            bg_count[0] += 1
            if bg_count[0] >= 2:
                mean_w1_bg()

        attn_weight_phase("q", tr_xnt, q_bf, 0, bg=bg_hook)
        tappoint("q", [(q_bf[:, t, :], [("sc", sc_evac[("q", t)])])
                       for t in range(NT)])

        attn_weight_phase("k", tr_xnt, k_bf, 1, bg=bg_hook)
        tappoint("k", [(k_bf[:, t, :], [("sc", sc_evac[("k", t)])])
                       for t in range(NT)])

        for t in range(NT):
            for hp in range(H // 2):
                for h in (2 * hp, 2 * hp + 1):
                    wts = []
                    if h == 0:
                        wts = [("sc", sc_evac[("q", t)]),
                               ("sc", sc_evac[("k", t)])]
                    qsl = q_bf[:, t, h * DK:(h + 1) * DK]
                    pl.op("vector", lambda v, t=t, h=h, qsl=qsl:
                          v.tensor_tensor(
                              tmp_buf[h % 2],
                              qsl.unsqueeze(1).broadcast_to([P, 16, DK]),
                              k_bf[:, t, :].rearrange("p (c f) -> p c f",
                                                      c=16),
                              OP.mult), waits=wts)
                for h in (2 * hp, 2 * hp + 1):
                    last = (h == H - 1)
                    pl.op("vector", lambda v, t=t, h=h: v.tensor_reduce(
                        F5[:, t, h, :], tmp_buf[h % 2].rearrange(
                            "p (c f) -> p c f", c=16), AX.X, OP.add),
                        incs=[pl.inc("v")] if last else [])
            g_scores[(t, H - 1)] = ("v", pl.val("v"))

        tappoint("scores", [(F5[:, t].rearrange("p a b -> p (a b)"),
                             [g_scores[(t, H - 1)]]) for t in range(NT)])

        attn_weight_phase("v", tr_xnt, v_bf, 2,
                          extra_evac_wait=lambda t: [g_scores[(t, H - 1)]],
                          bg=bg_hook)

        o3_tmp = [tmp_buf[0].rearrange("p (h f) -> p h f", h=16),
                  tmp_buf[1].rearrange("p (h f) -> p h f", h=16),
                  junkP0[:, 0:2048].rearrange("p (h f) -> p h f", h=16)]
        v_odone = {}
        for t in range(NT):
            o3 = o_bf[:, t, :].rearrange("p (h f) -> p h f", h=16)

            def omul(kh, dst, t=t):
                return lambda v, kh=kh, dst=dst, t=t: v.tensor_tensor(
                    dst,
                    F5[:, t, :, kh:kh + 1].broadcast_to([P, H, DK]),
                    v_bf[:, t, kh * DK:(kh + 1) * DK].unsqueeze(1)
                    .broadcast_to([P, H, DK]),
                    OP.mult)

            def oadd(kh, t=t):
                return lambda v, kh=kh, t=t: v.tensor_tensor(
                    o_bf[:, t, :].rearrange("p (h f) -> p h f", h=16),
                    o_bf[:, t, :].rearrange("p (h f) -> p h f", h=16),
                    o3_tmp[kh % 3], OP.add)

            pl.op("vector", omul(0, o3),
                  waits=[g_scores[(t, H - 1)], ("sc", sc_evac[("v", t)])])
            pl.op("vector", omul(1, o3_tmp[1]))
            pl.op("vector", omul(2, o3_tmp[2]))
            for kh in range(3, H):
                pl.op("vector", oadd(kh - 2))
                pl.op("vector", omul(kh, o3_tmp[kh % 3]))
            pl.op("vector", oadd(H - 2))
            pl.op("vector", oadd(H - 1), incs=[pl.inc("v")])
            v_odone[t] = ("v", pl.val("v"))

        tappoint("o", [(o_bf[:, t, :], [v_odone[t]]) for t in range(NT)])

        t_vdone = t_mm[("v", 3, NT - 1)]
        tr_ot = {}
        for t in range(NT):
            w = [v_odone[t], ("t", t_vdone)]
            if (t - 2) in tr_ot:
                w.append(tr_ot[t - 2])
            tr_ot[t] = dma("toA" if t % 2 == 0 else "toB",
                           lambda s, t=t: s.dma_start_transpose(
                               XNT[:, t], o_bf[:, t, :].rearrange(
                                   "p (c f) -> p c f", c=16)), waits=w)

        def wo_gate(t):
            w = []
            if t in (2, 3, 4, 5):
                w.append(v_odone[NT - 1])
            if t not in (2, 3):
                w.append(("sc", mean_sc[("w1", 127)]))
            return w
        attn_weight_phase("o", tr_ot, XPV, 3, extra_evac_wait=wo_gate)

        v_w1s = finalize_mean("w1", cM1, cSw1, cM1o127,
                              extra_mm_waits=[("sc", sc_evac[("o", 0)])])

        vq1_done = {}
        tr_xt1 = {}
        for t in range(NT):
            xps = xp_t(t)
            fstage = F1 if t % 2 == 0 else F2
            wld = []
            if (t - 2) in vq1_done:
                wld.append(("v", vq1_done[t - 2]))
            elif t == 0:
                wld.append(("v", v_xn[NT - 2]))
            else:
                wld.append(("sc", mean_sc[("w1", 127)]))
            ld = dma(xsem(t), lambda s, t=t: s.dma_start(
                out=xstage(t)[:], in_=x_in[t * P:(t + 1) * P, :]), waits=wld)
            pl.op("vector", lambda v, xps=xps, fstage=fstage:
                  v.tensor_tensor(xps, xps, fstage[:], OP.add),
                  waits=[ld, ("sc", sc_evac[("o", t)])],
                  incs=[pl.inc("v")])
            vadd = pl.val("v")
            dma("sp", lambda s, t=t, xps=xps: s.dma_start(
                out=xp_hbm.transpose([2, 0, 1, 3])[:, :, t, :],
                in_=xps.rearrange("p (c f) -> p c f", c=16)),
                waits=[("v", vadd), ("sp", pl.val("sp"))])
            vchain(lambda v, xps=xps, t=t: v.tensor_reduce(
                tc(t, 0), xps, AX.X, OP.add), extra_waits=[("v", vadd)])
            vchain(lambda v, xps=xps, t=t: v.scalar_tensor_tensor(
                junkP5, xps, 1.0, xps, OP.mult, OP.mult, accum_out=tc(t, 3)))
            vchain(lambda v, t=t: v.tensor_scalar(
                tc(t, 0), tc(t, 0), 1.0 / D, None, OP.mult))
            vchain(lambda v, t=t: v.tensor_scalar(
                tc(t, 3), tc(t, 3), 1.0 / D, None, OP.mult))
            vchain(lambda v, t=t: v.tensor_tensor(
                tc(t, 2), tc(t, 0), tc(t, 0), OP.mult))
            vchain(lambda v, t=t: v.tensor_tensor(
                tc(t, 3), tc(t, 3), tc(t, 2), OP.subtract))
            vchain(lambda v, t=t: v.tensor_scalar(
                tc(t, 3), tc(t, 3), EPS_LN, None, OP.add))
            vchain(lambda v, t=t: v.reciprocal(tc(t, 3), tc(t, 3)),
                   incs=[pl.inc("v")])
            vr = pl.val("v")
            pl.op("scalar", lambda s, t=t: s.activation(
                tc(t, 1), tc(t, 3), AF.Sqrt), waits=[("v", vr)],
                incs=[pl.inc("sc")])
            scr = pl.val("sc")
            vchain(lambda v, t=t: v.tensor_tensor(
                tc(t, 2), tc(t, 0), tc(t, 1), OP.mult),
                extra_waits=[("sc", scr)])
            vchain(lambda v, xps=xps, fstage=fstage, t=t:
                   v.tensor_scalar(fstage[:], xps, tc(t, 1), tc(t, 2),
                                   OP.mult, OP.subtract))
            vchain(lambda v, fstage=fstage, t=t:
                   v.scalar_tensor_tensor(junkP5, fstage[:], 1.0, fstage[:],
                                          OP.mult, OP.mult,
                                          accum_out=tc(t, 3)))
            vchain(lambda v, fstage=fstage, t=t: v.tensor_reduce(
                tc(t, 4), fstage[:], AX.X, OP.max,
                apply_absolute_value=True))
            vchain(lambda v, t=t: v.tensor_scalar(
                tc(t, 3), tc(t, 3), 1e-24, None, OP.max))
            vchain(lambda v, t=t: v.reciprocal(tc(t, 3), tc(t, 3)),
                   incs=[pl.inc("v")])
            vr2 = pl.val("v")
            pl.op("scalar", lambda s, t=t: s.activation(
                tc(t, 5), tc(t, 3), AF.Sqrt, scale=float(D)),
                waits=[("v", vr2)], incs=[pl.inc("sc")])
            scf = pl.val("sc")
            vchain(lambda v, t=t: v.tensor_tensor(
                tc(t, 6), tc(t, 4), tc(t, 5), OP.mult),
                extra_waits=[("sc", scf), ("v", v_w1s)])
            vchain(lambda v, t=t: v.tensor_scalar(
                tc(t, 6), tc(t, 6), EPS_Q, None, OP.max))
            vchain(lambda v, t=t: v.reciprocal(tc(t, 7), tc(t, 6)))
            vchain(lambda v, t=t: v.tensor_tensor(
                tc(t, 8), tc(t, 6), col(cM1o127), OP.mult))
            vchain(lambda v, t=t: v.tensor_scalar(
                tc(t, 7), tc(t, 7), 127.0, None, OP.mult))
            vchain(lambda v, t=t: v.tensor_tensor(
                tc(t, 7), tc(t, 7), tc(t, 5), OP.mult))
            vchain(lambda v, fstage=fstage, t=t: v.tensor_scalar(
                fstage[:], fstage[:], tc(t, 7), col(cC), OP.mult, OP.add))
            wtr = [tr_xt1[t - 2]] if (t - 2) in tr_xt1 else []
            vchain(lambda v, fstage=fstage, t=t: v.tensor_scalar(
                xint_buf[t % 2], fstage[:], col(cC), None, OP.subtract),
                extra_waits=wtr, incs=[pl.inc("v")])
            vq1_done[t] = pl.val("v")
            tr_xt1[t] = dma("t1A" if t % 2 == 0 else "t1B",
                            lambda s, t=t: s.dma_start_transpose(
                                XT1[:, t], xint_buf[t % 2].rearrange(
                                    "p (c f) -> p c f", c=16)),
                            waits=[("v", vq1_done[t]),
                                   ("t", t_mm[("o", 3, t)]), tr_ot[NT - 1],
                                   tr_ot[NT - 2]])
        sp_all = ("sp", pl.val("sp"))

        tappoint("xp", [(xp_t(t), [("v", vq1_done[t])]) for t in range(NT)])
        tappoint("xt1", [(XT1[:, t].rearrange("p c f -> p (c f)"),
                          [tr_xt1[t]]) for t in range(NT)])
        tappoint("tc", [(TC[:].rearrange("p a b -> p (a b)"),
                         [("v", vq1_done[NT - 1])])])

        h2_gate = {}
        for t in range(NT):
            if t < 2:
                h2_gate[t] = [v_odone[NT - 1]]
            elif t < 4:
                h2_gate[t] = [tr_ot[NT - 1], tr_ot[NT - 2]]
            elif t < 6:
                h2_gate[t] = [("t", qtr_mm_done[15])]
            else:
                h2_gate[t] = [sp_all, ("v", vq1_done[NT - 1])]

        xt1_all = [tr_xt1[NT - 1], tr_xt1[NT - 2]]
        g_tern1 = {}
        tr_w1 = {}
        t_mm1 = {}
        ev1 = {}
        gelu2_sc = {}
        for span in range(NSPAN1):
            for j in range(2):
                slab = 2 * span + j
                stg = F3 if j == 0 else F1
                sem = "s1A" if j == 0 else "s1B"
                wv = []
                if slab >= 2:
                    wv.append(("g", g_tern1[slab - 2]))
                elif j == 0:
                    wv.append(("sc", cvt_seq[-1]))
                    wv.append(("v", v_w1s))
                else:
                    wv.append(("v", vq1_done[NT - 2]))
                ld = dma(sem, lambda s, slab=slab, stg=stg: s.dma_start(
                    out=stg[:], in_=w1_in[slab * P:(slab + 1) * P, :]),
                    waits=wv)
                wq_gate = [ld]
                if span == 0:
                    wq_gate += xt1_all
                if span >= 1:
                    wq_gate.append(tr_w1[2 * (span - 1) + j])
                pl.op("gpsimd", lambda g, stg=stg: g.tensor_scalar(
                    stg[:], stg[:], col(cSw1), col(cC), OP.mult, OP.add),
                    waits=wq_gate)
                gap("gpsimd")
                pl.op("gpsimd", lambda g, stg=stg: g.tensor_scalar(
                    stg[:], stg[:], col(cCp1), col(cCm1), OP.min, OP.max))
                gap("gpsimd")
                pl.op("gpsimd", lambda g, stg=stg, j=j: g.tensor_scalar(
                    W1Q[:, j, :], stg[:], col(cOne), col(cNegC),
                    OP.mult, OP.add),
                    incs=[pl.inc("g")])
                g_tern1[slab] = pl.val("g")
                wtr = [("g", g_tern1[slab])]
                if span >= 2:
                    wtr.append(("t", t_mm1[(span - 2, 1)]))
                if span == 0:
                    wtr.append(("v", vq1_done[NT - 1]))
                if j == 1:
                    wtr.append(tr_w1[2 * span])
                tr_w1[slab] = dma(
                    "w1A" if span % 2 == 0 else "w1B",
                    lambda s, span=span, j=j: s.dma_start_transpose(
                        W1T[span % 2][:, :, j * 128:(j + 1) * 128],
                        W1Q[:, j, :].rearrange("p (c f) -> p c f", c=16)),
                    waits=wtr)
            for _ in range(4):
                if mean_state["w2"] < 128:
                    emit_mean_piece("w2", w2_in, F2, 8,
                                    ("v", vq1_done[NT - 1]))
            for grp in range(2):
                wts = [tr_w1[2 * span + 1]]
                if (span - 1, grp) in ev1:
                    wts.append(("sc", ev1[(span - 1, grp)]))
                for ti in range(4):
                    t = grp * 4 + ti
                    w0 = list(wts) if ti == 0 else []
                    if span == 0:
                        w0.append(tr_xt1[t])
                    for c in range(16):
                        pl.op("tensor", lambda e, t=t, c=c, span=span:
                              e.matmul(PS[t][:, 0:MS1], XT1[:, t, c, :],
                                       W1T[span % 2][:, c, :],
                                       start=(c == 0), stop=(c == 15)),
                              waits=w0 if c == 0 else [],
                              incs=[pl.inc("t")] if (ti == 3 and c == 15)
                              else [])
                t_mm1[(span, grp)] = pl.val("t")
                for pair in range(2):
                    ts_ = [grp * 4 + 2 * pair, grp * 4 + 2 * pair + 1]
                    for t in ts_:
                        wev = [("t", t_mm1[(span, grp)])]
                        if span == 0:
                            wev += h2_gate[t]
                        pl.op("scalar", lambda s, t=t: s.activation(
                            OSB[:, t % 2, :], PS[t][:, 0:MS1], AF.Gelu,
                            scale=tc(t, 8)), waits=wev)
                    for t in ts_:
                        pl.op("scalar", lambda s, t=t, span=span:
                              s.activation(
                                  h2_tile(t)[:, span * MS1:(span + 1) * MS1],
                                  OSB[:, t % 2, :], AF.Gelu),
                              incs=[pl.inc("sc")])
                        gelu2_sc[(span, t)] = pl.val("sc")
                ev1[(span, grp)] = pl.val("sc")
            for t in range(NT):
                h2s = h2_tile(t)[:, span * MS1:(span + 1) * MS1]
                acc = tc(t, 10) if span == 0 else tc(t, 11)
                pl.op("vector", lambda v, h2s=h2s, acc=acc:
                      v.scalar_tensor_tensor(JNK[:], h2s, 1.0, h2s,
                                             OP.mult, OP.mult,
                                             accum_out=acc),
                      waits=[("sc", gelu2_sc[(span, t)])])
            for t in range(NT):
                h2s = h2_tile(t)[:, span * MS1:(span + 1) * MS1]
                dstm = tc(t, 9) if span == 0 else tc(t, 2)
                pl.op("vector", lambda v, h2s=h2s, dstm=dstm:
                      v.tensor_reduce(dstm, h2s, AX.X, OP.max,
                                      apply_absolute_value=True))
            if span > 0:
                for t in range(NT):
                    pl.op("vector", lambda v, t=t: v.tensor_tensor(
                        tc(t, 10), tc(t, 10), tc(t, 11), OP.add))
                for t in range(NT):
                    pl.op("vector", lambda v, t=t: v.tensor_tensor(
                        tc(t, 9), tc(t, 9), tc(t, 2), OP.max))
        pl.op("vector", lambda v: v.memset(col(cT2), 0.0),
              incs=[pl.inc("v")])
        v_h2stats = pl.val("v")

        tappoint("h2", [(h2_tile(t)[:, 0:2048],
                         [("sc", gelu2_sc[(NSPAN1 - 1, t)])])
                        for t in range(NT)])

        v_w2s = finalize_mean("w2", cM2, cSw2, cM2o127,
                              extra_mm_waits=[("sc", ev1[(NSPAN1 - 1, 0)])])

        q2_done = {}
        tr_xt2 = {}
        mm1_done = ("t", t_mm1[(NSPAN1 - 1, 1)])
        for t in range(NT):
            w0_ = [("v", v_h2stats), ("v", v_w2s)]
            vchain(lambda v, t=t: v.tensor_scalar(
                tc(t, 3), tc(t, 10), 1e-24, None, OP.max), extra_waits=w0_)
            vchain(lambda v, t=t: v.reciprocal(tc(t, 3), tc(t, 3)),
                   incs=[pl.inc("v")])
            vr = pl.val("v")
            pl.op("scalar", lambda s, t=t: s.activation(
                tc(t, 5), tc(t, 3), AF.Sqrt, scale=float(INNER)),
                waits=[("v", vr)], incs=[pl.inc("sc")])
            scq = pl.val("sc")
            vchain(lambda v, t=t: v.tensor_tensor(
                tc(t, 6), tc(t, 9), tc(t, 5), OP.mult),
                extra_waits=[("sc", scq)])
            vchain(lambda v, t=t: v.tensor_scalar(
                tc(t, 6), tc(t, 6), EPS_Q, None, OP.max))
            vchain(lambda v, t=t: v.reciprocal(tc(t, 7), tc(t, 6)))
            vchain(lambda v, t=t: v.tensor_tensor(
                tc(t, 8), tc(t, 6), col(cM2o127), OP.mult))
            vchain(lambda v, t=t: v.tensor_scalar(
                tc(t, 7), tc(t, 7), 127.0, None, OP.mult))
            vchain(lambda v, t=t: v.tensor_tensor(
                tc(t, 7), tc(t, 7), tc(t, 5), OP.mult),
                incs=[pl.inc("v")])
            v_s2t = pl.val("v")
            eng = "vector" if t % 2 == 0 else "gpsimd"
            esem = "v" if t % 2 == 0 else "g"
            fstage = F1 if t % 2 == 0 else F3
            wv0 = [("v", v_s2t)]
            if t % 2 == 0:
                wv0.append(("g", g_tern1[2 * NSPAN1 - 1]))
            else:
                wv0.append(("g", g_tern1[2 * NSPAN1 - 2]))
            for ch in range(4):
                h2c = h2_tile(t)[:, ch * D:(ch + 1) * D]
                pl.op(eng, lambda e, h2c=h2c, fstage=fstage, t=t:
                      e.tensor_scalar(fstage[:], h2c, tc(t, 7), col(cC),
                                      OP.mult, OP.add),
                      waits=wv0 if ch == 0 else [])
                gap(eng)
                pl.op(eng, lambda e, h2c=h2c, fstage=fstage:
                      e.tensor_scalar(h2c, fstage[:], col(cOne), col(cNegC),
                                      OP.mult, OP.add),
                      incs=[pl.inc(esem)] if ch == 3 else [])
            q2_done[t] = (esem, pl.val(esem))
            wv = [q2_done[t]]
            if t < 2:
                wv.append(mm1_done)
            else:
                wv.append(tr_xt2[t - 2])
            tr_xt2[t] = dma("t2A" if t % 2 == 0 else "t2B",
                            lambda s, t=t: s.dma_start_transpose(
                                xt2_tile(t), h2_tile(t).rearrange(
                                    "p (c f) -> p c f", c=64)), waits=wv)
        xt2_all = [tr_xt2[NT - 1], tr_xt2[NT - 2]]

        tern2_seq = {}
        tr_w2p = {}
        t_mm2 = {}
        copy_sc = {}
        tstg_free = {}
        ob_last = {}
        stt_v = {}
        nst = [0]
        store_led = {}
        xps_loaded = {}
        gelu_all = ("sc", gelu2_sc[(NSPAN1 - 1, NT - 1)])
        for slab in range(NSLAB2):
            wxp = [sp_all]
            if slab < 2:
                wxp.append(mm1_done)
            if slab >= 2:
                wxp.append(stt_v[slab - 2])
            xps_loaded[slab] = dma(
                "xpA" if slab % 2 == 0 else "xpB",
                lambda s, slab=slab: s.dma_start(
                    out=XPS[slab % 2][:],
                    in_=xp_hbm[slab].transpose([1, 0, 2])), waits=wxp)
            for p4 in range(4):
                piece = slab * 4 + p4
                stg = F1 if piece % 2 == 0 else F3
                eng = "vector" if piece % 2 == 0 else "gpsimd"
                esem = "v" if eng == "vector" else "g"
                wv = []
                if piece >= 2:
                    wv.append(tern2_seq[piece - 2])
                else:
                    wv.append(q2_done[6 if piece % 2 == 0 else 7])
                ld = dma("s2A" if piece % 2 == 0 else "s2B",
                         lambda s, slab=slab, p4=p4, stg=stg: s.dma_start(
                             out=stg[:],
                             in_=w2_in[slab * P:(slab + 1) * P,
                                       p4 * D:(p4 + 1) * D]), waits=wv)
                wt0 = [ld]
                if piece < 2:
                    wt0 += xt2_all
                if piece >= 2:
                    wt0.append(tr_w2p[piece - 2])
                pl.op(eng, lambda e, stg=stg: e.tensor_scalar(
                    stg[:], stg[:], col(cSw2), col(cC), OP.mult, OP.add),
                    waits=wt0)
                gap(eng)
                pl.op(eng, lambda e, stg=stg: e.tensor_scalar(
                    stg[:], stg[:], col(cCp1), col(cCm1), OP.min, OP.max))
                gap(eng)
                pl.op(eng, lambda e, stg=stg, piece=piece: e.tensor_scalar(
                    PQ[piece % 2], stg[:], col(cOne), col(cNegC),
                    OP.mult, OP.add),
                    incs=[pl.inc(esem)])
                tern2_seq[piece] = (esem, pl.val(esem))
                wtr = [tern2_seq[piece]]
                if slab < 2 and p4 < 2:
                    wtr += xt2_all
                if slab >= 2:
                    wtr.append(("t", t_mm2[(slab - 2, 1)]))
                if p4 > 0:
                    wtr.append(tr_w2p[piece - 1])
                elif slab >= 2:
                    wtr.append(tr_w2p[4 * (slab - 2) + 3])
                tr_w2p[piece] = dma(
                    "w2A" if slab % 2 == 0 else "w2B",
                    lambda s, slab=slab, p4=p4, piece=piece:
                    s.dma_start_transpose(
                        W2T[slab % 2][:, p4 * 16:(p4 + 1) * 16, :],
                        PQ[piece % 2].rearrange("p (c f) -> p c f", c=16)),
                    waits=wtr)
            for half in range(2):
                bk = 2 * (slab % 2) + half
                wts = [xt2_all[0], xt2_all[1]]
                if slab < 2:
                    wts.append(gelu_all)
                    wts.append(("v", v_w2s))
                if (slab - 2, half) in copy_sc:
                    wts.append(("sc", copy_sc[(slab - 2, half)]))
                for c in range(64):
                    w = []
                    if c == 0:
                        w = wts + [tr_w2p[slab * 4]]
                    elif c % 16 == 0:
                        w = [tr_w2p[slab * 4 + c // 16]]
                    pl.op("tensor", lambda e, c=c, slab=slab, half=half,
                          bk=bk: e.matmul(
                              PS[bk][:],
                              W2T[slab % 2][:, c, :],
                              XT2V[:, 4 * half:4 * half + 4, c, :],
                              start=(c == 0), stop=(c == 63)),
                          waits=w,
                          incs=[pl.inc("t")] if c == 63 else [])
                t_mm2[(slab, half)] = pl.val("t")
                s_idx = (2 * slab + half) % 4
                wcp = [("t", t_mm2[(slab, half)])]
                if (s_idx,) in ob_last:
                    wcp.append(ob_last[(s_idx,)])
                pl.op("scalar", lambda s, bk=bk, s_idx=s_idx: s.copy(
                    MSTG[:, s_idx].rearrange("p c f -> p (c f)"),
                    PS[bk][:]),
                    waits=wcp, incs=[pl.inc("sc")])
                copy_sc[(slab, half)] = pl.val("sc")
                wob = [("sc", copy_sc[(slab, half)])]
                if (s_idx,) in tstg_free:
                    wob.append(tstg_free[(s_idx,)])
                ob = dma(f"ob{s_idx}",
                         lambda s, s_idx=s_idx: s.dma_start_transpose(
                             TSTG[:, s_idx], MSTG[:, s_idx]),
                         waits=wob)
                ob_last[(s_idx,)] = ob
                for ti in range(4):
                    t = 4 * half + ti
                    n = nst[0]
                    slot = n % 4
                    stsem = f"st{slot}"
                    oslot = OSB[:, slot // 2, (slot % 2) * 128:
                                (slot % 2) * 128 + 128]
                    wev = [ob, xps_loaded[slab]]
                    if (slot, "last") in store_led:
                        wev.append(store_led[(slot, "last")])
                    pl.op("vector", lambda v, t=t, slab=slab, oslot=oslot,
                          s_idx=s_idx, ti=ti: v.scalar_tensor_tensor(
                              oslot, TSTG[:, s_idx, ti, :], tc(t, 8),
                              XPS[slab % 2][:, t, :], OP.mult, OP.add),
                          waits=wev, incs=[pl.inc("v")])
                    vev = pl.val("v")
                    st = dma(stsem, lambda s, t=t, slab=slab, oslot=oslot:
                             s.dma_start(
                                 out=out_hbm[t * P:(t + 1) * P,
                                             slab * DK:(slab + 1) * DK],
                                 in_=oslot), waits=[("v", vev)],
                             eng="scalar")
                    store_led[(slot, "last")] = st
                    nst[0] += 1
                tstg_free[(s_idx,)] = ("v", pl.val("v"))
            stt_v[slab] = ("v", pl.val("v"))

        pl.op("sync", lambda s: s.nop(),
              waits=[(f"st{i}", pl.val(f"st{i}")) for i in range(4)])

        if hasattr(pl, "_tap_cut"):
            pl.ops = pl.ops[:pl._tap_cut]

        @block.sync
        def _(sync):
            pl.emit("sync", sync, handles)

        @block.vector
        def _(vector):
            pl.emit("vector", vector, handles)

        @block.scalar
        def _(scalar):
            pl.emit("scalar", scalar, handles)

        @block.gpsimd
        def _(gpsimd):
            pl.emit("gpsimd", gpsimd, handles)

        @block.tensor
        def _(tensor):
            pl.emit("tensor", tensor, handles)

    return nc


_NC_CACHE = []


def kernel(L0_attn_x, L0_attn_Wq, L0_attn_Wk, L0_attn_Wv, L0_attn_Wo,
           L0_ffn_w1, L0_ffn_w2):
    x = np.ascontiguousarray(np.asarray(L0_attn_x, dtype=np.float32)
                             .reshape(N_CORES * TOK, D))
    wq = np.ascontiguousarray(np.asarray(L0_attn_Wq, dtype=np.float32))
    wk = np.ascontiguousarray(np.asarray(L0_attn_Wk, dtype=np.float32))
    wv = np.ascontiguousarray(np.asarray(L0_attn_Wv, dtype=np.float32))
    wo = np.ascontiguousarray(np.asarray(L0_attn_Wo, dtype=np.float32))
    w1 = np.ascontiguousarray(np.asarray(L0_ffn_w1, dtype=np.float32))
    w2 = np.ascontiguousarray(np.asarray(L0_ffn_w2, dtype=np.float32))

    if not _NC_CACHE:
        _NC_CACHE.append(build_program())
    nc = _NC_CACHE[0]

    in_maps = []
    for c in range(N_CORES):
        in_maps.append({
            "x_in": x[c * TOK:(c + 1) * TOK],
            "wq_in": wq, "wk_in": wk, "wv_in": wv, "wo_in": wo,
            "w1_in": w1, "w2_in": w2,
        })
    res = run_bass_kernel_spmd(nc, in_maps, core_ids=list(range(N_CORES)))
    out = np.concatenate([np.asarray(r["out"]) for r in res.results], axis=0)
    return out.reshape(1, N_CORES * TOK, D).astype(np.float32)


# version 08
# speedup vs baseline: 1.0006x; 1.0006x over previous
import numpy as np
from contextlib import ExitStack

import concourse.bass as bass
import concourse.mybir as mybir
from concourse.bass_utils import run_bass_kernel_spmd

F32 = mybir.dt.float32
BF16 = mybir.dt.bfloat16
AF = mybir.ActivationFunctionType
OP = mybir.AluOpType
AX = mybir.AxisListType

N_CORES = 8
P = 128
D = 2048
INNER = 8192
TOK = 1024
NT = TOK // P
H = 16
DK = 128
EPS_LN = 1e-5
EPS_Q = 1e-5
EPS_W = 1e-5
C32 = float(1.5 * 2.0 ** 23)
C64 = float(1.5 * 2.0 ** 52)
MS1 = 256
NSPAN1 = INNER // MS1
NSLAB2 = D // P

SEMS = ["xA", "xB", "wA", "wB", "mA", "mB", "s1A", "s1B",
        "s2A", "s2B", "s2C",
        "xpA", "xpB", "txA", "txB", "toA", "toB", "t1A", "t1B", "t2A", "t2B",
        "w1A", "w1B", "w2p0", "w2p1", "w2p2", "w2p3", "sp",
        "ob0", "ob1", "ob2", "ob3",
        "st0", "st1", "st2", "st3", "v", "vv", "sc", "g", "t"]


class Plan:
    def __init__(self):
        self.ops = []
        self.sems = {}

    def sem(self, name):
        self.sems.setdefault(name, 0)

    def inc(self, name, n=1):
        self.sems[name] = self.sems.get(name, 0) + n
        return (name, n)

    def val(self, name):
        return self.sems.get(name, 0)

    def op(self, engine, fn, waits=(), incs=()):
        self.ops.append((engine, fn,
                         [w for w in waits if w is not None and w[1] > 0],
                         list(incs)))

    def emit(self, engine_name, eng, handles):
        for e, fn, waits, incs in self.ops:
            if e != engine_name:
                continue
            for (s, v) in waits:
                eng.wait_ge(handles[s], v)
            ins = fn(eng)
            for (s, amt) in incs:
                ins.then_inc(handles[s], amt)


def build_program(tap=None):
    nc = bass.Bass("TRN2", target_bir_lowering=False,
                   detect_race_conditions=False,
                   dynamic_dma_scratch_size=2048)

    x_in = nc.dram_tensor("x_in", [TOK, D], F32, kind="ExternalInput").ap()
    w_attn = {
        "q": nc.dram_tensor("wq_in", [D, D], F32, kind="ExternalInput").ap(),
        "k": nc.dram_tensor("wk_in", [D, D], F32, kind="ExternalInput").ap(),
        "v": nc.dram_tensor("wv_in", [D, D], F32, kind="ExternalInput").ap(),
        "o": nc.dram_tensor("wo_in", [D, D], F32, kind="ExternalInput").ap(),
    }
    w1_in = nc.dram_tensor("w1_in", [INNER, D], F32, kind="ExternalInput").ap()
    w2_in = nc.dram_tensor("w2_in", [D, INNER], F32, kind="ExternalInput").ap()
    out_hbm = nc.dram_tensor("out", [TOK, D], F32, kind="ExternalOutput").ap()
    xp_hbm = nc.dram_tensor("xp_spill", [NSLAB2, NT, P, DK], BF16,
                            kind="Internal").ap()

    pl = Plan()
    for s in SEMS:
        pl.sem(s)

    with ExitStack() as ctx:
        def sbt(name, shape, dt=BF16):
            return ctx.enter_context(nc.sbuf_tensor(name, shape, dt))

        AR14 = sbt("AR14", [P, 4, NT, D])
        AR1 = AR14[:, 0]
        AR2 = AR14[:, 1]
        AR3 = AR14[:, 2]
        AR4 = AR14[:, 3]
        AR8 = sbt("AR8", [P, 28672])
        F1 = sbt("F1", [P, D], F32)
        F2 = sbt("F2", [P, D], F32)
        F3 = sbt("F3", [P, D], F32)
        F5 = sbt("F5", [P, NT, H, H], F32)
        OSB = sbt("OSB", [P, 2, 256], F32)
        COLS = sbt("COLS", [P, 16], F32)
        TC = sbt("TC", [P, NT, 12], F32)
        MEANP = sbt("MEANP", [P, 128], F32)
        JNK = sbt("JNK", [P, 256], BF16)
        ONESF = sbt("ONESF", [P, 128], F32)

        PS = [ctx.enter_context(nc.psum_tensor(f"ps{i}", [P, 512], F32))
              for i in range(8)]

        handles = {s: ctx.enter_context(nc.semaphore(s)) for s in pl.sems}
        block = ctx.enter_context(nc.Block())

        cC, cProbe, cFlag, cTmp = 0, 1, 2, 3
        cM1, cSw1, cM1o127 = 4, 5, 6
        cM2, cSw2, cM2o127 = 7, 8, 9
        cT2, cCp1, cCm1, cZero = 10, 11, 12, 13
        cOne, cNegC = 14, 15

        def col(i):
            return COLS[:, i:i + 1]

        def tc(t, i):
            return TC[:, t, i:i + 1]

        XNT = AR1[:].rearrange("p t (c f) -> p t c f", c=16)
        XT1 = XNT
        q_bf = AR2
        k_bf = AR3
        v_bf = AR2
        o_bf = AR3

        def wqtr(i):
            return AR4[:, 4 * (i % 2):4 * (i % 2) + 4, :].rearrange(
                "p t f -> p (t f)").rearrange("p (c g) -> p c g", c=16)

        def ar8(lo, hi):
            return AR8[:, lo:hi]

        xn_buf = [ar8(0, 2048), ar8(2048, 4096)]
        tmp_buf = [ar8(4096, 6144), ar8(6144, 8192)]
        junk1k = ar8(8192, 9216)
        junkP0 = ar8(9216, 11264)

        def xp_t(t):
            return ar8(t * 2048, (t + 1) * 2048)

        XPV = AR8[:, 0:16384].rearrange("p (t f) -> p t f", t=NT)
        xint_buf = [ar8(16384, 18432), ar8(18432, 20480)]
        junkP5 = ar8(20480, 22528)
        W1Q = ar8(16384, 20480).rearrange("p (s f) -> p s f", s=2)
        W1T = [ar8(20480, 24576).rearrange("p (c f) -> p c f", c=16),
               ar8(24576, 28672).rearrange("p (c f) -> p c f", c=16)]

        def h2_tile(t):
            if t < 2:
                return AR2[:, 4 * t:4 * (t + 1), :].rearrange(
                    "p a b -> p (a b)")
            if t < 4:
                return AR3[:, 4 * (t - 2):4 * (t - 1), :].rearrange(
                    "p a b -> p (a b)")
            if t < 6:
                return AR4[:, 4 * (t - 4):4 * (t - 3), :].rearrange(
                    "p a b -> p (a b)")
            return ar8(8192 * (t - 6), 8192 * (t - 5))

        def xt2_tile(t):
            if t < 2:
                base = AR1[:, 4 * t:4 * (t + 1), :].rearrange(
                    "p a b -> p (a b)")
            else:
                base = h2_tile(t - 2)
            return base.rearrange("p (c f) -> p c f", c=64)

        W2T = [ar8(0, 8192).rearrange("p (c f) -> p c f", c=64),
               ar8(8192, 16384).rearrange("p (c f) -> p c f", c=64)]
        PQ = [ar8(16384, 18432), ar8(18432, 20480)]
        XPS = [ar8(20480, 21504).rearrange("p (t f) -> p t f", t=NT),
               ar8(21504, 22528).rearrange("p (t f) -> p t f", t=NT)]
        XT2V = AR14[:].rearrange("p a t d -> p (a t d)").rearrange(
            "p (T c f) -> p T c f", T=NT, c=64)
        MSTG = ar8(22528, 24576).rearrange("p (s c f) -> p s c f", s=4, c=4)
        TSTG = ar8(24576, 26624).rearrange("p (s c f) -> p s c f", s=4, c=4)

        def gap(eng="vector"):
            pl.op(eng, lambda e: e.memset(col(cT2), 0.0))

        def vchain(fn, extra_waits=(), incs=()):
            pl.op("vector", fn,
                  waits=[("vv", pl.val("vv"))] + list(extra_waits),
                  incs=[pl.inc("vv")])
            if incs:
                pl.op("vector", lambda v: v.memset(col(cT2), 0.0),
                      waits=[("vv", pl.val("vv"))], incs=list(incs))

        def dma(sem, fn, waits=(), eng="sync"):
            pl.op(eng, fn, waits=waits, incs=[pl.inc(sem, 16)])
            return (sem, pl.val(sem))

        def tappoint(name, rows_src):
            if tap != name:
                return
            for t, (ap, w) in enumerate(rows_src):
                W = min(ap.free_size(), 2048)
                pl.op("vector", lambda v, ap=ap, W=W: v.tensor_copy(
                    F3[:, 0:W], ap),
                    waits=list(w) + [("st0", pl.val("st0"))],
                    incs=[pl.inc("v")])
                vv_ = pl.val("v")
                dma("st0", lambda s, t=t, W=W: s.dma_start(
                    out=out_hbm[t * P:(t + 1) * P, 0:W], in_=F3[:, 0:W]),
                    waits=[("v", vv_)])
            pl.op("sync", lambda s: s.nop(),
                  waits=[("st0", pl.val("st0"))])
            pl._tap_cut = len(pl.ops)

        vchain(lambda v: v.memset(ONESF[:], 1.0))
        vchain(lambda v: v.memset(col(cProbe), 0.25))
        vchain(lambda v: v.tensor_scalar(
            col(cProbe), col(cProbe), C32, C32, OP.add, OP.subtract))
        vchain(lambda v: v.tensor_scalar(
            col(cFlag), col(cProbe), 0.1, None, OP.is_gt))
        vchain(lambda v: v.tensor_scalar(
            col(cC), col(cFlag), C64, None, OP.mult))
        vchain(lambda v: v.tensor_scalar(
            col(cTmp), col(cFlag), -C32, C32, OP.mult, OP.add))
        vchain(lambda v: v.tensor_tensor(
            col(cC), col(cC), col(cTmp), OP.add))
        vchain(lambda v: v.tensor_scalar(
            col(cCp1), col(cC), 1.0, None, OP.add))
        vchain(lambda v: v.tensor_scalar(
            col(cCm1), col(cC), -1.0, None, OP.add))
        vchain(lambda v: v.memset(col(cZero), 0.0))
        vchain(lambda v: v.memset(col(cOne), 1.0))
        vchain(lambda v: v.tensor_scalar(
            col(cNegC), col(cC), -1.0, None, OP.mult))
        vchain(lambda v: v.memset(JNK[:], 0.0), incs=[pl.inc("v")])
        vC = pl.val("v")

        def xstage(t):
            return F1 if t % 2 == 0 else F2

        def xsem(t):
            return "xA" if t % 2 == 0 else "xB"

        def txsem(t):
            return "txA" if t % 2 == 0 else "txB"

        v_xn = {}
        tr_xnt = {}
        x_ld = {}
        for t in range(NT):
            w = [("v", v_xn[t - 2])] if (t - 2) in v_xn else []
            x_ld[t] = dma(xsem(t), lambda s, t=t: s.dma_start(
                out=xstage(t)[:], in_=x_in[t * P:(t + 1) * P, :]), waits=w)
            vchain(lambda v, t=t: v.tensor_reduce(
                tc(t, 0), xstage(t)[:], AX.X, OP.add),
                extra_waits=[x_ld[t], ("v", vC)])
            vchain(lambda v, t=t: v.scalar_tensor_tensor(
                junkP0, xstage(t)[:], 1.0, xstage(t)[:], OP.mult, OP.mult,
                accum_out=tc(t, 3)))
            vchain(lambda v, t=t: v.tensor_scalar(
                tc(t, 0), tc(t, 0), 1.0 / D, None, OP.mult))
            vchain(lambda v, t=t: v.tensor_scalar(
                tc(t, 3), tc(t, 3), 1.0 / D, None, OP.mult))
            vchain(lambda v, t=t: v.tensor_tensor(
                tc(t, 2), tc(t, 0), tc(t, 0), OP.mult))
            vchain(lambda v, t=t: v.tensor_tensor(
                tc(t, 3), tc(t, 3), tc(t, 2), OP.subtract))
            vchain(lambda v, t=t: v.tensor_scalar(
                tc(t, 3), tc(t, 3), EPS_LN, None, OP.add))
            vchain(lambda v, t=t: v.reciprocal(tc(t, 3), tc(t, 3)),
                   incs=[pl.inc("v")])
            vr = pl.val("v")
            pl.op("scalar", lambda s, t=t: s.activation(
                tc(t, 1), tc(t, 3), AF.Sqrt), waits=[("v", vr)],
                incs=[pl.inc("sc")])
            scr = pl.val("sc")
            vchain(lambda v, t=t: v.tensor_tensor(
                tc(t, 2), tc(t, 0), tc(t, 1), OP.mult),
                extra_waits=[("sc", scr)])
            wtr = [tr_xnt[t - 2]] if (t - 2) in tr_xnt else []
            vchain(lambda v, t=t: v.tensor_scalar(
                xn_buf[t % 2], xstage(t)[:], tc(t, 1), tc(t, 2),
                OP.mult, OP.subtract), extra_waits=wtr, incs=[pl.inc("v")])
            v_xn[t] = pl.val("v")
            tr_xnt[t] = dma(txsem(t), lambda s, t=t: s.dma_start_transpose(
                XNT[:, t], xn_buf[t % 2].rearrange("p (c f) -> p c f", c=16)),
                waits=[("v", v_xn[t])])

        tappoint("xnt", [(XNT[:, t].rearrange("p c f -> p (c f)"),
                          [tr_xnt[t]]) for t in range(NT)])

        mean_state = {"w1": 0, "w2": 0}
        mean_sc = {}

        junk_f5 = F5[:].rearrange("p t h k -> p (t h k)")[:, 0:1024]

        def emit_mean_piece(tag, whbm, stage_buf, halves_per_row, first_wait):
            i = mean_state[tag]
            mean_state[tag] += 1
            row0 = i // halves_per_row
            cof = (i % halves_per_row) * 1024
            half = stage_buf[:, (i % 2) * 1024:(i % 2) * 1024 + 1024]
            junk = junk1k if tag == "w1" else junk_f5
            sem = "mA" if i % 2 == 0 else "mB"
            wv = []
            if (tag, i - 2) in mean_sc:
                wv.append(("sc", mean_sc[(tag, i - 2)]))
            elif i < 2 and first_wait is not None:
                wv.append(first_wait)
            ld = dma(sem, lambda s, whbm=whbm, row0=row0, cof=cof, half=half:
                     s.dma_start(out=half,
                                 in_=whbm[row0 * P:(row0 + 1) * P,
                                          cof:cof + 1024]), waits=wv)
            pl.op("scalar", lambda s, half=half, i=i, junk=junk: s.activation(
                junk, half, AF.Abs, accum_out=MEANP[:, i:i + 1]),
                waits=[ld], incs=[pl.inc("sc")])
            mean_sc[(tag, i)] = pl.val("sc")

        def finalize_mean(tag, cM, cSw, cMo127, extra_mm_waits=()):
            vchain(lambda v, cM=cM: v.tensor_reduce(
                col(cM), MEANP[:, 0:128], AX.X, OP.add),
                extra_waits=[("sc", mean_sc[(tag, 127)])],
                incs=[pl.inc("v")])
            vv_ = pl.val("v")
            pl.op("tensor", lambda e, cM=cM: e.matmul(
                PS[0][:, 0:1], ONESF[:], col(cM), start=True, stop=True),
                waits=[("v", vv_)] + list(extra_mm_waits),
                incs=[pl.inc("t")])
            tv = pl.val("t")
            vchain(lambda v, cM=cM: v.tensor_copy(
                col(cM), PS[0][:, 0:1]), extra_waits=[("t", tv)])
            vchain(lambda v, cM=cM: v.tensor_scalar(
                col(cM), col(cM), 1.0 / (INNER * D), None, OP.mult))
            vchain(lambda v, cM=cM: v.tensor_scalar(
                col(cM), col(cM), EPS_W, None, OP.max))
            vchain(lambda v, cM=cM, cSw=cSw: v.reciprocal(
                col(cSw), col(cM)))
            vchain(lambda v, cM=cM, cMo127=cMo127: v.tensor_scalar(
                col(cMo127), col(cM), 1.0 / 127.0, None, OP.mult))
            vchain(lambda v: v.memset(JNK[:], 0.0), incs=[pl.inc("v")])
            return pl.val("v")

        cvt_seq = []
        qtr_mm_done = {}
        sc_evac = {}
        t_mm = {}
        bank_free = {}
        g_scores = {}

        def attn_weight_phase(wname, stat_tr, dst, wi,
                              extra_evac_wait=None, bg=None):
            whbm = w_attn[wname]
            for qtr in range(4):
                gq = wi * 4 + qtr
                buf = wqtr(gq % 2)
                for i8 in range(8):
                    slot = i8 % 2
                    fsl = F3[:, slot * 1024:(slot + 1) * 1024]
                    sem = "wA" if slot == 0 else "wB"
                    wv = []
                    if len(cvt_seq) >= 2:
                        wv.append(("sc", cvt_seq[-2]))
                    if i8 == 0 and (gq - 2) in qtr_mm_done:
                        wv.append(("t", qtr_mm_done[gq - 2]))
                    src = whbm[i8 * 256:(i8 + 1) * 256,
                               qtr * 512:(qtr + 1) * 512]
                    ld = dma(sem, lambda s, src=src, fsl=fsl: s.dma_start(
                        out=fsl.rearrange("p (a g) -> p a g", a=2),
                        in_=src.rearrange("(a p) g -> p a g", p=128)),
                        waits=wv)
                    pl.op("scalar", lambda s, buf=buf, i8=i8, fsl=fsl:
                          s.copy(buf[:, 2 * i8:2 * i8 + 2, :].rearrange(
                              "p a g -> p (a g)"), fsl),
                          waits=[ld], incs=[pl.inc("sc")])
                    cvt_seq.append(pl.val("sc"))
                cvt_all = pl.val("sc")
                if bg is not None:
                    bg()
                for t in range(NT):
                    wts = [("sc", cvt_all), stat_tr[t]]
                    if t in bank_free:
                        wts.append(bank_free[t])
                    for c in range(16):
                        pl.op("tensor", lambda e, t=t, c=c, buf=buf:
                              e.matmul(PS[t][:], XNT[:, t, c, :],
                                       buf[:, c, :],
                                       start=(c == 0), stop=(c == 15)),
                              waits=wts if c == 0 else [],
                              incs=[pl.inc("t")] if c == 15 else [])
                    t_mm[(wname, qtr, t)] = pl.val("t")
                    wev = [("t", t_mm[(wname, qtr, t)])]
                    if extra_evac_wait is not None:
                        wev.extend(extra_evac_wait(t))
                    pl.op("scalar", lambda s, t=t, qtr=qtr, dst=dst:
                          s.copy(dst[:, t, qtr * 512:(qtr + 1) * 512],
                                 PS[t][:]),
                          waits=wev, incs=[pl.inc("sc")])
                    bank_free[t] = ("sc", pl.val("sc"))
                    if qtr == 3:
                        sc_evac[(wname, t)] = pl.val("sc")
                qtr_mm_done[gq] = pl.val("t")

        def mean_w1_bg():
            for _ in range(12):
                if mean_state["w1"] < 128:
                    emit_mean_piece("w1", w1_in, F2, 2, ("v", v_xn[NT - 1]))

        bg_count = [0]

        def bg_hook():
            bg_count[0] += 1
            if bg_count[0] >= 2:
                mean_w1_bg()

        attn_weight_phase("q", tr_xnt, q_bf, 0, bg=bg_hook)
        tappoint("q", [(q_bf[:, t, :], [("sc", sc_evac[("q", t)])])
                       for t in range(NT)])

        attn_weight_phase("k", tr_xnt, k_bf, 1, bg=bg_hook)
        tappoint("k", [(k_bf[:, t, :], [("sc", sc_evac[("k", t)])])
                       for t in range(NT)])

        for t in range(NT):
            for hp in range(H // 2):
                for h in (2 * hp, 2 * hp + 1):
                    wts = []
                    if h == 0:
                        wts = [("sc", sc_evac[("q", t)]),
                               ("sc", sc_evac[("k", t)])]
                    qsl = q_bf[:, t, h * DK:(h + 1) * DK]
                    pl.op("vector", lambda v, t=t, h=h, qsl=qsl:
                          v.tensor_tensor(
                              tmp_buf[h % 2],
                              qsl.unsqueeze(1).broadcast_to([P, 16, DK]),
                              k_bf[:, t, :].rearrange("p (c f) -> p c f",
                                                      c=16),
                              OP.mult), waits=wts)
                for h in (2 * hp, 2 * hp + 1):
                    last = (h == H - 1)
                    pl.op("vector", lambda v, t=t, h=h: v.tensor_reduce(
                        F5[:, t, h, :], tmp_buf[h % 2].rearrange(
                            "p (c f) -> p c f", c=16), AX.X, OP.add),
                        incs=[pl.inc("v")] if last else [])
            g_scores[(t, H - 1)] = ("v", pl.val("v"))

        tappoint("scores", [(F5[:, t].rearrange("p a b -> p (a b)"),
                             [g_scores[(t, H - 1)]]) for t in range(NT)])

        attn_weight_phase("v", tr_xnt, v_bf, 2,
                          extra_evac_wait=lambda t: [g_scores[(t, H - 1)]],
                          bg=bg_hook)

        o3_tmp = [tmp_buf[0].rearrange("p (h f) -> p h f", h=16),
                  tmp_buf[1].rearrange("p (h f) -> p h f", h=16),
                  junkP0[:, 0:2048].rearrange("p (h f) -> p h f", h=16)]
        v_odone = {}
        for t in range(NT):
            o3 = o_bf[:, t, :].rearrange("p (h f) -> p h f", h=16)

            def omul(kh, dst, t=t):
                return lambda v, kh=kh, dst=dst, t=t: v.tensor_tensor(
                    dst,
                    F5[:, t, :, kh:kh + 1].broadcast_to([P, H, DK]),
                    v_bf[:, t, kh * DK:(kh + 1) * DK].unsqueeze(1)
                    .broadcast_to([P, H, DK]),
                    OP.mult)

            def oadd(kh, t=t):
                return lambda v, kh=kh, t=t: v.tensor_tensor(
                    o_bf[:, t, :].rearrange("p (h f) -> p h f", h=16),
                    o_bf[:, t, :].rearrange("p (h f) -> p h f", h=16),
                    o3_tmp[kh % 3], OP.add)

            pl.op("vector", omul(0, o3),
                  waits=[g_scores[(t, H - 1)], ("sc", sc_evac[("v", t)])])
            pl.op("vector", omul(1, o3_tmp[1]))
            pl.op("vector", omul(2, o3_tmp[2]))
            for kh in range(3, H):
                pl.op("vector", oadd(kh - 2))
                pl.op("vector", omul(kh, o3_tmp[kh % 3]))
            pl.op("vector", oadd(H - 2))
            pl.op("vector", oadd(H - 1), incs=[pl.inc("v")])
            v_odone[t] = ("v", pl.val("v"))

        tappoint("o", [(o_bf[:, t, :], [v_odone[t]]) for t in range(NT)])

        t_vdone = t_mm[("v", 3, NT - 1)]
        tr_ot = {}
        for t in range(NT):
            w = [v_odone[t], ("t", t_vdone)]
            if (t - 2) in tr_ot:
                w.append(tr_ot[t - 2])
            tr_ot[t] = dma("toA" if t % 2 == 0 else "toB",
                           lambda s, t=t: s.dma_start_transpose(
                               XNT[:, t], o_bf[:, t, :].rearrange(
                                   "p (c f) -> p c f", c=16)), waits=w)

        def wo_gate(t):
            w = []
            if t in (2, 3, 4, 5):
                w.append(v_odone[NT - 1])
            if t not in (2, 3):
                w.append(("sc", mean_sc[("w1", 127)]))
            return w
        attn_weight_phase("o", tr_ot, XPV, 3, extra_evac_wait=wo_gate)

        v_w1s = finalize_mean("w1", cM1, cSw1, cM1o127,
                              extra_mm_waits=[("sc", sc_evac[("o", 0)])])

        vq1_done = {}
        tr_xt1 = {}
        for t in range(NT):
            xps = xp_t(t)
            fstage = F1 if t % 2 == 0 else F2
            wld = []
            if (t - 2) in vq1_done:
                wld.append(("v", vq1_done[t - 2]))
            elif t == 0:
                wld.append(("v", v_xn[NT - 2]))
            else:
                wld.append(("sc", mean_sc[("w1", 127)]))
            ld = dma(xsem(t), lambda s, t=t: s.dma_start(
                out=xstage(t)[:], in_=x_in[t * P:(t + 1) * P, :]), waits=wld)
            pl.op("vector", lambda v, xps=xps, fstage=fstage:
                  v.tensor_tensor(xps, xps, fstage[:], OP.add),
                  waits=[ld, ("sc", sc_evac[("o", t)])],
                  incs=[pl.inc("v")])
            vadd = pl.val("v")
            dma("sp", lambda s, t=t, xps=xps: s.dma_start(
                out=xp_hbm.transpose([2, 0, 1, 3])[:, :, t, :],
                in_=xps.rearrange("p (c f) -> p c f", c=16)),
                waits=[("v", vadd), ("sp", pl.val("sp"))])
            vchain(lambda v, xps=xps, t=t: v.tensor_reduce(
                tc(t, 0), xps, AX.X, OP.add), extra_waits=[("v", vadd)])
            vchain(lambda v, xps=xps, t=t: v.scalar_tensor_tensor(
                junkP5, xps, 1.0, xps, OP.mult, OP.mult, accum_out=tc(t, 3)))
            vchain(lambda v, t=t: v.tensor_scalar(
                tc(t, 0), tc(t, 0), 1.0 / D, None, OP.mult))
            vchain(lambda v, t=t: v.tensor_scalar(
                tc(t, 3), tc(t, 3), 1.0 / D, None, OP.mult))
            vchain(lambda v, t=t: v.tensor_tensor(
                tc(t, 2), tc(t, 0), tc(t, 0), OP.mult))
            vchain(lambda v, t=t: v.tensor_tensor(
                tc(t, 3), tc(t, 3), tc(t, 2), OP.subtract))
            vchain(lambda v, t=t: v.tensor_scalar(
                tc(t, 3), tc(t, 3), EPS_LN, None, OP.add))
            vchain(lambda v, t=t: v.reciprocal(tc(t, 3), tc(t, 3)),
                   incs=[pl.inc("v")])
            vr = pl.val("v")
            pl.op("scalar", lambda s, t=t: s.activation(
                tc(t, 1), tc(t, 3), AF.Sqrt), waits=[("v", vr)],
                incs=[pl.inc("sc")])
            scr = pl.val("sc")
            vchain(lambda v, t=t: v.tensor_tensor(
                tc(t, 2), tc(t, 0), tc(t, 1), OP.mult),
                extra_waits=[("sc", scr)])
            vchain(lambda v, xps=xps, fstage=fstage, t=t:
                   v.tensor_scalar(fstage[:], xps, tc(t, 1), tc(t, 2),
                                   OP.mult, OP.subtract))
            vchain(lambda v, fstage=fstage, t=t:
                   v.scalar_tensor_tensor(junkP5, fstage[:], 1.0, fstage[:],
                                          OP.mult, OP.mult,
                                          accum_out=tc(t, 3)))
            vchain(lambda v, fstage=fstage, t=t: v.tensor_reduce(
                tc(t, 4), fstage[:], AX.X, OP.max,
                apply_absolute_value=True))
            vchain(lambda v, t=t: v.tensor_scalar(
                tc(t, 3), tc(t, 3), 1e-24, None, OP.max))
            vchain(lambda v, t=t: v.reciprocal(tc(t, 3), tc(t, 3)),
                   incs=[pl.inc("v")])
            vr2 = pl.val("v")
            pl.op("scalar", lambda s, t=t: s.activation(
                tc(t, 5), tc(t, 3), AF.Sqrt, scale=float(D)),
                waits=[("v", vr2)], incs=[pl.inc("sc")])
            scf = pl.val("sc")
            vchain(lambda v, t=t: v.tensor_tensor(
                tc(t, 6), tc(t, 4), tc(t, 5), OP.mult),
                extra_waits=[("sc", scf), ("v", v_w1s)])
            vchain(lambda v, t=t: v.tensor_scalar(
                tc(t, 6), tc(t, 6), EPS_Q, None, OP.max))
            vchain(lambda v, t=t: v.reciprocal(tc(t, 7), tc(t, 6)))
            vchain(lambda v, t=t: v.tensor_tensor(
                tc(t, 8), tc(t, 6), col(cM1o127), OP.mult))
            vchain(lambda v, t=t: v.tensor_scalar(
                tc(t, 7), tc(t, 7), 127.0, None, OP.mult))
            vchain(lambda v, t=t: v.tensor_tensor(
                tc(t, 7), tc(t, 7), tc(t, 5), OP.mult))
            vchain(lambda v, fstage=fstage, t=t: v.tensor_scalar(
                fstage[:], fstage[:], tc(t, 7), col(cC), OP.mult, OP.add))
            wtr = [tr_xt1[t - 2]] if (t - 2) in tr_xt1 else []
            vchain(lambda v, fstage=fstage, t=t: v.tensor_scalar(
                xint_buf[t % 2], fstage[:], col(cC), None, OP.subtract),
                extra_waits=wtr, incs=[pl.inc("v")])
            vq1_done[t] = pl.val("v")
            tr_xt1[t] = dma("t1A" if t % 2 == 0 else "t1B",
                            lambda s, t=t: s.dma_start_transpose(
                                XT1[:, t], xint_buf[t % 2].rearrange(
                                    "p (c f) -> p c f", c=16)),
                            waits=[("v", vq1_done[t]),
                                   ("t", t_mm[("o", 3, t)]), tr_ot[NT - 1],
                                   tr_ot[NT - 2]])
        sp_all = ("sp", pl.val("sp"))

        tappoint("xp", [(xp_t(t), [("v", vq1_done[t])]) for t in range(NT)])
        tappoint("xt1", [(XT1[:, t].rearrange("p c f -> p (c f)"),
                          [tr_xt1[t]]) for t in range(NT)])
        tappoint("tc", [(TC[:].rearrange("p a b -> p (a b)"),
                         [("v", vq1_done[NT - 1])])])

        h2_gate = {}
        for t in range(NT):
            if t < 2:
                h2_gate[t] = [v_odone[NT - 1]]
            elif t < 4:
                h2_gate[t] = [tr_ot[NT - 1], tr_ot[NT - 2]]
            elif t < 6:
                h2_gate[t] = [("t", qtr_mm_done[15])]
            else:
                h2_gate[t] = [sp_all, ("v", vq1_done[NT - 1])]

        xt1_all = [tr_xt1[NT - 1], tr_xt1[NT - 2]]
        g_tern1 = {}
        tr_w1 = {}
        t_mm1 = {}
        ev1 = {}
        gelu2_sc = {}
        for span in range(NSPAN1):
            for j in range(2):
                slab = 2 * span + j
                stg = F3 if j == 0 else F1
                sem = "s1A" if j == 0 else "s1B"
                wv = []
                if slab >= 2:
                    wv.append(("g", g_tern1[slab - 2]))
                elif j == 0:
                    wv.append(("sc", cvt_seq[-1]))
                    wv.append(("v", v_w1s))
                else:
                    wv.append(("v", vq1_done[NT - 2]))
                ld = dma(sem, lambda s, slab=slab, stg=stg: s.dma_start(
                    out=stg[:], in_=w1_in[slab * P:(slab + 1) * P, :]),
                    waits=wv)
                wq_gate = [ld]
                if span == 0:
                    wq_gate += xt1_all
                if span >= 1:
                    wq_gate.append(tr_w1[2 * (span - 1) + j])
                pl.op("gpsimd", lambda g, stg=stg: g.tensor_scalar(
                    stg[:], stg[:], col(cSw1), col(cC), OP.mult, OP.add),
                    waits=wq_gate)
                gap("gpsimd")
                pl.op("gpsimd", lambda g, stg=stg: g.tensor_scalar(
                    stg[:], stg[:], col(cCp1), col(cCm1), OP.min, OP.max))
                gap("gpsimd")
                pl.op("gpsimd", lambda g, stg=stg, j=j: g.tensor_scalar(
                    W1Q[:, j, :], stg[:], col(cOne), col(cNegC),
                    OP.mult, OP.add),
                    incs=[pl.inc("g")])
                g_tern1[slab] = pl.val("g")
                wtr = [("g", g_tern1[slab])]
                if span >= 2:
                    wtr.append(("t", t_mm1[(span - 2, 1)]))
                if span == 0:
                    wtr.append(("v", vq1_done[NT - 1]))
                if span >= 1:
                    wtr.append(tr_w1[2 * (span - 1) + j])
                tr_w1[slab] = dma(
                    "w1A" if j == 0 else "w1B",
                    lambda s, span=span, j=j: s.dma_start_transpose(
                        W1T[span % 2][:, :, j * 128:(j + 1) * 128],
                        W1Q[:, j, :].rearrange("p (c f) -> p c f", c=16)),
                    waits=wtr)
            for _ in range(4):
                if mean_state["w2"] < 128:
                    emit_mean_piece("w2", w2_in, F2, 8,
                                    ("v", vq1_done[NT - 1]))
            for grp in range(2):
                wts = [tr_w1[2 * span], tr_w1[2 * span + 1]]
                if (span - 1, grp) in ev1:
                    wts.append(("sc", ev1[(span - 1, grp)]))
                for ti in range(4):
                    t = grp * 4 + ti
                    w0 = list(wts) if ti == 0 else []
                    if span == 0:
                        w0.append(tr_xt1[t])
                    for c in range(16):
                        pl.op("tensor", lambda e, t=t, c=c, span=span:
                              e.matmul(PS[t][:, 0:MS1], XT1[:, t, c, :],
                                       W1T[span % 2][:, c, :],
                                       start=(c == 0), stop=(c == 15)),
                              waits=w0 if c == 0 else [],
                              incs=[pl.inc("t")] if (ti == 3 and c == 15)
                              else [])
                t_mm1[(span, grp)] = pl.val("t")
                for pair in range(2):
                    ts_ = [grp * 4 + 2 * pair, grp * 4 + 2 * pair + 1]
                    for t in ts_:
                        wev = [("t", t_mm1[(span, grp)])]
                        if span == 0:
                            wev += h2_gate[t]
                        pl.op("scalar", lambda s, t=t: s.activation(
                            OSB[:, t % 2, :], PS[t][:, 0:MS1], AF.Gelu,
                            scale=tc(t, 8)), waits=wev)
                    for t in ts_:
                        pl.op("scalar", lambda s, t=t, span=span:
                              s.activation(
                                  h2_tile(t)[:, span * MS1:(span + 1) * MS1],
                                  OSB[:, t % 2, :], AF.Gelu),
                              incs=[pl.inc("sc")])
                        gelu2_sc[(span, t)] = pl.val("sc")
                ev1[(span, grp)] = pl.val("sc")
            for t in range(NT):
                h2s = h2_tile(t)[:, span * MS1:(span + 1) * MS1]
                acc = tc(t, 10) if span == 0 else tc(t, 11)
                pl.op("vector", lambda v, h2s=h2s, acc=acc:
                      v.scalar_tensor_tensor(JNK[:], h2s, 1.0, h2s,
                                             OP.mult, OP.mult,
                                             accum_out=acc),
                      waits=[("sc", gelu2_sc[(span, t)])])
            for t in range(NT):
                h2s = h2_tile(t)[:, span * MS1:(span + 1) * MS1]
                dstm = tc(t, 9) if span == 0 else tc(t, 2)
                pl.op("vector", lambda v, h2s=h2s, dstm=dstm:
                      v.tensor_reduce(dstm, h2s, AX.X, OP.max,
                                      apply_absolute_value=True))
            if span > 0:
                for t in range(NT):
                    pl.op("vector", lambda v, t=t: v.tensor_tensor(
                        tc(t, 10), tc(t, 10), tc(t, 11), OP.add))
                for t in range(NT):
                    pl.op("vector", lambda v, t=t: v.tensor_tensor(
                        tc(t, 9), tc(t, 9), tc(t, 2), OP.max))
        pl.op("vector", lambda v: v.memset(col(cT2), 0.0),
              incs=[pl.inc("v")])
        v_h2stats = pl.val("v")

        tappoint("h2", [(h2_tile(t)[:, 0:2048],
                         [("sc", gelu2_sc[(NSPAN1 - 1, t)])])
                        for t in range(NT)])

        v_w2s = finalize_mean("w2", cM2, cSw2, cM2o127,
                              extra_mm_waits=[("sc", ev1[(NSPAN1 - 1, 0)])])

        q2_done = {}
        tr_xt2 = {}
        mm1_done = ("t", t_mm1[(NSPAN1 - 1, 1)])
        for t in range(NT):
            w0_ = [("v", v_h2stats), ("v", v_w2s)]
            vchain(lambda v, t=t: v.tensor_scalar(
                tc(t, 3), tc(t, 10), 1e-24, None, OP.max), extra_waits=w0_)
            vchain(lambda v, t=t: v.reciprocal(tc(t, 3), tc(t, 3)),
                   incs=[pl.inc("v")])
            vr = pl.val("v")
            pl.op("scalar", lambda s, t=t: s.activation(
                tc(t, 5), tc(t, 3), AF.Sqrt, scale=float(INNER)),
                waits=[("v", vr)], incs=[pl.inc("sc")])
            scq = pl.val("sc")
            vchain(lambda v, t=t: v.tensor_tensor(
                tc(t, 6), tc(t, 9), tc(t, 5), OP.mult),
                extra_waits=[("sc", scq)])
            vchain(lambda v, t=t: v.tensor_scalar(
                tc(t, 6), tc(t, 6), EPS_Q, None, OP.max))
            vchain(lambda v, t=t: v.reciprocal(tc(t, 7), tc(t, 6)))
            vchain(lambda v, t=t: v.tensor_tensor(
                tc(t, 8), tc(t, 6), col(cM2o127), OP.mult))
            vchain(lambda v, t=t: v.tensor_scalar(
                tc(t, 7), tc(t, 7), 127.0, None, OP.mult))
            vchain(lambda v, t=t: v.tensor_tensor(
                tc(t, 7), tc(t, 7), tc(t, 5), OP.mult),
                incs=[pl.inc("v")])
            v_s2t = pl.val("v")
            eng = "vector" if t % 2 == 0 else "gpsimd"
            esem = "v" if t % 2 == 0 else "g"
            fstage = F1 if t % 2 == 0 else F3
            wv0 = [("v", v_s2t)]
            if t % 2 == 0:
                wv0.append(("g", g_tern1[2 * NSPAN1 - 1]))
            else:
                wv0.append(("g", g_tern1[2 * NSPAN1 - 2]))
            for ch in range(4):
                h2c = h2_tile(t)[:, ch * D:(ch + 1) * D]
                pl.op(eng, lambda e, h2c=h2c, fstage=fstage, t=t:
                      e.tensor_scalar(fstage[:], h2c, tc(t, 7), col(cC),
                                      OP.mult, OP.add),
                      waits=wv0 if ch == 0 else [])
                gap(eng)
                pl.op(eng, lambda e, h2c=h2c, fstage=fstage:
                      e.tensor_scalar(h2c, fstage[:], col(cOne), col(cNegC),
                                      OP.mult, OP.add),
                      incs=[pl.inc(esem)] if ch == 3 else [])
            q2_done[t] = (esem, pl.val(esem))
            wv = [q2_done[t]]
            if t < 2:
                wv.append(mm1_done)
            else:
                wv.append(tr_xt2[t - 2])
            tr_xt2[t] = dma("t2A" if t % 2 == 0 else "t2B",
                            lambda s, t=t: s.dma_start_transpose(
                                xt2_tile(t), h2_tile(t).rearrange(
                                    "p (c f) -> p c f", c=64)), waits=wv)
        xt2_all = [tr_xt2[NT - 1], tr_xt2[NT - 2]]

        tern2_seq = {}
        tr_w2p = {}
        t_mm2 = {}
        copy_sc = {}
        tstg_free = {}
        ob_last = {}
        stt_v = {}
        nst = [0]
        store_led = {}
        xps_loaded = {}
        gelu_all = ("sc", gelu2_sc[(NSPAN1 - 1, NT - 1)])
        for slab in range(NSLAB2):
            wxp = [sp_all]
            if slab < 2:
                wxp.append(mm1_done)
            if slab >= 2:
                wxp.append(stt_v[slab - 2])
            xps_loaded[slab] = dma(
                "xpA" if slab % 2 == 0 else "xpB",
                lambda s, slab=slab: s.dma_start(
                    out=XPS[slab % 2][:],
                    in_=xp_hbm[slab].transpose([1, 0, 2])), waits=wxp)
            for p4 in range(4):
                piece = slab * 4 + p4
                stg = [F1, F3, F2][piece % 3]
                eng = "vector" if piece % 2 == 0 else "gpsimd"
                esem = "v" if eng == "vector" else "g"
                wv = []
                if piece >= 3:
                    wv.append(tern2_seq[piece - 3])
                elif piece < 2:
                    wv.append(q2_done[6 if piece % 2 == 0 else 7])
                else:
                    wv.append(("sc", mean_sc[("w2", 127)]))
                ld = dma(["s2A", "s2B", "s2C"][piece % 3],
                         lambda s, slab=slab, p4=p4, stg=stg: s.dma_start(
                             out=stg[:],
                             in_=w2_in[slab * P:(slab + 1) * P,
                                       p4 * D:(p4 + 1) * D]), waits=wv)
                wt0 = [ld]
                if piece < 2:
                    wt0 += xt2_all
                if piece >= 2:
                    wt0.append(tr_w2p[piece - 2])
                pl.op(eng, lambda e, stg=stg: e.tensor_scalar(
                    stg[:], stg[:], col(cSw2), col(cC), OP.mult, OP.add),
                    waits=wt0)
                gap(eng)
                pl.op(eng, lambda e, stg=stg: e.tensor_scalar(
                    stg[:], stg[:], col(cCp1), col(cCm1), OP.min, OP.max))
                gap(eng)
                pl.op(eng, lambda e, stg=stg, piece=piece: e.tensor_scalar(
                    PQ[piece % 2], stg[:], col(cOne), col(cNegC),
                    OP.mult, OP.add),
                    incs=[pl.inc(esem)])
                tern2_seq[piece] = (esem, pl.val(esem))
                wtr = [tern2_seq[piece]]
                if slab < 2 and p4 < 2:
                    wtr += xt2_all
                if slab >= 2:
                    wtr.append(("t", t_mm2[(slab - 2, 1)]))
                if slab >= 1:
                    wtr.append(tr_w2p[(slab - 1) * 4 + p4])
                tr_w2p[piece] = dma(
                    f"w2p{p4}",
                    lambda s, slab=slab, p4=p4, piece=piece:
                    s.dma_start_transpose(
                        W2T[slab % 2][:, p4 * 16:(p4 + 1) * 16, :],
                        PQ[piece % 2].rearrange("p (c f) -> p c f", c=16)),
                    waits=wtr)
            for half in range(2):
                bk = 2 * (slab % 2) + half
                wts = [xt2_all[0], xt2_all[1]]
                if slab < 2:
                    wts.append(gelu_all)
                    wts.append(("v", v_w2s))
                if (slab - 2, half) in copy_sc:
                    wts.append(("sc", copy_sc[(slab - 2, half)]))
                for c in range(64):
                    w = []
                    if c == 0:
                        w = wts + [tr_w2p[slab * 4]]
                    elif c % 16 == 0:
                        w = [tr_w2p[slab * 4 + c // 16]]
                    pl.op("tensor", lambda e, c=c, slab=slab, half=half,
                          bk=bk: e.matmul(
                              PS[bk][:],
                              W2T[slab % 2][:, c, :],
                              XT2V[:, 4 * half:4 * half + 4, c, :],
                              start=(c == 0), stop=(c == 63)),
                          waits=w,
                          incs=[pl.inc("t")] if c == 63 else [])
                t_mm2[(slab, half)] = pl.val("t")
                s_idx = (2 * slab + half) % 4
                wcp = [("t", t_mm2[(slab, half)])]
                if (s_idx,) in ob_last:
                    wcp.append(ob_last[(s_idx,)])
                pl.op("scalar", lambda s, bk=bk, s_idx=s_idx: s.copy(
                    MSTG[:, s_idx].rearrange("p c f -> p (c f)"),
                    PS[bk][:]),
                    waits=wcp, incs=[pl.inc("sc")])
                copy_sc[(slab, half)] = pl.val("sc")
                wob = [("sc", copy_sc[(slab, half)])]
                if (s_idx,) in tstg_free:
                    wob.append(tstg_free[(s_idx,)])
                ob = dma(f"ob{s_idx}",
                         lambda s, s_idx=s_idx: s.dma_start_transpose(
                             TSTG[:, s_idx], MSTG[:, s_idx]),
                         waits=wob)
                ob_last[(s_idx,)] = ob
                for ti in range(4):
                    t = 4 * half + ti
                    n = nst[0]
                    slot = n % 4
                    stsem = f"st{slot}"
                    oslot = OSB[:, slot // 2, (slot % 2) * 128:
                                (slot % 2) * 128 + 128]
                    wev = [ob, xps_loaded[slab]]
                    if (slot, "last") in store_led:
                        wev.append(store_led[(slot, "last")])
                    pl.op("vector", lambda v, t=t, slab=slab, oslot=oslot,
                          s_idx=s_idx, ti=ti: v.scalar_tensor_tensor(
                              oslot, TSTG[:, s_idx, ti, :], tc(t, 8),
                              XPS[slab % 2][:, t, :], OP.mult, OP.add),
                          waits=wev, incs=[pl.inc("v")])
                    vev = pl.val("v")
                    st = dma(stsem, lambda s, t=t, slab=slab, oslot=oslot:
                             s.dma_start(
                                 out=out_hbm[t * P:(t + 1) * P,
                                             slab * DK:(slab + 1) * DK],
                                 in_=oslot), waits=[("v", vev)],
                             eng="scalar")
                    store_led[(slot, "last")] = st
                    nst[0] += 1
                tstg_free[(s_idx,)] = ("v", pl.val("v"))
            stt_v[slab] = ("v", pl.val("v"))

        pl.op("sync", lambda s: s.nop(),
              waits=[(f"st{i}", pl.val(f"st{i}")) for i in range(4)])

        if hasattr(pl, "_tap_cut"):
            pl.ops = pl.ops[:pl._tap_cut]

        @block.sync
        def _(sync):
            pl.emit("sync", sync, handles)

        @block.vector
        def _(vector):
            pl.emit("vector", vector, handles)

        @block.scalar
        def _(scalar):
            pl.emit("scalar", scalar, handles)

        @block.gpsimd
        def _(gpsimd):
            pl.emit("gpsimd", gpsimd, handles)

        @block.tensor
        def _(tensor):
            pl.emit("tensor", tensor, handles)

    return nc


_NC_CACHE = []


def kernel(L0_attn_x, L0_attn_Wq, L0_attn_Wk, L0_attn_Wv, L0_attn_Wo,
           L0_ffn_w1, L0_ffn_w2):
    x = np.ascontiguousarray(np.asarray(L0_attn_x, dtype=np.float32)
                             .reshape(N_CORES * TOK, D))
    wq = np.ascontiguousarray(np.asarray(L0_attn_Wq, dtype=np.float32))
    wk = np.ascontiguousarray(np.asarray(L0_attn_Wk, dtype=np.float32))
    wv = np.ascontiguousarray(np.asarray(L0_attn_Wv, dtype=np.float32))
    wo = np.ascontiguousarray(np.asarray(L0_attn_Wo, dtype=np.float32))
    w1 = np.ascontiguousarray(np.asarray(L0_ffn_w1, dtype=np.float32))
    w2 = np.ascontiguousarray(np.asarray(L0_ffn_w2, dtype=np.float32))

    if not _NC_CACHE:
        _NC_CACHE.append(build_program())
    nc = _NC_CACHE[0]

    in_maps = []
    for c in range(N_CORES):
        in_maps.append({
            "x_in": x[c * TOK:(c + 1) * TOK],
            "wq_in": wq, "wk_in": wk, "wv_in": wv, "wo_in": wo,
            "w1_in": w1, "w2_in": w2,
        })
    res = run_bass_kernel_spmd(nc, in_maps, core_ids=list(range(N_CORES)))
    out = np.concatenate([np.asarray(r["out"]) for r in res.results], axis=0)
    return out.reshape(1, N_CORES * TOK, D).astype(np.float32)


# version 09
# speedup vs baseline: 1.1855x; 1.1848x over previous
import numpy as np
from contextlib import ExitStack

import concourse.bass as bass
import concourse.mybir as mybir
from concourse.bass_utils import run_bass_kernel_spmd

F32 = mybir.dt.float32
BF16 = mybir.dt.bfloat16
AF = mybir.ActivationFunctionType
OP = mybir.AluOpType
AX = mybir.AxisListType

N_CORES = 8
P = 128
D = 2048
INNER = 8192
TOK = 1024
NT = TOK // P
H = 16
DK = 128
EPS_LN = 1e-5
EPS_Q = 1e-5
EPS_W = 1e-5
C32 = float(1.5 * 2.0 ** 23)
C64 = float(1.5 * 2.0 ** 52)
MS1 = 256
NSPAN1 = INNER // MS1
NSLAB2 = D // P

SEMS = ["xA", "xB", "wA", "wB", "mA", "mB", "s1A", "s1B",
        "s2A", "s2B", "s2C",
        "xpA", "xpB", "txA", "txB", "toA", "toB", "t1A", "t1B", "t2A", "t2B",
        "w1A", "w1B", "w2p0", "w2p1", "w2p2", "w2p3", "sp", "toG0", "toG1",
        "ob0", "ob1", "ob2", "ob3",
        "st0", "st1", "st2", "st3", "v", "vv", "sc", "g", "t"]


class Plan:
    def __init__(self):
        self.ops = []
        self.sems = {}

    def sem(self, name):
        self.sems.setdefault(name, 0)

    def inc(self, name, n=1):
        self.sems[name] = self.sems.get(name, 0) + n
        return (name, n)

    def val(self, name):
        return self.sems.get(name, 0)

    def op(self, engine, fn, waits=(), incs=()):
        self.ops.append((engine, fn,
                         [w for w in waits if w is not None and w[1] > 0],
                         list(incs)))

    def emit(self, engine_name, eng, handles):
        for e, fn, waits, incs in self.ops:
            if e != engine_name:
                continue
            for (s, v) in waits:
                eng.wait_ge(handles[s], v)
            ins = fn(eng)
            for (s, amt) in incs:
                ins.then_inc(handles[s], amt)


def build_program(tap=None):
    nc = bass.Bass("TRN2", target_bir_lowering=False,
                   detect_race_conditions=False,
                   dynamic_dma_scratch_size=2048)

    x_in = nc.dram_tensor("x_in", [TOK, D], F32, kind="ExternalInput").ap()
    w_attn = {
        "q": nc.dram_tensor("wq_in", [D, D], F32, kind="ExternalInput").ap(),
        "k": nc.dram_tensor("wk_in", [D, D], F32, kind="ExternalInput").ap(),
        "v": nc.dram_tensor("wv_in", [D, D], F32, kind="ExternalInput").ap(),
        "o": nc.dram_tensor("wo_in", [D, D], F32, kind="ExternalInput").ap(),
    }
    w1_in = nc.dram_tensor("w1_in", [INNER, D], F32, kind="ExternalInput").ap()
    w2_in = nc.dram_tensor("w2_in", [D, INNER], F32, kind="ExternalInput").ap()
    out_hbm = nc.dram_tensor("out", [TOK, D], F32, kind="ExternalOutput").ap()
    xp_hbm = nc.dram_tensor("xp_spill", [NSLAB2, NT, P, DK], BF16,
                            kind="Internal").ap()

    pl = Plan()
    for s in SEMS:
        pl.sem(s)

    with ExitStack() as ctx:
        def sbt(name, shape, dt=BF16):
            return ctx.enter_context(nc.sbuf_tensor(name, shape, dt))

        AR14 = sbt("AR14", [P, 4, NT, D])
        AR1 = AR14[:, 0]
        AR2 = AR14[:, 1]
        AR3 = AR14[:, 2]
        AR4 = AR14[:, 3]
        AR8 = sbt("AR8", [P, 28672])
        F1 = sbt("F1", [P, D], F32)
        F2 = sbt("F2", [P, D], F32)
        F3 = sbt("F3", [P, D], F32)
        F5 = sbt("F5", [P, NT, H, H], F32)
        OSB = sbt("OSB", [P, 2, 256], F32)
        COLS = sbt("COLS", [P, 16], F32)
        TC = sbt("TC", [P, NT, 12], F32)
        MEANP = sbt("MEANP", [P, 128], F32)
        JNK = sbt("JNK", [P, 256], BF16)
        ONESF = sbt("ONESF", [P, 128], F32)

        PS = [ctx.enter_context(nc.psum_tensor(f"ps{i}", [P, 512], F32))
              for i in range(8)]

        handles = {s: ctx.enter_context(nc.semaphore(s)) for s in pl.sems}
        block = ctx.enter_context(nc.Block())

        cC, cProbe, cFlag, cTmp = 0, 1, 2, 3
        cM1, cSw1, cM1o127 = 4, 5, 6
        cM2, cSw2, cM2o127 = 7, 8, 9
        cT2, cCp1, cCm1, cZero = 10, 11, 12, 13
        cOne, cNegC = 14, 15

        def col(i):
            return COLS[:, i:i + 1]

        def tc(t, i):
            return TC[:, t, i:i + 1]

        XNT = AR1[:].rearrange("p t (c f) -> p t c f", c=16)
        XT1 = XNT
        q_bf = AR2
        k_bf = AR3
        o_bf = AR3

        def wqtr(i):
            return AR4[:, 4 * (i % 2):4 * (i % 2) + 4, :].rearrange(
                "p t f -> p (t f)").rearrange("p (c g) -> p c g", c=16)

        def ar8(lo, hi):
            return AR8[:, lo:hi]

        xn_buf = [ar8(0, 2048), ar8(2048, 4096)]
        tmp_buf = [ar8(4096, 6144), ar8(6144, 8192)]
        junk1k = ar8(8192, 9216)
        junkP0 = ar8(9216, 11264)
        gtmp = [ar8(0, 2048), ar8(2048, 4096)]
        VBF = ar8(12288, 28672).rearrange("p (t f) -> p t f", t=NT)

        def xp_t(t):
            return ar8(t * 2048, (t + 1) * 2048)

        XPV = AR8[:, 0:16384].rearrange("p (t f) -> p t f", t=NT)
        xint_buf = [ar8(16384, 18432), ar8(18432, 20480)]
        junkP5 = ar8(20480, 22528)
        W1Q = ar8(16384, 20480).rearrange("p (s f) -> p s f", s=2)
        W1T = [ar8(20480, 24576).rearrange("p (c f) -> p c f", c=16),
               ar8(24576, 28672).rearrange("p (c f) -> p c f", c=16)]

        def h2_tile(t):
            if t < 2:
                return AR2[:, 4 * t:4 * (t + 1), :].rearrange(
                    "p a b -> p (a b)")
            if t < 4:
                return AR3[:, 4 * (t - 2):4 * (t - 1), :].rearrange(
                    "p a b -> p (a b)")
            if t < 6:
                return AR4[:, 4 * (t - 4):4 * (t - 3), :].rearrange(
                    "p a b -> p (a b)")
            return ar8(8192 * (t - 6), 8192 * (t - 5))

        def xt2_tile(t):
            if t < 2:
                base = AR1[:, 4 * t:4 * (t + 1), :].rearrange(
                    "p a b -> p (a b)")
            else:
                base = h2_tile(t - 2)
            return base.rearrange("p (c f) -> p c f", c=64)

        W2T = [ar8(0, 8192).rearrange("p (c f) -> p c f", c=64),
               ar8(8192, 16384).rearrange("p (c f) -> p c f", c=64)]
        PQ = [ar8(16384, 18432), ar8(18432, 20480)]
        XPS = [ar8(20480, 21504).rearrange("p (t f) -> p t f", t=NT),
               ar8(21504, 22528).rearrange("p (t f) -> p t f", t=NT)]
        XT2V = AR14[:].rearrange("p a t d -> p (a t d)").rearrange(
            "p (T c f) -> p T c f", T=NT, c=64)
        MSTG = ar8(22528, 24576).rearrange("p (s c f) -> p s c f", s=4, c=4)
        TSTG = ar8(24576, 26624).rearrange("p (s c f) -> p s c f", s=4, c=4)

        def gap(eng="vector"):
            pl.op(eng, lambda e: e.memset(col(cT2), 0.0))

        def vchain(fn, extra_waits=(), incs=()):
            pl.op("vector", fn,
                  waits=[("vv", pl.val("vv"))] + list(extra_waits),
                  incs=[pl.inc("vv")])
            if incs:
                pl.op("vector", lambda v: v.memset(col(cT2), 0.0),
                      waits=[("vv", pl.val("vv"))], incs=list(incs))

        def dma(sem, fn, waits=(), eng="sync"):
            pl.op(eng, fn, waits=waits, incs=[pl.inc(sem, 16)])
            return (sem, pl.val(sem))

        def tappoint(name, rows_src):
            if tap != name:
                return
            for t, (ap, w) in enumerate(rows_src):
                W = min(ap.free_size(), 2048)
                pl.op("vector", lambda v, ap=ap, W=W: v.tensor_copy(
                    F3[:, 0:W], ap),
                    waits=list(w) + [("st0", pl.val("st0"))],
                    incs=[pl.inc("v")])
                vv_ = pl.val("v")
                dma("st0", lambda s, t=t, W=W: s.dma_start(
                    out=out_hbm[t * P:(t + 1) * P, 0:W], in_=F3[:, 0:W]),
                    waits=[("v", vv_)])
            pl.op("sync", lambda s: s.nop(),
                  waits=[("st0", pl.val("st0"))])
            pl._tap_cut = len(pl.ops)

        vchain(lambda v: v.memset(ONESF[:], 1.0))
        vchain(lambda v: v.memset(col(cProbe), 0.25))
        vchain(lambda v: v.tensor_scalar(
            col(cProbe), col(cProbe), C32, C32, OP.add, OP.subtract))
        vchain(lambda v: v.tensor_scalar(
            col(cFlag), col(cProbe), 0.1, None, OP.is_gt))
        vchain(lambda v: v.tensor_scalar(
            col(cC), col(cFlag), C64, None, OP.mult))
        vchain(lambda v: v.tensor_scalar(
            col(cTmp), col(cFlag), -C32, C32, OP.mult, OP.add))
        vchain(lambda v: v.tensor_tensor(
            col(cC), col(cC), col(cTmp), OP.add))
        vchain(lambda v: v.tensor_scalar(
            col(cCp1), col(cC), 1.0, None, OP.add))
        vchain(lambda v: v.tensor_scalar(
            col(cCm1), col(cC), -1.0, None, OP.add))
        vchain(lambda v: v.memset(col(cZero), 0.0))
        vchain(lambda v: v.memset(col(cOne), 1.0))
        vchain(lambda v: v.tensor_scalar(
            col(cNegC), col(cC), -1.0, None, OP.mult))
        vchain(lambda v: v.memset(JNK[:], 0.0), incs=[pl.inc("v")])
        vC = pl.val("v")

        def xstage(t):
            return F1 if t % 2 == 0 else F2

        def xsem(t):
            return "xA" if t % 2 == 0 else "xB"

        def txsem(t):
            return "txA" if t % 2 == 0 else "txB"

        v_xn = {}
        tr_xnt = {}
        x_ld = {}
        for t in range(NT):
            w = [("v", v_xn[t - 2])] if (t - 2) in v_xn else []
            x_ld[t] = dma(xsem(t), lambda s, t=t: s.dma_start(
                out=xstage(t)[:], in_=x_in[t * P:(t + 1) * P, :]), waits=w,
                eng="gpsimd")
            vchain(lambda v, t=t: v.tensor_reduce(
                tc(t, 0), xstage(t)[:], AX.X, OP.add),
                extra_waits=[x_ld[t], ("v", vC)])
            vchain(lambda v, t=t: v.scalar_tensor_tensor(
                junkP0, xstage(t)[:], 1.0, xstage(t)[:], OP.mult, OP.mult,
                accum_out=tc(t, 3)))
            vchain(lambda v, t=t: v.tensor_scalar(
                tc(t, 0), tc(t, 0), 1.0 / D, None, OP.mult))
            vchain(lambda v, t=t: v.tensor_scalar(
                tc(t, 3), tc(t, 3), 1.0 / D, None, OP.mult))
            vchain(lambda v, t=t: v.tensor_tensor(
                tc(t, 2), tc(t, 0), tc(t, 0), OP.mult))
            vchain(lambda v, t=t: v.tensor_tensor(
                tc(t, 3), tc(t, 3), tc(t, 2), OP.subtract))
            vchain(lambda v, t=t: v.tensor_scalar(
                tc(t, 3), tc(t, 3), EPS_LN, None, OP.add))
            vchain(lambda v, t=t: v.reciprocal(tc(t, 3), tc(t, 3)),
                   incs=[pl.inc("v")])
            vr = pl.val("v")
            pl.op("scalar", lambda s, t=t: s.activation(
                tc(t, 1), tc(t, 3), AF.Sqrt), waits=[("v", vr)],
                incs=[pl.inc("sc")])
            scr = pl.val("sc")
            vchain(lambda v, t=t: v.tensor_tensor(
                tc(t, 2), tc(t, 0), tc(t, 1), OP.mult),
                extra_waits=[("sc", scr)])
            wtr = [tr_xnt[t - 2]] if (t - 2) in tr_xnt else []
            vchain(lambda v, t=t: v.tensor_scalar(
                xn_buf[t % 2], xstage(t)[:], tc(t, 1), tc(t, 2),
                OP.mult, OP.subtract), extra_waits=wtr, incs=[pl.inc("v")])
            v_xn[t] = pl.val("v")
            tr_xnt[t] = dma(txsem(t), lambda s, t=t: s.dma_start_transpose(
                XNT[:, t], xn_buf[t % 2].rearrange("p (c f) -> p c f", c=16)),
                waits=[("v", v_xn[t])], eng="scalar")

        tappoint("xnt", [(XNT[:, t].rearrange("p c f -> p (c f)"),
                          [tr_xnt[t]]) for t in range(NT)])

        mean_state = {"w1": 0, "w2": 0}
        mean_sc = {}

        junk_f5 = F5[:].rearrange("p t h k -> p (t h k)")[:, 0:1024]

        def emit_mean_piece(tag, whbm, stage_buf, halves_per_row, first_wait):
            i = mean_state[tag]
            mean_state[tag] += 1
            row0 = i // halves_per_row
            cof = (i % halves_per_row) * 1024
            half = stage_buf[:, (i % 2) * 1024:(i % 2) * 1024 + 1024]
            junk = junk1k if tag == "w1" else junk_f5
            sem = "mA" if i % 2 == 0 else "mB"
            wv = []
            if (tag, i - 2) in mean_sc:
                wv.append(("sc", mean_sc[(tag, i - 2)]))
            elif i < 2 and first_wait is not None:
                wv.append(first_wait)
            ld = dma(sem, lambda s, whbm=whbm, row0=row0, cof=cof, half=half:
                     s.dma_start(out=half,
                                 in_=whbm[row0 * P:(row0 + 1) * P,
                                          cof:cof + 1024]), waits=wv)
            pl.op("scalar", lambda s, half=half, i=i, junk=junk: s.activation(
                junk, half, AF.Abs, accum_out=MEANP[:, i:i + 1]),
                waits=[ld], incs=[pl.inc("sc")])
            mean_sc[(tag, i)] = pl.val("sc")

        def finalize_mean(tag, cM, cSw, cMo127, extra_mm_waits=()):
            vchain(lambda v, cM=cM: v.tensor_reduce(
                col(cM), MEANP[:, 0:128], AX.X, OP.add),
                extra_waits=[("sc", mean_sc[(tag, 127)])],
                incs=[pl.inc("v")])
            vv_ = pl.val("v")
            pl.op("tensor", lambda e, cM=cM: e.matmul(
                PS[0][:, 0:1], ONESF[:], col(cM), start=True, stop=True),
                waits=[("v", vv_)] + list(extra_mm_waits),
                incs=[pl.inc("t")])
            tv = pl.val("t")
            vchain(lambda v, cM=cM: v.tensor_copy(
                col(cM), PS[0][:, 0:1]), extra_waits=[("t", tv)])
            vchain(lambda v, cM=cM: v.tensor_scalar(
                col(cM), col(cM), 1.0 / (INNER * D), None, OP.mult))
            vchain(lambda v, cM=cM: v.tensor_scalar(
                col(cM), col(cM), EPS_W, None, OP.max))
            vchain(lambda v, cM=cM, cSw=cSw: v.reciprocal(
                col(cSw), col(cM)))
            vchain(lambda v, cM=cM, cMo127=cMo127: v.tensor_scalar(
                col(cMo127), col(cM), 1.0 / 127.0, None, OP.mult))
            vchain(lambda v: v.memset(JNK[:], 0.0), incs=[pl.inc("v")])
            return pl.val("v")

        cvt_seq = []
        qtr_mm_done = {}
        sc_evac = {}
        t_mm = {}
        bank_free = {}
        g_scores = {}

        def attn_weight_phase(wname, stat_tr, dst, wi,
                              extra_evac_wait=None, bg=None):
            whbm = w_attn[wname]
            for qtr in range(4):
                gq = wi * 4 + qtr
                buf = wqtr(gq % 2)
                for i8 in range(8):
                    slot = i8 % 2
                    fsl = F3[:, slot * 1024:(slot + 1) * 1024]
                    sem = "wA" if slot == 0 else "wB"
                    wv = []
                    if len(cvt_seq) >= 2:
                        wv.append(("sc", cvt_seq[-2]))
                    if i8 == 0 and (gq - 2) in qtr_mm_done:
                        wv.append(("t", qtr_mm_done[gq - 2]))
                    src = whbm[i8 * 256:(i8 + 1) * 256,
                               qtr * 512:(qtr + 1) * 512]
                    ld = dma(sem, lambda s, src=src, fsl=fsl: s.dma_start(
                        out=fsl.rearrange("p (a g) -> p a g", a=2),
                        in_=src.rearrange("(a p) g -> p a g", p=128)),
                        waits=wv)
                    pl.op("scalar", lambda s, buf=buf, i8=i8, fsl=fsl:
                          s.copy(buf[:, 2 * i8:2 * i8 + 2, :].rearrange(
                              "p a g -> p (a g)"), fsl),
                          waits=[ld], incs=[pl.inc("sc")])
                    cvt_seq.append(pl.val("sc"))
                cvt_all = pl.val("sc")
                if bg is not None:
                    bg()
                for t in range(NT):
                    wts = [("sc", cvt_all), stat_tr[t]]
                    if t in bank_free:
                        wts.append(bank_free[t])
                    for c in range(16):
                        pl.op("tensor", lambda e, t=t, c=c, buf=buf:
                              e.matmul(PS[t][:], XNT[:, t, c, :],
                                       buf[:, c, :],
                                       start=(c == 0), stop=(c == 15)),
                              waits=wts if c == 0 else [],
                              incs=[pl.inc("t")] if c == 15 else [])
                    t_mm[(wname, qtr, t)] = pl.val("t")
                    wev = [("t", t_mm[(wname, qtr, t)])]
                    if extra_evac_wait is not None:
                        wev.extend(extra_evac_wait(t))
                    pl.op("scalar", lambda s, t=t, qtr=qtr, dst=dst:
                          s.copy(dst[:, t, qtr * 512:(qtr + 1) * 512],
                                 PS[t][:]),
                          waits=wev, incs=[pl.inc("sc")])
                    bank_free[t] = ("sc", pl.val("sc"))
                    if qtr == 3:
                        sc_evac[(wname, t)] = pl.val("sc")
                qtr_mm_done[gq] = pl.val("t")

        def mean_w1_bg():
            for _ in range(12):
                if mean_state["w1"] < 128:
                    emit_mean_piece("w1", w1_in, F2, 2, ("v", v_xn[NT - 1]))

        bg_count = [0]

        def bg_hook():
            bg_count[0] += 1
            if bg_count[0] >= 2:
                mean_w1_bg()

        attn_weight_phase("q", tr_xnt, q_bf, 0, bg=bg_hook)
        tappoint("q", [(q_bf[:, t, :], [("sc", sc_evac[("q", t)])])
                       for t in range(NT)])

        attn_weight_phase("k", tr_xnt, k_bf, 1, bg=bg_hook)
        tappoint("k", [(k_bf[:, t, :], [("sc", sc_evac[("k", t)])])
                       for t in range(NT)])

        attn_weight_phase("v", tr_xnt, VBF, 2, bg=bg_hook)

        v_odone = {}
        tr_ot = {}
        t_vdone = t_mm[("v", 3, NT - 1)]

        def emit_o(t, eng, esem, tmps):
            o3 = o_bf[:, t, :].rearrange("p (h f) -> p h f", h=16)

            def omul(kh, dst, t=t):
                return lambda v, kh=kh, dst=dst, t=t: v.tensor_tensor(
                    dst,
                    F5[:, t, :, kh:kh + 1].broadcast_to([P, H, DK]),
                    VBF[:, t, kh * DK:(kh + 1) * DK].unsqueeze(1)
                    .broadcast_to([P, H, DK]),
                    OP.mult)

            def oadd(kh, t=t):
                return lambda v, kh=kh, t=t: v.tensor_tensor(
                    o_bf[:, t, :].rearrange("p (h f) -> p h f", h=16),
                    o_bf[:, t, :].rearrange("p (h f) -> p h f", h=16),
                    tmps[kh % 2], OP.add)

            w0 = [("sc", sc_evac[("v", t)])]
            if eng != "vector":
                w0.append(g_scores[(t, H - 1)])
            pl.op(eng, omul(0, o3), waits=w0)
            pl.op(eng, omul(1, tmps[1]))
            for kh in range(2, H):
                pl.op(eng, omul(kh, tmps[kh % 2]))
                pl.op(eng, oadd(kh - 1))
            pl.op(eng, oadd(H - 1), incs=[pl.inc(esem)])
            v_odone[t] = (esem, pl.val(esem))

        def emit_trot():
            if tr_ot:
                return
            for t in range(NT):
                w = [v_odone[t], ("t", t_vdone)]
                if t >= 2:
                    w.append(tr_ot[t - 2])
                tr_ot[t] = dma("toA" if t % 2 == 0 else "toB",
                               lambda s, t=t: s.dma_start_transpose(
                                   XNT[:, t], o_bf[:, t, :].rearrange(
                                       "p (c f) -> p c f", c=16)),
                               waits=w)

        otmp_v = [tmp_buf[0].rearrange("p (h f) -> p h f", h=16),
                  tmp_buf[1].rearrange("p (h f) -> p h f", h=16)]
        otmp_g = [gtmp[0].rearrange("p (h f) -> p h f", h=16),
                  gtmp[1].rearrange("p (h f) -> p h f", h=16)]
        for t in range(NT):
            for hp in range(H // 2):
                for h in (2 * hp, 2 * hp + 1):
                    wts = []
                    if h == 0:
                        wts = [("sc", sc_evac[("q", t)]),
                               ("sc", sc_evac[("k", t)])]
                    qsl = q_bf[:, t, h * DK:(h + 1) * DK]
                    pl.op("vector", lambda v, t=t, h=h, qsl=qsl:
                          v.tensor_tensor(
                              tmp_buf[h % 2],
                              qsl.unsqueeze(1).broadcast_to([P, 16, DK]),
                              k_bf[:, t, :].rearrange("p (c f) -> p c f",
                                                      c=16),
                              OP.mult), waits=wts)
                for h in (2 * hp, 2 * hp + 1):
                    last = (h == H - 1)
                    pl.op("vector", lambda v, t=t, h=h: v.tensor_reduce(
                        F5[:, t, h, :], tmp_buf[h % 2].rearrange(
                            "p (c f) -> p c f", c=16), AX.X, OP.add),
                        incs=[pl.inc("v")] if last else [])
            g_scores[(t, H - 1)] = ("v", pl.val("v"))
            if t >= 3 and t <= 5:
                emit_o(t - 3, "vector", "v", otmp_v)

        tappoint("scores", [(F5[:, t].rearrange("p a b -> p (a b)"),
                             [g_scores[(t, H - 1)]]) for t in range(NT)])

        for t in range(3, NT):
            emit_o(t, "gpsimd", "g", otmp_g)
        v_odone_g_last = v_odone[NT - 1]

        tappoint("o", [(o_bf[:, t, :], [v_odone[t]]) for t in range(NT)])

        def wo_gate(t):
            w = []
            if t in (0, 1):
                w.append(v_odone_g_last)
            if t in (2, 3):
                w.append(g_scores[(NT - 1, H - 1)])
            if t not in (2, 3):
                w.append(("sc", mean_sc[("w1", 127)]))
            return w
        attn_weight_phase("o", tr_ot, XPV, 3, extra_evac_wait=wo_gate,
                          bg=emit_trot)

        v_w1s = finalize_mean("w1", cM1, cSw1, cM1o127,
                              extra_mm_waits=[("sc", sc_evac[("o", 0)])])

        vq1_done = {}
        tr_xt1 = {}
        for t in range(NT):
            xps = xp_t(t)
            fstage = F1 if t % 2 == 0 else F2
            wld = []
            if (t - 2) in vq1_done:
                wld.append(("v", vq1_done[t - 2]))
            elif t == 0:
                wld.append(("v", v_xn[NT - 2]))
            else:
                wld.append(("sc", mean_sc[("w1", 127)]))
            ld = dma(xsem(t), lambda s, t=t: s.dma_start(
                out=xstage(t)[:], in_=x_in[t * P:(t + 1) * P, :]), waits=wld)
            pl.op("vector", lambda v, xps=xps, fstage=fstage:
                  v.tensor_tensor(xps, xps, fstage[:], OP.add),
                  waits=[ld, ("sc", sc_evac[("o", t)])],
                  incs=[pl.inc("v")])
            vadd = pl.val("v")
            dma("sp", lambda s, t=t, xps=xps: s.dma_start(
                out=xp_hbm.transpose([2, 0, 1, 3])[:, :, t, :],
                in_=xps.rearrange("p (c f) -> p c f", c=16)),
                waits=[("v", vadd), ("sp", pl.val("sp"))])
            vchain(lambda v, xps=xps, t=t: v.tensor_reduce(
                tc(t, 0), xps, AX.X, OP.add), extra_waits=[("v", vadd)])
            vchain(lambda v, xps=xps, t=t: v.scalar_tensor_tensor(
                junkP5, xps, 1.0, xps, OP.mult, OP.mult, accum_out=tc(t, 3)))
            vchain(lambda v, t=t: v.tensor_scalar(
                tc(t, 0), tc(t, 0), 1.0 / D, None, OP.mult))
            vchain(lambda v, t=t: v.tensor_scalar(
                tc(t, 3), tc(t, 3), 1.0 / D, None, OP.mult))
            vchain(lambda v, t=t: v.tensor_tensor(
                tc(t, 2), tc(t, 0), tc(t, 0), OP.mult))
            vchain(lambda v, t=t: v.tensor_tensor(
                tc(t, 3), tc(t, 3), tc(t, 2), OP.subtract))
            vchain(lambda v, t=t: v.tensor_scalar(
                tc(t, 3), tc(t, 3), EPS_LN, None, OP.add))
            vchain(lambda v, t=t: v.reciprocal(tc(t, 3), tc(t, 3)),
                   incs=[pl.inc("v")])
            vr = pl.val("v")
            pl.op("scalar", lambda s, t=t: s.activation(
                tc(t, 1), tc(t, 3), AF.Sqrt), waits=[("v", vr)],
                incs=[pl.inc("sc")])
            scr = pl.val("sc")
            vchain(lambda v, t=t: v.tensor_tensor(
                tc(t, 2), tc(t, 0), tc(t, 1), OP.mult),
                extra_waits=[("sc", scr)])
            vchain(lambda v, xps=xps, fstage=fstage, t=t:
                   v.tensor_scalar(fstage[:], xps, tc(t, 1), tc(t, 2),
                                   OP.mult, OP.subtract))
            vchain(lambda v, fstage=fstage, t=t:
                   v.scalar_tensor_tensor(junkP5, fstage[:], 1.0, fstage[:],
                                          OP.mult, OP.mult,
                                          accum_out=tc(t, 3)))
            vchain(lambda v, fstage=fstage, t=t: v.tensor_reduce(
                tc(t, 4), fstage[:], AX.X, OP.max,
                apply_absolute_value=True))
            vchain(lambda v, t=t: v.tensor_scalar(
                tc(t, 3), tc(t, 3), 1e-24, None, OP.max))
            vchain(lambda v, t=t: v.reciprocal(tc(t, 3), tc(t, 3)),
                   incs=[pl.inc("v")])
            vr2 = pl.val("v")
            pl.op("scalar", lambda s, t=t: s.activation(
                tc(t, 5), tc(t, 3), AF.Sqrt, scale=float(D)),
                waits=[("v", vr2)], incs=[pl.inc("sc")])
            scf = pl.val("sc")
            vchain(lambda v, t=t: v.tensor_tensor(
                tc(t, 6), tc(t, 4), tc(t, 5), OP.mult),
                extra_waits=[("sc", scf), ("v", v_w1s)])
            vchain(lambda v, t=t: v.tensor_scalar(
                tc(t, 6), tc(t, 6), EPS_Q, None, OP.max))
            vchain(lambda v, t=t: v.reciprocal(tc(t, 7), tc(t, 6)))
            vchain(lambda v, t=t: v.tensor_tensor(
                tc(t, 8), tc(t, 6), col(cM1o127), OP.mult))
            vchain(lambda v, t=t: v.tensor_scalar(
                tc(t, 7), tc(t, 7), 127.0, None, OP.mult))
            vchain(lambda v, t=t: v.tensor_tensor(
                tc(t, 7), tc(t, 7), tc(t, 5), OP.mult))
            vchain(lambda v, fstage=fstage, t=t: v.tensor_scalar(
                fstage[:], fstage[:], tc(t, 7), col(cC), OP.mult, OP.add))
            wtr = [tr_xt1[t - 2]] if (t - 2) in tr_xt1 else []
            vchain(lambda v, fstage=fstage, t=t: v.tensor_scalar(
                xint_buf[t % 2], fstage[:], col(cC), None, OP.subtract),
                extra_waits=wtr, incs=[pl.inc("v")])
            vq1_done[t] = pl.val("v")
            tr_xt1[t] = dma("t1A" if t % 2 == 0 else "t1B",
                            lambda s, t=t: s.dma_start_transpose(
                                XT1[:, t], xint_buf[t % 2].rearrange(
                                    "p (c f) -> p c f", c=16)),
                            waits=[("v", vq1_done[t]),
                                   ("t", t_mm[("o", 3, t)])], eng="scalar")
        sp_all = ("sp", pl.val("sp"))

        tappoint("xp", [(xp_t(t), [("v", vq1_done[t])]) for t in range(NT)])
        tappoint("xt1", [(XT1[:, t].rearrange("p c f -> p (c f)"),
                          [tr_xt1[t]]) for t in range(NT)])
        tappoint("tc", [(TC[:].rearrange("p a b -> p (a b)"),
                         [("v", vq1_done[NT - 1])])])

        h2_gate = {}
        for t in range(NT):
            if t < 2:
                h2_gate[t] = [g_scores[(NT - 1, H - 1)]]
            elif t < 4:
                h2_gate[t] = [tr_ot[NT - 1], tr_ot[NT - 2]]
            elif t < 6:
                h2_gate[t] = [("t", qtr_mm_done[15])]
            else:
                h2_gate[t] = [sp_all, ("v", vq1_done[NT - 1])]

        xt1_all = [tr_xt1[NT - 1], tr_xt1[NT - 2]]
        g_tern1 = {}
        tr_w1 = {}
        t_mm1 = {}
        ev1 = {}
        gelu2_sc = {}
        for span in range(NSPAN1):
            for j in range(2):
                slab = 2 * span + j
                stg = F3 if j == 0 else F1
                sem = "s1A" if j == 0 else "s1B"
                wv = []
                if slab >= 2:
                    wv.append(("g", g_tern1[slab - 2]))
                elif j == 0:
                    wv.append(("sc", cvt_seq[-1]))
                    wv.append(("v", v_w1s))
                else:
                    wv.append(("v", vq1_done[NT - 2]))
                ld = dma(sem, lambda s, slab=slab, stg=stg: s.dma_start(
                    out=stg[:], in_=w1_in[slab * P:(slab + 1) * P, :]),
                    waits=wv)
                wq_gate = [ld]
                if span == 0:
                    wq_gate += xt1_all
                if span >= 1:
                    wq_gate.append(tr_w1[2 * (span - 1) + j])
                pl.op("gpsimd", lambda g, stg=stg: g.tensor_scalar(
                    stg[:], stg[:], col(cSw1), col(cC), OP.mult, OP.add),
                    waits=wq_gate)
                gap("gpsimd")
                pl.op("gpsimd", lambda g, stg=stg: g.tensor_scalar(
                    stg[:], stg[:], col(cCp1), col(cCm1), OP.min, OP.max))
                gap("gpsimd")
                pl.op("gpsimd", lambda g, stg=stg, j=j: g.tensor_scalar(
                    W1Q[:, j, :], stg[:], col(cOne), col(cNegC),
                    OP.mult, OP.add),
                    incs=[pl.inc("g")])
                g_tern1[slab] = pl.val("g")
                wtr = [("g", g_tern1[slab])]
                if span >= 2:
                    wtr.append(("t", t_mm1[(span - 2, 1)]))
                if span == 0:
                    wtr.append(("v", vq1_done[NT - 1]))
                if span >= 1:
                    wtr.append(tr_w1[2 * (span - 1) + j])
                tr_w1[slab] = dma(
                    "w1A" if j == 0 else "w1B",
                    lambda s, span=span, j=j: s.dma_start_transpose(
                        W1T[span % 2][:, :, j * 128:(j + 1) * 128],
                        W1Q[:, j, :].rearrange("p (c f) -> p c f", c=16)),
                    waits=wtr, eng="scalar")
            for _ in range(4):
                if mean_state["w2"] < 128:
                    emit_mean_piece("w2", w2_in, F2, 8,
                                    ("v", vq1_done[NT - 1]))
            for grp in range(2):
                wts = [tr_w1[2 * span], tr_w1[2 * span + 1]]
                if (span - 1, grp) in ev1:
                    wts.append(("sc", ev1[(span - 1, grp)]))
                for ti in range(4):
                    t = grp * 4 + ti
                    w0 = list(wts) if ti == 0 else []
                    if span == 0:
                        w0.append(tr_xt1[t])
                    for c in range(16):
                        pl.op("tensor", lambda e, t=t, c=c, span=span:
                              e.matmul(PS[t][:, 0:MS1], XT1[:, t, c, :],
                                       W1T[span % 2][:, c, :],
                                       start=(c == 0), stop=(c == 15)),
                              waits=w0 if c == 0 else [],
                              incs=[pl.inc("t")] if (ti == 3 and c == 15)
                              else [])
                t_mm1[(span, grp)] = pl.val("t")
                for pair in range(2):
                    ts_ = [grp * 4 + 2 * pair, grp * 4 + 2 * pair + 1]
                    for t in ts_:
                        wev = [("t", t_mm1[(span, grp)])]
                        if span == 0:
                            wev += h2_gate[t]
                        pl.op("scalar", lambda s, t=t: s.activation(
                            OSB[:, t % 2, :], PS[t][:, 0:MS1], AF.Gelu,
                            scale=tc(t, 8)), waits=wev)
                    for t in ts_:
                        pl.op("scalar", lambda s, t=t, span=span:
                              s.activation(
                                  h2_tile(t)[:, span * MS1:(span + 1) * MS1],
                                  OSB[:, t % 2, :], AF.Gelu),
                              incs=[pl.inc("sc")])
                        gelu2_sc[(span, t)] = pl.val("sc")
                ev1[(span, grp)] = pl.val("sc")
            for t in range(NT):
                h2s = h2_tile(t)[:, span * MS1:(span + 1) * MS1]
                acc = tc(t, 10) if span == 0 else tc(t, 11)
                pl.op("vector", lambda v, h2s=h2s, acc=acc:
                      v.scalar_tensor_tensor(JNK[:], h2s, 1.0, h2s,
                                             OP.mult, OP.mult,
                                             accum_out=acc),
                      waits=[("sc", gelu2_sc[(span, t)])])
            for t in range(NT):
                h2s = h2_tile(t)[:, span * MS1:(span + 1) * MS1]
                dstm = tc(t, 9) if span == 0 else tc(t, 2)
                pl.op("vector", lambda v, h2s=h2s, dstm=dstm:
                      v.tensor_reduce(dstm, h2s, AX.X, OP.max,
                                      apply_absolute_value=True))
            if span > 0:
                for t in range(NT):
                    pl.op("vector", lambda v, t=t: v.tensor_tensor(
                        tc(t, 10), tc(t, 10), tc(t, 11), OP.add))
                for t in range(NT):
                    pl.op("vector", lambda v, t=t: v.tensor_tensor(
                        tc(t, 9), tc(t, 9), tc(t, 2), OP.max))
        pl.op("vector", lambda v: v.memset(col(cT2), 0.0),
              incs=[pl.inc("v")])
        v_h2stats = pl.val("v")

        tappoint("h2", [(h2_tile(t)[:, 0:2048],
                         [("sc", gelu2_sc[(NSPAN1 - 1, t)])])
                        for t in range(NT)])

        v_w2s = finalize_mean("w2", cM2, cSw2, cM2o127,
                              extra_mm_waits=[("sc", ev1[(NSPAN1 - 1, 0)])])

        q2_done = {}
        tr_xt2 = {}
        mm1_done = ("t", t_mm1[(NSPAN1 - 1, 1)])
        for t in range(NT):
            w0_ = [("v", v_h2stats), ("v", v_w2s)]
            vchain(lambda v, t=t: v.tensor_scalar(
                tc(t, 3), tc(t, 10), 1e-24, None, OP.max), extra_waits=w0_)
            vchain(lambda v, t=t: v.reciprocal(tc(t, 3), tc(t, 3)),
                   incs=[pl.inc("v")])
            vr = pl.val("v")
            pl.op("scalar", lambda s, t=t: s.activation(
                tc(t, 5), tc(t, 3), AF.Sqrt, scale=float(INNER)),
                waits=[("v", vr)], incs=[pl.inc("sc")])
            scq = pl.val("sc")
            vchain(lambda v, t=t: v.tensor_tensor(
                tc(t, 6), tc(t, 9), tc(t, 5), OP.mult),
                extra_waits=[("sc", scq)])
            vchain(lambda v, t=t: v.tensor_scalar(
                tc(t, 6), tc(t, 6), EPS_Q, None, OP.max))
            vchain(lambda v, t=t: v.reciprocal(tc(t, 7), tc(t, 6)))
            vchain(lambda v, t=t: v.tensor_tensor(
                tc(t, 8), tc(t, 6), col(cM2o127), OP.mult))
            vchain(lambda v, t=t: v.tensor_scalar(
                tc(t, 7), tc(t, 7), 127.0, None, OP.mult))
            vchain(lambda v, t=t: v.tensor_tensor(
                tc(t, 7), tc(t, 7), tc(t, 5), OP.mult),
                incs=[pl.inc("v")])
            v_s2t = pl.val("v")
            eng = "vector" if t % 2 == 0 else "gpsimd"
            esem = "v" if t % 2 == 0 else "g"
            fstage = F1 if t % 2 == 0 else F3
            wv0 = [("v", v_s2t)]
            if t % 2 == 0:
                wv0.append(("g", g_tern1[2 * NSPAN1 - 1]))
            else:
                wv0.append(("g", g_tern1[2 * NSPAN1 - 2]))
            for ch in range(4):
                h2c = h2_tile(t)[:, ch * D:(ch + 1) * D]
                pl.op(eng, lambda e, h2c=h2c, fstage=fstage, t=t:
                      e.tensor_scalar(fstage[:], h2c, tc(t, 7), col(cC),
                                      OP.mult, OP.add),
                      waits=wv0 if ch == 0 else [])
                gap(eng)
                pl.op(eng, lambda e, h2c=h2c, fstage=fstage:
                      e.tensor_scalar(h2c, fstage[:], col(cOne), col(cNegC),
                                      OP.mult, OP.add),
                      incs=[pl.inc(esem)] if ch == 3 else [])
            q2_done[t] = (esem, pl.val(esem))
            wv = [q2_done[t]]
            if t < 2:
                wv.append(mm1_done)
            else:
                wv.append(tr_xt2[t - 2])
            tr_xt2[t] = dma("t2A" if t % 2 == 0 else "t2B",
                            lambda s, t=t: s.dma_start_transpose(
                                xt2_tile(t), h2_tile(t).rearrange(
                                    "p (c f) -> p c f", c=64)), waits=wv,
                            eng="scalar")
        xt2_all = [tr_xt2[NT - 1], tr_xt2[NT - 2]]

        tern2_seq = {}
        tr_w2p = {}
        t_mm2 = {}
        copy_sc = {}
        tstg_free = {}
        ob_last = {}
        stt_v = {}
        nst = [0]
        store_led = {}
        xps_loaded = {}
        gelu_all = ("sc", gelu2_sc[(NSPAN1 - 1, NT - 1)])
        for slab in range(NSLAB2):
            wxp = [sp_all]
            if slab < 2:
                wxp.append(mm1_done)
            if slab >= 2:
                wxp.append(stt_v[slab - 2])
            xps_loaded[slab] = dma(
                "xpA" if slab % 2 == 0 else "xpB",
                lambda s, slab=slab: s.dma_start(
                    out=XPS[slab % 2][:],
                    in_=xp_hbm[slab].transpose([1, 0, 2])), waits=wxp)
            for p4 in range(4):
                piece = slab * 4 + p4
                stg = [F1, F3, F2][piece % 3]
                eng = "vector" if piece % 2 == 0 else "gpsimd"
                esem = "v" if eng == "vector" else "g"
                wv = []
                if piece >= 3:
                    wv.append(tern2_seq[piece - 3])
                elif piece < 2:
                    wv.append(q2_done[6 if piece % 2 == 0 else 7])
                else:
                    wv.append(("sc", mean_sc[("w2", 127)]))
                ld = dma(["s2A", "s2B", "s2C"][piece % 3],
                         lambda s, slab=slab, p4=p4, stg=stg: s.dma_start(
                             out=stg[:],
                             in_=w2_in[slab * P:(slab + 1) * P,
                                       p4 * D:(p4 + 1) * D]), waits=wv)
                wt0 = [ld]
                if piece < 2:
                    wt0 += xt2_all
                if piece >= 2:
                    wt0.append(tr_w2p[piece - 2])
                pl.op(eng, lambda e, stg=stg: e.tensor_scalar(
                    stg[:], stg[:], col(cSw2), col(cC), OP.mult, OP.add),
                    waits=wt0)
                gap(eng)
                pl.op(eng, lambda e, stg=stg: e.tensor_scalar(
                    stg[:], stg[:], col(cCp1), col(cCm1), OP.min, OP.max))
                gap(eng)
                pl.op(eng, lambda e, stg=stg, piece=piece: e.tensor_scalar(
                    PQ[piece % 2], stg[:], col(cOne), col(cNegC),
                    OP.mult, OP.add),
                    incs=[pl.inc(esem)])
                tern2_seq[piece] = (esem, pl.val(esem))
                wtr = [tern2_seq[piece]]
                if slab < 2 and p4 < 2:
                    wtr += xt2_all
                if slab >= 2:
                    wtr.append(("t", t_mm2[(slab - 2, 1)]))
                if slab >= 1:
                    wtr.append(tr_w2p[(slab - 1) * 4 + p4])
                tr_w2p[piece] = dma(
                    f"w2p{p4}",
                    lambda s, slab=slab, p4=p4, piece=piece:
                    s.dma_start_transpose(
                        W2T[slab % 2][:, p4 * 16:(p4 + 1) * 16, :],
                        PQ[piece % 2].rearrange("p (c f) -> p c f", c=16)),
                    waits=wtr, eng="scalar")
            for half in range(2):
                bk = 2 * (slab % 2) + half
                wts = [xt2_all[0], xt2_all[1]]
                if slab < 2:
                    wts.append(gelu_all)
                    wts.append(("v", v_w2s))
                if (slab - 2, half) in copy_sc:
                    wts.append(("sc", copy_sc[(slab - 2, half)]))
                for c in range(64):
                    w = []
                    if c == 0:
                        w = wts + [tr_w2p[slab * 4]]
                    elif c % 16 == 0:
                        w = [tr_w2p[slab * 4 + c // 16]]
                    pl.op("tensor", lambda e, c=c, slab=slab, half=half,
                          bk=bk: e.matmul(
                              PS[bk][:],
                              W2T[slab % 2][:, c, :],
                              XT2V[:, 4 * half:4 * half + 4, c, :],
                              start=(c == 0), stop=(c == 63)),
                          waits=w,
                          incs=[pl.inc("t")] if c == 63 else [])
                t_mm2[(slab, half)] = pl.val("t")
                s_idx = (2 * slab + half) % 4
                wcp = [("t", t_mm2[(slab, half)])]
                if (s_idx,) in ob_last:
                    wcp.append(ob_last[(s_idx,)])
                pl.op("scalar", lambda s, bk=bk, s_idx=s_idx: s.copy(
                    MSTG[:, s_idx].rearrange("p c f -> p (c f)"),
                    PS[bk][:]),
                    waits=wcp, incs=[pl.inc("sc")])
                copy_sc[(slab, half)] = pl.val("sc")
                wob = [("sc", copy_sc[(slab, half)])]
                if (s_idx,) in tstg_free:
                    wob.append(tstg_free[(s_idx,)])
                ob = dma(f"ob{s_idx}",
                         lambda s, s_idx=s_idx: s.dma_start_transpose(
                             TSTG[:, s_idx], MSTG[:, s_idx]),
                         waits=wob, eng="scalar")
                ob_last[(s_idx,)] = ob
                for ti in range(4):
                    t = 4 * half + ti
                    n = nst[0]
                    slot = n % 4
                    stsem = f"st{slot}"
                    oslot = OSB[:, slot // 2, (slot % 2) * 128:
                                (slot % 2) * 128 + 128]
                    wev = [ob, xps_loaded[slab]]
                    if (slot, "last") in store_led:
                        wev.append(store_led[(slot, "last")])
                    pl.op("vector", lambda v, t=t, slab=slab, oslot=oslot,
                          s_idx=s_idx, ti=ti: v.scalar_tensor_tensor(
                              oslot, TSTG[:, s_idx, ti, :], tc(t, 8),
                              XPS[slab % 2][:, t, :], OP.mult, OP.add),
                          waits=wev, incs=[pl.inc("v")])
                    vev = pl.val("v")
                    st = dma(stsem, lambda s, t=t, slab=slab, oslot=oslot:
                             s.dma_start(
                                 out=out_hbm[t * P:(t + 1) * P,
                                             slab * DK:(slab + 1) * DK],
                                 in_=oslot), waits=[("v", vev)],
                             eng="scalar")
                    store_led[(slot, "last")] = st
                    nst[0] += 1
                tstg_free[(s_idx,)] = ("v", pl.val("v"))
            stt_v[slab] = ("v", pl.val("v"))

        pl.op("sync", lambda s: s.nop(),
              waits=[(f"st{i}", pl.val(f"st{i}")) for i in range(4)])

        if hasattr(pl, "_tap_cut"):
            pl.ops = pl.ops[:pl._tap_cut]

        @block.sync
        def _(sync):
            pl.emit("sync", sync, handles)

        @block.vector
        def _(vector):
            pl.emit("vector", vector, handles)

        @block.scalar
        def _(scalar):
            pl.emit("scalar", scalar, handles)

        @block.gpsimd
        def _(gpsimd):
            pl.emit("gpsimd", gpsimd, handles)

        @block.tensor
        def _(tensor):
            pl.emit("tensor", tensor, handles)

    return nc


_NC_CACHE = []


def kernel(L0_attn_x, L0_attn_Wq, L0_attn_Wk, L0_attn_Wv, L0_attn_Wo,
           L0_ffn_w1, L0_ffn_w2):
    x = np.ascontiguousarray(np.asarray(L0_attn_x, dtype=np.float32)
                             .reshape(N_CORES * TOK, D))
    wq = np.ascontiguousarray(np.asarray(L0_attn_Wq, dtype=np.float32))
    wk = np.ascontiguousarray(np.asarray(L0_attn_Wk, dtype=np.float32))
    wv = np.ascontiguousarray(np.asarray(L0_attn_Wv, dtype=np.float32))
    wo = np.ascontiguousarray(np.asarray(L0_attn_Wo, dtype=np.float32))
    w1 = np.ascontiguousarray(np.asarray(L0_ffn_w1, dtype=np.float32))
    w2 = np.ascontiguousarray(np.asarray(L0_ffn_w2, dtype=np.float32))

    if not _NC_CACHE:
        _NC_CACHE.append(build_program())
    nc = _NC_CACHE[0]

    in_maps = []
    for c in range(N_CORES):
        in_maps.append({
            "x_in": x[c * TOK:(c + 1) * TOK],
            "wq_in": wq, "wk_in": wk, "wv_in": wv, "wo_in": wo,
            "w1_in": w1, "w2_in": w2,
        })
    res = run_bass_kernel_spmd(nc, in_maps, core_ids=list(range(N_CORES)))
    out = np.concatenate([np.asarray(r["out"]) for r in res.results], axis=0)
    return out.reshape(1, N_CORES * TOK, D).astype(np.float32)


# version 11
# speedup vs baseline: 1.2618x; 1.0644x over previous
import numpy as np
from contextlib import ExitStack

import concourse.bass as bass
import concourse.mybir as mybir
from concourse.bass_utils import run_bass_kernel_spmd

F32 = mybir.dt.float32
BF16 = mybir.dt.bfloat16
AF = mybir.ActivationFunctionType
OP = mybir.AluOpType
AX = mybir.AxisListType

N_CORES = 8
P = 128
D = 2048
INNER = 8192
TOK = 1024
NT = TOK // P
H = 16
DK = 128
EPS_LN = 1e-5
EPS_Q = 1e-5
EPS_W = 1e-5
C32 = float(1.5 * 2.0 ** 23)
C64 = float(1.5 * 2.0 ** 52)
MS1 = 256
NSPAN1 = INNER // MS1
NSLAB2 = D // P

SEMS = ["xA", "xB", "wA", "wB", "mA", "mB", "s1A", "s1B",
        "s2A", "s2B", "s2C",
        "xpA", "xpB", "txA", "txB", "toA", "toB", "t1A", "t1B", "t2A", "t2B",
        "w1A", "w1B", "w2p0", "w2p1", "w2p2", "w2p3", "sp",
        "qq0", "qq1", "kk0", "kk1", "sr0", "sr1", "sr2", "sr3",
        "ob0", "ob1", "ob2", "ob3",
        "st0", "st1", "st2", "st3", "v", "vv", "sc", "g", "t"]


class Plan:
    def __init__(self):
        self.ops = []
        self.sems = {}

    def sem(self, name):
        self.sems.setdefault(name, 0)

    def inc(self, name, n=1):
        self.sems[name] = self.sems.get(name, 0) + n
        return (name, n)

    def val(self, name):
        return self.sems.get(name, 0)

    def op(self, engine, fn, waits=(), incs=()):
        self.ops.append((engine, fn,
                         [w for w in waits if w is not None and w[1] > 0],
                         list(incs)))

    def emit(self, engine_name, eng, handles):
        for e, fn, waits, incs in self.ops:
            if e != engine_name:
                continue
            for (s, v) in waits:
                eng.wait_ge(handles[s], v)
            ins = fn(eng)
            for (s, amt) in incs:
                ins.then_inc(handles[s], amt)


def build_program(tap=None):
    nc = bass.Bass("TRN2", target_bir_lowering=False,
                   detect_race_conditions=False,
                   dynamic_dma_scratch_size=2048)

    x_in = nc.dram_tensor("x_in", [TOK, D], F32, kind="ExternalInput").ap()
    w_attn = {
        "q": nc.dram_tensor("wq_in", [D, D], F32, kind="ExternalInput").ap(),
        "k": nc.dram_tensor("wk_in", [D, D], F32, kind="ExternalInput").ap(),
        "v": nc.dram_tensor("wv_in", [D, D], F32, kind="ExternalInput").ap(),
        "o": nc.dram_tensor("wo_in", [D, D], F32, kind="ExternalInput").ap(),
    }
    w1_in = nc.dram_tensor("w1_in", [INNER, D], F32, kind="ExternalInput").ap()
    w2_in = nc.dram_tensor("w2_in", [D, INNER], F32, kind="ExternalInput").ap()
    out_hbm = nc.dram_tensor("out", [TOK, D], F32, kind="ExternalOutput").ap()
    xp_hbm = nc.dram_tensor("xp_spill", [NSLAB2, NT, P, DK], BF16,
                            kind="Internal").ap()

    pl = Plan()
    for s in SEMS:
        pl.sem(s)

    with ExitStack() as ctx:
        def sbt(name, shape, dt=BF16):
            return ctx.enter_context(nc.sbuf_tensor(name, shape, dt))

        AR14 = sbt("AR14", [P, 4, NT, D])
        AR1 = AR14[:, 0]
        AR2 = AR14[:, 1]
        AR3 = AR14[:, 2]
        AR4 = AR14[:, 3]
        AR8 = sbt("AR8", [P, 28672])
        F1 = sbt("F1", [P, D], F32)
        F2 = sbt("F2", [P, D], F32)
        F3 = sbt("F3", [P, D], F32)
        F5 = sbt("F5", [P, NT, H, H])
        OSB = sbt("OSB", [P, 2, 256], F32)
        COLS = sbt("COLS", [P, 16], F32)
        TC = sbt("TC", [P, NT, 12], F32)
        MEANP = sbt("MEANP", [P, 128], F32)
        JNK = sbt("JNK", [P, 256], BF16)
        ONESF = sbt("ONESF", [P, 128], F32)

        PS = [ctx.enter_context(nc.psum_tensor(f"ps{i}", [P, 512], F32))
              for i in range(8)]

        handles = {s: ctx.enter_context(nc.semaphore(s)) for s in pl.sems}
        block = ctx.enter_context(nc.Block())

        cC, cProbe, cFlag, cTmp = 0, 1, 2, 3
        cM1, cSw1, cM1o127 = 4, 5, 6
        cM2, cSw2, cM2o127 = 7, 8, 9
        cT2, cCp1, cCm1, cZero = 10, 11, 12, 13
        cOne, cNegC = 14, 15

        def col(i):
            return COLS[:, i:i + 1]

        def tc(t, i):
            return TC[:, t, i:i + 1]

        XNT = AR1[:].rearrange("p t (c f) -> p t c f", c=16)
        XT1 = XNT
        q_bf = AR2
        k_bf = AR3
        o_bf = AR3

        def wqtr(i):
            return AR4[:, 4 * (i % 2):4 * (i % 2) + 4, :].rearrange(
                "p t f -> p (t f)").rearrange("p (c g) -> p c g", c=16)

        def ar8(lo, hi):
            return AR8[:, lo:hi]

        xn_buf = [ar8(0, 2048), ar8(2048, 4096)]
        tmp_buf = [ar8(4096, 6144), ar8(6144, 8192)]
        junk1k = ar8(8192, 9216)
        junkP0 = ar8(9216, 11264)
        gtmp = [ar8(0, 2048), ar8(2048, 4096)]
        VBF = ar8(12288, 28672).rearrange("p (t f) -> p t f", t=NT)

        def xp_t(t):
            return ar8(t * 2048, (t + 1) * 2048)

        XPV = AR8[:, 0:16384].rearrange("p (t f) -> p t f", t=NT)
        xint_buf = [ar8(16384, 18432), ar8(18432, 20480)]
        junkP5 = ar8(20480, 22528)
        W1Q = ar8(16384, 20480).rearrange("p (s f) -> p s f", s=2)
        W1T = [ar8(20480, 24576).rearrange("p (c f) -> p c f", c=16),
               ar8(24576, 28672).rearrange("p (c f) -> p c f", c=16)]

        def h2_tile(t):
            if t < 2:
                return AR2[:, 4 * t:4 * (t + 1), :].rearrange(
                    "p a b -> p (a b)")
            if t < 4:
                return AR3[:, 4 * (t - 2):4 * (t - 1), :].rearrange(
                    "p a b -> p (a b)")
            if t < 6:
                return AR4[:, 4 * (t - 4):4 * (t - 3), :].rearrange(
                    "p a b -> p (a b)")
            return ar8(8192 * (t - 6), 8192 * (t - 5))

        def xt2_tile(t):
            if t < 2:
                base = AR1[:, 4 * t:4 * (t + 1), :].rearrange(
                    "p a b -> p (a b)")
            else:
                base = h2_tile(t - 2)
            return base.rearrange("p (c f) -> p c f", c=64)

        W2T = [ar8(0, 8192).rearrange("p (c f) -> p c f", c=64),
               ar8(8192, 16384).rearrange("p (c f) -> p c f", c=64)]
        PQ = [ar8(16384, 18432), ar8(18432, 20480)]
        XPS = [ar8(20480, 21504).rearrange("p (t f) -> p t f", t=NT),
               ar8(21504, 22528).rearrange("p (t f) -> p t f", t=NT)]
        XT2V = AR14[:].rearrange("p a t d -> p (a t d)").rearrange(
            "p (T c f) -> p T c f", T=NT, c=64)
        MSTG = ar8(22528, 24576).rearrange("p (s c f) -> p s c f", s=4, c=4)
        TSTG = ar8(24576, 26624).rearrange("p (s c f) -> p s c f", s=4, c=4)

        def gap(eng="vector"):
            pl.op(eng, lambda e: e.memset(col(cT2), 0.0))

        def vchain(fn, extra_waits=(), incs=()):
            pl.op("vector", fn,
                  waits=[("vv", pl.val("vv"))] + list(extra_waits),
                  incs=[pl.inc("vv")])
            if incs:
                pl.op("vector", lambda v: v.memset(col(cT2), 0.0),
                      waits=[("vv", pl.val("vv"))], incs=list(incs))

        def dma(sem, fn, waits=(), eng="sync"):
            pl.op(eng, fn, waits=waits, incs=[pl.inc(sem, 16)])
            return (sem, pl.val(sem))

        def tappoint(name, rows_src):
            if tap != name:
                return
            for t, (ap, w) in enumerate(rows_src):
                W = min(ap.free_size(), 2048)
                pl.op("vector", lambda v, ap=ap, W=W: v.tensor_copy(
                    F3[:, 0:W], ap),
                    waits=list(w) + [("st0", pl.val("st0"))],
                    incs=[pl.inc("v")])
                vv_ = pl.val("v")
                dma("st0", lambda s, t=t, W=W: s.dma_start(
                    out=out_hbm[t * P:(t + 1) * P, 0:W], in_=F3[:, 0:W]),
                    waits=[("v", vv_)])
            pl.op("sync", lambda s: s.nop(),
                  waits=[("st0", pl.val("st0"))])
            pl._tap_cut = len(pl.ops)

        vchain(lambda v: v.memset(ONESF[:], 1.0))
        vchain(lambda v: v.memset(col(cProbe), 0.25))
        vchain(lambda v: v.tensor_scalar(
            col(cProbe), col(cProbe), C32, C32, OP.add, OP.subtract))
        vchain(lambda v: v.tensor_scalar(
            col(cFlag), col(cProbe), 0.1, None, OP.is_gt))
        vchain(lambda v: v.tensor_scalar(
            col(cC), col(cFlag), C64, None, OP.mult))
        vchain(lambda v: v.tensor_scalar(
            col(cTmp), col(cFlag), -C32, C32, OP.mult, OP.add))
        vchain(lambda v: v.tensor_tensor(
            col(cC), col(cC), col(cTmp), OP.add))
        vchain(lambda v: v.tensor_scalar(
            col(cCp1), col(cC), 1.0, None, OP.add))
        vchain(lambda v: v.tensor_scalar(
            col(cCm1), col(cC), -1.0, None, OP.add))
        vchain(lambda v: v.memset(col(cZero), 0.0))
        vchain(lambda v: v.memset(col(cOne), 1.0))
        vchain(lambda v: v.tensor_scalar(
            col(cNegC), col(cC), -1.0, None, OP.mult))
        vchain(lambda v: v.memset(JNK[:], 0.0), incs=[pl.inc("v")])
        vC = pl.val("v")

        def xstage(t):
            return F1 if t % 2 == 0 else F2

        def xsem(t):
            return "xA" if t % 2 == 0 else "xB"

        def txsem(t):
            return "txA" if t % 2 == 0 else "txB"

        v_xn = {}
        tr_xnt = {}
        x_ld = {}
        for t in range(NT):
            w = [("v", v_xn[t - 2])] if (t - 2) in v_xn else []
            x_ld[t] = dma(xsem(t), lambda s, t=t: s.dma_start(
                out=xstage(t)[:], in_=x_in[t * P:(t + 1) * P, :]), waits=w,
                eng="gpsimd")
            vchain(lambda v, t=t: v.tensor_reduce(
                tc(t, 0), xstage(t)[:], AX.X, OP.add),
                extra_waits=[x_ld[t], ("v", vC)])
            vchain(lambda v, t=t: v.scalar_tensor_tensor(
                junkP0, xstage(t)[:], 1.0, xstage(t)[:], OP.mult, OP.mult,
                accum_out=tc(t, 3)))
            vchain(lambda v, t=t: v.tensor_scalar(
                tc(t, 0), tc(t, 0), 1.0 / D, None, OP.mult))
            vchain(lambda v, t=t: v.tensor_scalar(
                tc(t, 3), tc(t, 3), 1.0 / D, None, OP.mult))
            vchain(lambda v, t=t: v.tensor_tensor(
                tc(t, 2), tc(t, 0), tc(t, 0), OP.mult))
            vchain(lambda v, t=t: v.tensor_tensor(
                tc(t, 3), tc(t, 3), tc(t, 2), OP.subtract))
            vchain(lambda v, t=t: v.tensor_scalar(
                tc(t, 3), tc(t, 3), EPS_LN, None, OP.add))
            vchain(lambda v, t=t: v.reciprocal(tc(t, 3), tc(t, 3)),
                   incs=[pl.inc("v")])
            vr = pl.val("v")
            pl.op("scalar", lambda s, t=t: s.activation(
                tc(t, 1), tc(t, 3), AF.Sqrt), waits=[("v", vr)],
                incs=[pl.inc("sc")])
            scr = pl.val("sc")
            vchain(lambda v, t=t: v.tensor_tensor(
                tc(t, 2), tc(t, 0), tc(t, 1), OP.mult),
                extra_waits=[("sc", scr)])
            wtr = [tr_xnt[t - 2]] if (t - 2) in tr_xnt else []
            vchain(lambda v, t=t: v.tensor_scalar(
                xn_buf[t % 2], xstage(t)[:], tc(t, 1), tc(t, 2),
                OP.mult, OP.subtract), extra_waits=wtr, incs=[pl.inc("v")])
            v_xn[t] = pl.val("v")
            tr_xnt[t] = dma(txsem(t), lambda s, t=t: s.dma_start_transpose(
                XNT[:, t], xn_buf[t % 2].rearrange("p (c f) -> p c f", c=16)),
                waits=[("v", v_xn[t])], eng="scalar")

        tappoint("xnt", [(XNT[:, t].rearrange("p c f -> p (c f)"),
                          [tr_xnt[t]]) for t in range(NT)])

        mean_state = {"w1": 0, "w2": 0}
        mean_sc = {}

        junk_f5 = F5[:].rearrange("p t h k -> p (t h k)")[:, 0:1024]

        def emit_mean_piece(tag, whbm, stage_buf, halves_per_row, first_wait):
            i = mean_state[tag]
            mean_state[tag] += 1
            row0 = i // halves_per_row
            cof = (i % halves_per_row) * 1024
            half = stage_buf[:, (i % 2) * 1024:(i % 2) * 1024 + 1024]
            junk = junk1k if tag == "w1" else junk_f5
            sem = "mA" if i % 2 == 0 else "mB"
            wv = []
            if (tag, i - 2) in mean_sc:
                wv.append(("sc", mean_sc[(tag, i - 2)]))
            elif i < 2 and first_wait is not None:
                wv.append(first_wait)
            ld = dma(sem, lambda s, whbm=whbm, row0=row0, cof=cof, half=half:
                     s.dma_start(out=half,
                                 in_=whbm[row0 * P:(row0 + 1) * P,
                                          cof:cof + 1024]), waits=wv)
            pl.op("scalar", lambda s, half=half, i=i, junk=junk: s.activation(
                junk, half, AF.Abs, accum_out=MEANP[:, i:i + 1]),
                waits=[ld], incs=[pl.inc("sc")])
            mean_sc[(tag, i)] = pl.val("sc")

        def finalize_mean(tag, cM, cSw, cMo127, extra_mm_waits=()):
            vchain(lambda v, cM=cM: v.tensor_reduce(
                col(cM), MEANP[:, 0:128], AX.X, OP.add),
                extra_waits=[("sc", mean_sc[(tag, 127)])],
                incs=[pl.inc("v")])
            vv_ = pl.val("v")
            pl.op("tensor", lambda e, cM=cM: e.matmul(
                PS[0][:, 0:1], ONESF[:], col(cM), start=True, stop=True),
                waits=[("v", vv_)] + list(extra_mm_waits),
                incs=[pl.inc("t")])
            tv = pl.val("t")
            vchain(lambda v, cM=cM: v.tensor_copy(
                col(cM), PS[0][:, 0:1]), extra_waits=[("t", tv)])
            vchain(lambda v, cM=cM: v.tensor_scalar(
                col(cM), col(cM), 1.0 / (INNER * D), None, OP.mult))
            vchain(lambda v, cM=cM: v.tensor_scalar(
                col(cM), col(cM), EPS_W, None, OP.max))
            vchain(lambda v, cM=cM, cSw=cSw: v.reciprocal(
                col(cSw), col(cM)))
            vchain(lambda v, cM=cM, cMo127=cMo127: v.tensor_scalar(
                col(cMo127), col(cM), 1.0 / 127.0, None, OP.mult))
            vchain(lambda v: v.memset(JNK[:], 0.0), incs=[pl.inc("v")])
            return pl.val("v")

        cvt_seq = []
        qtr_mm_done = {}
        sc_evac = {}
        t_mm = {}
        bank_free = {}
        g_scores = {}

        def attn_weight_phase(wname, stat_tr, dst, wi,
                              extra_evac_wait=None, bg=None):
            whbm = w_attn[wname]
            for qtr in range(4):
                gq = wi * 4 + qtr
                buf = wqtr(gq % 2)
                for i8 in range(8):
                    slot = i8 % 2
                    fsl = F3[:, slot * 1024:(slot + 1) * 1024]
                    sem = "wA" if slot == 0 else "wB"
                    wv = []
                    if len(cvt_seq) >= 2:
                        wv.append(("sc", cvt_seq[-2]))
                    if i8 == 0 and (gq - 2) in qtr_mm_done:
                        wv.append(("t", qtr_mm_done[gq - 2]))
                    src = whbm[i8 * 256:(i8 + 1) * 256,
                               qtr * 512:(qtr + 1) * 512]
                    ld = dma(sem, lambda s, src=src, fsl=fsl: s.dma_start(
                        out=fsl.rearrange("p (a g) -> p a g", a=2),
                        in_=src.rearrange("(a p) g -> p a g", p=128)),
                        waits=wv)
                    pl.op("scalar", lambda s, buf=buf, i8=i8, fsl=fsl:
                          s.copy(buf[:, 2 * i8:2 * i8 + 2, :].rearrange(
                              "p a g -> p (a g)"), fsl),
                          waits=[ld], incs=[pl.inc("sc")])
                    cvt_seq.append(pl.val("sc"))
                cvt_all = pl.val("sc")
                if bg is not None:
                    bg()
                for t in range(NT):
                    wts = [("sc", cvt_all), stat_tr[t]]
                    if t in bank_free:
                        wts.append(bank_free[t])
                    for c in range(16):
                        pl.op("tensor", lambda e, t=t, c=c, buf=buf:
                              e.matmul(PS[t][:], XNT[:, t, c, :],
                                       buf[:, c, :],
                                       start=(c == 0), stop=(c == 15)),
                              waits=wts if c == 0 else [],
                              incs=[pl.inc("t")] if c == 15 else [])
                    t_mm[(wname, qtr, t)] = pl.val("t")
                    wev = [("t", t_mm[(wname, qtr, t)])]
                    if extra_evac_wait is not None:
                        wev.extend(extra_evac_wait(t))
                    pl.op("scalar", lambda s, t=t, qtr=qtr, dst=dst:
                          s.copy(dst[:, t, qtr * 512:(qtr + 1) * 512],
                                 PS[t][:]),
                          waits=wev, incs=[pl.inc("sc")])
                    bank_free[t] = ("sc", pl.val("sc"))
                    if qtr == 3:
                        sc_evac[(wname, t)] = pl.val("sc")
                qtr_mm_done[gq] = pl.val("t")

        def mean_w1_bg():
            for _ in range(12):
                if mean_state["w1"] < 128:
                    emit_mean_piece("w1", w1_in, F2, 2, ("v", v_xn[NT - 1]))

        bg_count = [0]

        def bg_hook():
            bg_count[0] += 1
            if bg_count[0] >= 2:
                mean_w1_bg()

        attn_weight_phase("q", tr_xnt, q_bf, 0, bg=bg_hook)
        tappoint("q", [(q_bf[:, t, :], [("sc", sc_evac[("q", t)])])
                       for t in range(NT)])

        attn_weight_phase("k", tr_xnt, k_bf, 1, bg=bg_hook)
        tappoint("k", [(k_bf[:, t, :], [("sc", sc_evac[("k", t)])])
                       for t in range(NT)])

        attn_weight_phase("v", tr_xnt, VBF, 2, bg=bg_hook)

        v_odone = {}
        tr_ot = {}
        t_vdone = t_mm[("v", 3, NT - 1)]
        QKZ = [(ar8(0, 2048), ar8(2048, 4096)),
               (ar8(4096, 6144), ar8(6144, 8192))]

        def qkview(t, which):
            return QKZ[t % 2][which].rearrange("p (h f) -> p h f", h=16)

        qk_tr = {}
        for t in range(NT):
            qk_tr[(t, 0)] = ("qq" + str(t % 2), 16 * (t // 2 + 1))
            qk_tr[(t, 1)] = ("kk" + str(t % 2), 16 * (t // 2 + 1))

        gmm = {}
        smm = {}
        scp = {}
        f5cp = {}
        otmp = ar8(9216, 11264).rearrange("p (h f) -> p h f", h=16)
        F1V = F1[0:16, :].rearrange("p (k tok) -> p k tok", k=16)
        ID16 = OSB[0:16, 0, 0:16]
        psv = {}
        for t in range(NT):
            psv[t] = PS[t][0:16, :].rearrange("p (k g) -> p k g", k=16)

        pl.op("gpsimd", lambda g: g.memset(ID16, 0.0))
        pl.op("gpsimd", lambda g: g.affine_select(
            out=ID16, in_=ID16, compare_op=OP.not_equal, fill=1.0,
            base=0, pattern=[[-1, 16]], channel_multiplier=1),
            incs=[pl.inc("g")])
        id_done = ("g", pl.val("g"))

        def emit_o(t):
            o3 = o_bf[:, t, :].rearrange("p (h f) -> p h f", h=16)

            def omul(kh, dst, t=t):
                return lambda v, kh=kh, dst=dst, t=t: v.tensor_tensor(
                    dst,
                    F5[:, t, kh, :].unsqueeze(2).broadcast_to([P, H, DK]),
                    VBF[:, t, kh * DK:(kh + 1) * DK].unsqueeze(1)
                    .broadcast_to([P, H, DK]),
                    OP.mult)

            def oadd(t=t):
                return lambda v, t=t: v.tensor_tensor(
                    o_bf[:, t, :].rearrange("p (h f) -> p h f", h=16),
                    o_bf[:, t, :].rearrange("p (h f) -> p h f", h=16),
                    otmp, OP.add)

            w0 = [("sc", sc_evac[("v", t)]), ("v", f5cp[t])]
            pl.op("vector", omul(0, o3), waits=w0)
            for kh in range(1, H):
                pl.op("vector", omul(kh, otmp))
                pl.op("vector", oadd(),
                      incs=[pl.inc("v")] if kh == H - 1 else [])
            v_odone[t] = ("v", pl.val("v"))

        for t in range(NT):
            for g in range(4):
                for tau in range(32):
                    w = []
                    if tau == 0:
                        if g == 0:
                            w = [qk_tr[(t, 0)], qk_tr[(t, 1)], bank_free[t]]
                        else:
                            w = [("v", scp[(t, g - 1)])]
                    pl.op("tensor", lambda e, t=t, g=g, tau=tau: e.matmul(
                        psv[t][:, :, tau],
                        qkview(t, 0)[:, :, g * 32 + tau],
                        qkview(t, 1)[:, :, g * 32 + tau],
                        start=True, stop=True),
                        waits=w,
                        incs=[pl.inc("t")] if tau == 31 else [])
                gmm[(t, g)] = pl.val("t")
                w = [("t", gmm[(t, g)])]
                pl.op("vector", lambda v, t=t, g=g: v.tensor_copy(
                    F1V[:, :, g * 32:(g + 1) * 32],
                    psv[t]),
                    waits=w, incs=[pl.inc("v")])
                scp[(t, g)] = pl.val("v")
            for kh in range(H):
                w = []
                if kh == 0:
                    w = [("v", scp[(t, 3)])]
                    if t == 0:
                        w.append(id_done)
                pl.op("tensor", lambda e, t=t, kh=kh: e.transpose(
                    PS[t][:, kh * 16:(kh + 1) * 16],
                    F1[0:16, kh * 128:(kh + 1) * 128],
                    ID16),
                    waits=w,
                    incs=[pl.inc("t")] if kh == H - 1 else [])
            smm[t] = ("t", pl.val("t"))
            pl.op("vector", lambda v, t=t: v.tensor_copy(
                F5[:, t, :, :], PS[t][:, 0:256].rearrange(
                    "p (a b) -> p a b", a=16)),
                waits=[smm[t]], incs=[pl.inc("v")])
            f5cp[t] = pl.val("v")
            if t >= 1:
                emit_o(t - 1)
        emit_o(NT - 1)
        for t in range(NT):
            bank_free[t] = ("v", f5cp[t])

        def emit_trot():
            if tr_ot:
                return

            def qkdma(t):
                for which, src in ((0, q_bf), (1, k_bf)):
                    w = [("sc", sc_evac[("q" if which == 0 else "k", t)])]
                    if t >= 2:
                        w.append(smm[t - 2])
                    got = dma(qk_tr[(t, 0)][0] if which == 0
                              else qk_tr[(t, 1)][0],
                              lambda s, t=t, which=which, src=src:
                              s.dma_start_transpose(
                                  qkview(t, which),
                                  src[:, t, :].rearrange(
                                      "p (c f) -> p c f", c=16)),
                              waits=w)
                    assert got == qk_tr[(t, which)], (got, t, which)

            qkdma(0)
            qkdma(1)
            for t in range(NT):
                w = [v_odone[t], ("t", t_vdone)]
                if t >= 2:
                    w.append(tr_ot[t - 2])
                tr_ot[t] = dma("toA" if t % 2 == 0 else "toB",
                               lambda s, t=t: s.dma_start_transpose(
                                   XNT[:, t], o_bf[:, t, :].rearrange(
                                       "p (c f) -> p c f", c=16)),
                               waits=w)
                if t + 2 < NT:
                    qkdma(t + 2)

        tappoint("o", [(o_bf[:, t, :], [v_odone[t]]) for t in range(NT)])

        def wo_gate(t):
            w = []
            if t < 4:
                w.append(smm[NT - 1])
            if t in (4, 5):
                w.append(v_odone[NT - 1])
            if t not in (2, 3):
                w.append(("sc", mean_sc[("w1", 127)]))
            return w
        attn_weight_phase("o", tr_ot, XPV, 3, extra_evac_wait=wo_gate,
                          bg=emit_trot)

        v_w1s = finalize_mean("w1", cM1, cSw1, cM1o127,
                              extra_mm_waits=[("sc", sc_evac[("o", 0)])])

        vq1_done = {}
        tr_xt1 = {}
        for t in range(NT):
            xps = xp_t(t)
            fstage = F1 if t % 2 == 0 else F2
            wld = []
            if (t - 2) in vq1_done:
                wld.append(("v", vq1_done[t - 2]))
            elif t == 0:
                wld.append(("v", v_xn[NT - 2]))
            else:
                wld.append(("sc", mean_sc[("w1", 127)]))
            ld = dma(xsem(t), lambda s, t=t: s.dma_start(
                out=xstage(t)[:], in_=x_in[t * P:(t + 1) * P, :]), waits=wld)
            pl.op("vector", lambda v, xps=xps, fstage=fstage:
                  v.tensor_tensor(xps, xps, fstage[:], OP.add),
                  waits=[ld, ("sc", sc_evac[("o", t)])],
                  incs=[pl.inc("v")])
            vadd = pl.val("v")
            dma("sp", lambda s, t=t, xps=xps: s.dma_start(
                out=xp_hbm.transpose([2, 0, 1, 3])[:, :, t, :],
                in_=xps.rearrange("p (c f) -> p c f", c=16)),
                waits=[("v", vadd), ("sp", pl.val("sp"))])
            vchain(lambda v, xps=xps, t=t: v.tensor_reduce(
                tc(t, 0), xps, AX.X, OP.add), extra_waits=[("v", vadd)])
            vchain(lambda v, xps=xps, t=t: v.scalar_tensor_tensor(
                junkP5, xps, 1.0, xps, OP.mult, OP.mult, accum_out=tc(t, 3)))
            vchain(lambda v, t=t: v.tensor_scalar(
                tc(t, 0), tc(t, 0), 1.0 / D, None, OP.mult))
            vchain(lambda v, t=t: v.tensor_scalar(
                tc(t, 3), tc(t, 3), 1.0 / D, None, OP.mult))
            vchain(lambda v, t=t: v.tensor_tensor(
                tc(t, 2), tc(t, 0), tc(t, 0), OP.mult))
            vchain(lambda v, t=t: v.tensor_tensor(
                tc(t, 3), tc(t, 3), tc(t, 2), OP.subtract))
            vchain(lambda v, t=t: v.tensor_scalar(
                tc(t, 3), tc(t, 3), EPS_LN, None, OP.add))
            vchain(lambda v, t=t: v.reciprocal(tc(t, 3), tc(t, 3)),
                   incs=[pl.inc("v")])
            vr = pl.val("v")
            pl.op("scalar", lambda s, t=t: s.activation(
                tc(t, 1), tc(t, 3), AF.Sqrt), waits=[("v", vr)],
                incs=[pl.inc("sc")])
            scr = pl.val("sc")
            vchain(lambda v, t=t: v.tensor_tensor(
                tc(t, 2), tc(t, 0), tc(t, 1), OP.mult),
                extra_waits=[("sc", scr)])
            vchain(lambda v, xps=xps, fstage=fstage, t=t:
                   v.tensor_scalar(fstage[:], xps, tc(t, 1), tc(t, 2),
                                   OP.mult, OP.subtract))
            vchain(lambda v, fstage=fstage, t=t:
                   v.scalar_tensor_tensor(junkP5, fstage[:], 1.0, fstage[:],
                                          OP.mult, OP.mult,
                                          accum_out=tc(t, 3)))
            vchain(lambda v, fstage=fstage, t=t: v.tensor_reduce(
                tc(t, 4), fstage[:], AX.X, OP.max,
                apply_absolute_value=True))
            vchain(lambda v, t=t: v.tensor_scalar(
                tc(t, 3), tc(t, 3), 1e-24, None, OP.max))
            vchain(lambda v, t=t: v.reciprocal(tc(t, 3), tc(t, 3)),
                   incs=[pl.inc("v")])
            vr2 = pl.val("v")
            pl.op("scalar", lambda s, t=t: s.activation(
                tc(t, 5), tc(t, 3), AF.Sqrt, scale=float(D)),
                waits=[("v", vr2)], incs=[pl.inc("sc")])
            scf = pl.val("sc")
            vchain(lambda v, t=t: v.tensor_tensor(
                tc(t, 6), tc(t, 4), tc(t, 5), OP.mult),
                extra_waits=[("sc", scf), ("v", v_w1s)])
            vchain(lambda v, t=t: v.tensor_scalar(
                tc(t, 6), tc(t, 6), EPS_Q, None, OP.max))
            vchain(lambda v, t=t: v.reciprocal(tc(t, 7), tc(t, 6)))
            vchain(lambda v, t=t: v.tensor_tensor(
                tc(t, 8), tc(t, 6), col(cM1o127), OP.mult))
            vchain(lambda v, t=t: v.tensor_scalar(
                tc(t, 7), tc(t, 7), 127.0, None, OP.mult))
            vchain(lambda v, t=t: v.tensor_tensor(
                tc(t, 7), tc(t, 7), tc(t, 5), OP.mult))
            vchain(lambda v, fstage=fstage, t=t: v.tensor_scalar(
                fstage[:], fstage[:], tc(t, 7), col(cC), OP.mult, OP.add))
            wtr = [tr_xt1[t - 2]] if (t - 2) in tr_xt1 else []
            vchain(lambda v, fstage=fstage, t=t: v.tensor_scalar(
                xint_buf[t % 2], fstage[:], col(cC), None, OP.subtract),
                extra_waits=wtr, incs=[pl.inc("v")])
            vq1_done[t] = pl.val("v")
            tr_xt1[t] = dma("t1A" if t % 2 == 0 else "t1B",
                            lambda s, t=t: s.dma_start_transpose(
                                XT1[:, t], xint_buf[t % 2].rearrange(
                                    "p (c f) -> p c f", c=16)),
                            waits=[("v", vq1_done[t]),
                                   ("t", t_mm[("o", 3, t)])], eng="scalar")
        sp_all = ("sp", pl.val("sp"))

        tappoint("xp", [(xp_t(t), [("v", vq1_done[t])]) for t in range(NT)])
        tappoint("xt1", [(XT1[:, t].rearrange("p c f -> p (c f)"),
                          [tr_xt1[t]]) for t in range(NT)])
        tappoint("tc", [(TC[:].rearrange("p a b -> p (a b)"),
                         [("v", vq1_done[NT - 1])])])

        h2_gate = {}
        for t in range(NT):
            if t < 2:
                h2_gate[t] = [qk_tr[(NT - 1, 0)]]
            elif t < 4:
                h2_gate[t] = [tr_ot[NT - 1], tr_ot[NT - 2]]
            elif t < 6:
                h2_gate[t] = [("t", qtr_mm_done[15])]
            else:
                h2_gate[t] = [sp_all, ("v", vq1_done[NT - 1])]

        xt1_all = [tr_xt1[NT - 1], tr_xt1[NT - 2]]
        g_tern1 = {}
        tr_w1 = {}
        t_mm1 = {}
        ev1 = {}
        gelu2_sc = {}
        for span in range(NSPAN1):
            for j in range(2):
                slab = 2 * span + j
                stg = F3 if j == 0 else F1
                sem = "s1A" if j == 0 else "s1B"
                wv = []
                if slab >= 2:
                    wv.append(("g", g_tern1[slab - 2]))
                elif j == 0:
                    wv.append(("sc", cvt_seq[-1]))
                    wv.append(("v", v_w1s))
                else:
                    wv.append(("v", vq1_done[NT - 2]))
                ld = dma(sem, lambda s, slab=slab, stg=stg: s.dma_start(
                    out=stg[:], in_=w1_in[slab * P:(slab + 1) * P, :]),
                    waits=wv)
                wq_gate = [ld]
                if span == 0:
                    wq_gate += xt1_all
                if span >= 1:
                    wq_gate.append(tr_w1[2 * (span - 1) + j])
                pl.op("gpsimd", lambda g, stg=stg: g.tensor_scalar(
                    stg[:], stg[:], col(cSw1), col(cC), OP.mult, OP.add),
                    waits=wq_gate)
                gap("gpsimd")
                pl.op("gpsimd", lambda g, stg=stg: g.tensor_scalar(
                    stg[:], stg[:], col(cCp1), col(cCm1), OP.min, OP.max))
                gap("gpsimd")
                pl.op("gpsimd", lambda g, stg=stg, j=j: g.tensor_scalar(
                    W1Q[:, j, :], stg[:], col(cOne), col(cNegC),
                    OP.mult, OP.add),
                    incs=[pl.inc("g")])
                g_tern1[slab] = pl.val("g")
                wtr = [("g", g_tern1[slab])]
                if span >= 2:
                    wtr.append(("t", t_mm1[(span - 2, 1)]))
                if span == 0:
                    wtr.append(("v", vq1_done[NT - 1]))
                if span >= 1:
                    wtr.append(tr_w1[2 * (span - 1) + j])
                tr_w1[slab] = dma(
                    "w1A" if j == 0 else "w1B",
                    lambda s, span=span, j=j: s.dma_start_transpose(
                        W1T[span % 2][:, :, j * 128:(j + 1) * 128],
                        W1Q[:, j, :].rearrange("p (c f) -> p c f", c=16)),
                    waits=wtr, eng="scalar")
            for _ in range(4):
                if mean_state["w2"] < 128:
                    emit_mean_piece("w2", w2_in, F2, 8,
                                    ("v", vq1_done[NT - 1]))
            for grp in range(2):
                wts = [tr_w1[2 * span], tr_w1[2 * span + 1]]
                if (span - 1, grp) in ev1:
                    wts.append(("sc", ev1[(span - 1, grp)]))
                for ti in range(4):
                    t = grp * 4 + ti
                    w0 = list(wts) if ti == 0 else []
                    if span == 0:
                        w0.append(tr_xt1[t])
                    for c in range(16):
                        pl.op("tensor", lambda e, t=t, c=c, span=span:
                              e.matmul(PS[t][:, 0:MS1], XT1[:, t, c, :],
                                       W1T[span % 2][:, c, :],
                                       start=(c == 0), stop=(c == 15)),
                              waits=w0 if c == 0 else [],
                              incs=[pl.inc("t")] if (ti == 3 and c == 15)
                              else [])
                t_mm1[(span, grp)] = pl.val("t")
                for pair in range(2):
                    ts_ = [grp * 4 + 2 * pair, grp * 4 + 2 * pair + 1]
                    for t in ts_:
                        wev = [("t", t_mm1[(span, grp)])]
                        if span == 0:
                            wev += h2_gate[t]
                        pl.op("scalar", lambda s, t=t: s.activation(
                            OSB[:, t % 2, :], PS[t][:, 0:MS1], AF.Gelu,
                            scale=tc(t, 8)), waits=wev)
                    for t in ts_:
                        pl.op("scalar", lambda s, t=t, span=span:
                              s.activation(
                                  h2_tile(t)[:, span * MS1:(span + 1) * MS1],
                                  OSB[:, t % 2, :], AF.Gelu),
                              incs=[pl.inc("sc")])
                        gelu2_sc[(span, t)] = pl.val("sc")
                ev1[(span, grp)] = pl.val("sc")
            for t in range(NT):
                h2s = h2_tile(t)[:, span * MS1:(span + 1) * MS1]
                acc = tc(t, 10) if span == 0 else tc(t, 11)
                pl.op("vector", lambda v, h2s=h2s, acc=acc:
                      v.scalar_tensor_tensor(JNK[:], h2s, 1.0, h2s,
                                             OP.mult, OP.mult,
                                             accum_out=acc),
                      waits=[("sc", gelu2_sc[(span, t)])])
            for t in range(NT):
                h2s = h2_tile(t)[:, span * MS1:(span + 1) * MS1]
                dstm = tc(t, 9) if span == 0 else tc(t, 2)
                pl.op("vector", lambda v, h2s=h2s, dstm=dstm:
                      v.tensor_reduce(dstm, h2s, AX.X, OP.max,
                                      apply_absolute_value=True))
            if span > 0:
                for t in range(NT):
                    pl.op("vector", lambda v, t=t: v.tensor_tensor(
                        tc(t, 10), tc(t, 10), tc(t, 11), OP.add))
                for t in range(NT):
                    pl.op("vector", lambda v, t=t: v.tensor_tensor(
                        tc(t, 9), tc(t, 9), tc(t, 2), OP.max))
        pl.op("vector", lambda v: v.memset(col(cT2), 0.0),
              incs=[pl.inc("v")])
        v_h2stats = pl.val("v")

        tappoint("h2", [(h2_tile(t)[:, 0:2048],
                         [("sc", gelu2_sc[(NSPAN1 - 1, t)])])
                        for t in range(NT)])

        v_w2s = finalize_mean("w2", cM2, cSw2, cM2o127,
                              extra_mm_waits=[("sc", ev1[(NSPAN1 - 1, 0)])])

        q2_done = {}
        tr_xt2 = {}
        mm1_done = ("t", t_mm1[(NSPAN1 - 1, 1)])
        for t in range(NT):
            w0_ = [("v", v_h2stats), ("v", v_w2s)]
            vchain(lambda v, t=t: v.tensor_scalar(
                tc(t, 3), tc(t, 10), 1e-24, None, OP.max), extra_waits=w0_)
            vchain(lambda v, t=t: v.reciprocal(tc(t, 3), tc(t, 3)),
                   incs=[pl.inc("v")])
            vr = pl.val("v")
            pl.op("scalar", lambda s, t=t: s.activation(
                tc(t, 5), tc(t, 3), AF.Sqrt, scale=float(INNER)),
                waits=[("v", vr)], incs=[pl.inc("sc")])
            scq = pl.val("sc")
            vchain(lambda v, t=t: v.tensor_tensor(
                tc(t, 6), tc(t, 9), tc(t, 5), OP.mult),
                extra_waits=[("sc", scq)])
            vchain(lambda v, t=t: v.tensor_scalar(
                tc(t, 6), tc(t, 6), EPS_Q, None, OP.max))
            vchain(lambda v, t=t: v.reciprocal(tc(t, 7), tc(t, 6)))
            vchain(lambda v, t=t: v.tensor_tensor(
                tc(t, 8), tc(t, 6), col(cM2o127), OP.mult))
            vchain(lambda v, t=t: v.tensor_scalar(
                tc(t, 7), tc(t, 7), 127.0, None, OP.mult))
            vchain(lambda v, t=t: v.tensor_tensor(
                tc(t, 7), tc(t, 7), tc(t, 5), OP.mult),
                incs=[pl.inc("v")])
            v_s2t = pl.val("v")
            eng = "vector" if t % 2 == 0 else "gpsimd"
            esem = "v" if t % 2 == 0 else "g"
            fstage = F1 if t % 2 == 0 else F3
            wv0 = [("v", v_s2t)]
            if t % 2 == 0:
                wv0.append(("g", g_tern1[2 * NSPAN1 - 1]))
            else:
                wv0.append(("g", g_tern1[2 * NSPAN1 - 2]))
            for ch in range(4):
                h2c = h2_tile(t)[:, ch * D:(ch + 1) * D]
                pl.op(eng, lambda e, h2c=h2c, fstage=fstage, t=t:
                      e.tensor_scalar(fstage[:], h2c, tc(t, 7), col(cC),
                                      OP.mult, OP.add),
                      waits=wv0 if ch == 0 else [])
                gap(eng)
                pl.op(eng, lambda e, h2c=h2c, fstage=fstage:
                      e.tensor_scalar(h2c, fstage[:], col(cOne), col(cNegC),
                                      OP.mult, OP.add),
                      incs=[pl.inc(esem)] if ch == 3 else [])
            q2_done[t] = (esem, pl.val(esem))
            wv = [q2_done[t]]
            if t < 2:
                wv.append(mm1_done)
            else:
                wv.append(tr_xt2[t - 2])
            tr_xt2[t] = dma("t2A" if t % 2 == 0 else "t2B",
                            lambda s, t=t: s.dma_start_transpose(
                                xt2_tile(t), h2_tile(t).rearrange(
                                    "p (c f) -> p c f", c=64)), waits=wv,
                            eng="scalar")
        xt2_all = [tr_xt2[NT - 1], tr_xt2[NT - 2]]

        tern2_seq = {}
        tr_w2p = {}
        t_mm2 = {}
        copy_sc = {}
        tstg_free = {}
        ob_last = {}
        stt_v = {}
        nst = [0]
        store_led = {}
        xps_loaded = {}
        gelu_all = ("sc", gelu2_sc[(NSPAN1 - 1, NT - 1)])
        for slab in range(NSLAB2):
            wxp = [sp_all]
            if slab < 2:
                wxp.append(mm1_done)
            if slab >= 2:
                wxp.append(stt_v[slab - 2])
            xps_loaded[slab] = dma(
                "xpA" if slab % 2 == 0 else "xpB",
                lambda s, slab=slab: s.dma_start(
                    out=XPS[slab % 2][:],
                    in_=xp_hbm[slab].transpose([1, 0, 2])), waits=wxp)
            for p4 in range(4):
                piece = slab * 4 + p4
                stg = [F1, F3, F2][piece % 3]
                eng = "vector" if piece % 2 == 0 else "gpsimd"
                esem = "v" if eng == "vector" else "g"
                wv = []
                if piece >= 3:
                    wv.append(tern2_seq[piece - 3])
                elif piece < 2:
                    wv.append(q2_done[6 if piece % 2 == 0 else 7])
                else:
                    wv.append(("sc", mean_sc[("w2", 127)]))
                ld = dma(["s2A", "s2B", "s2C"][piece % 3],
                         lambda s, slab=slab, p4=p4, stg=stg: s.dma_start(
                             out=stg[:],
                             in_=w2_in[slab * P:(slab + 1) * P,
                                       p4 * D:(p4 + 1) * D]), waits=wv)
                wt0 = [ld]
                if piece < 2:
                    wt0 += xt2_all
                if piece >= 2:
                    wt0.append(tr_w2p[piece - 2])
                pl.op(eng, lambda e, stg=stg: e.tensor_scalar(
                    stg[:], stg[:], col(cSw2), col(cC), OP.mult, OP.add),
                    waits=wt0)
                gap(eng)
                pl.op(eng, lambda e, stg=stg: e.tensor_scalar(
                    stg[:], stg[:], col(cCp1), col(cCm1), OP.min, OP.max))
                gap(eng)
                pl.op(eng, lambda e, stg=stg, piece=piece: e.tensor_scalar(
                    PQ[piece % 2], stg[:], col(cOne), col(cNegC),
                    OP.mult, OP.add),
                    incs=[pl.inc(esem)])
                tern2_seq[piece] = (esem, pl.val(esem))
                wtr = [tern2_seq[piece]]
                if slab < 2 and p4 < 2:
                    wtr += xt2_all
                if slab >= 2:
                    wtr.append(("t", t_mm2[(slab - 2, 1)]))
                if slab >= 1:
                    wtr.append(tr_w2p[(slab - 1) * 4 + p4])
                tr_w2p[piece] = dma(
                    f"w2p{p4}",
                    lambda s, slab=slab, p4=p4, piece=piece:
                    s.dma_start_transpose(
                        W2T[slab % 2][:, p4 * 16:(p4 + 1) * 16, :],
                        PQ[piece % 2].rearrange("p (c f) -> p c f", c=16)),
                    waits=wtr, eng="scalar")
            for half in range(2):
                bk = 2 * (slab % 2) + half
                wts = [xt2_all[0], xt2_all[1]]
                if slab < 2:
                    wts.append(gelu_all)
                    wts.append(("v", v_w2s))
                if (slab - 2, half) in copy_sc:
                    wts.append(("sc", copy_sc[(slab - 2, half)]))
                for c in range(64):
                    w = []
                    if c == 0:
                        w = wts + [tr_w2p[slab * 4]]
                    elif c % 16 == 0:
                        w = [tr_w2p[slab * 4 + c // 16]]
                    pl.op("tensor", lambda e, c=c, slab=slab, half=half,
                          bk=bk: e.matmul(
                              PS[bk][:],
                              W2T[slab % 2][:, c, :],
                              XT2V[:, 4 * half:4 * half + 4, c, :],
                              start=(c == 0), stop=(c == 63)),
                          waits=w,
                          incs=[pl.inc("t")] if c == 63 else [])
                t_mm2[(slab, half)] = pl.val("t")
                s_idx = (2 * slab + half) % 4
                wcp = [("t", t_mm2[(slab, half)])]
                if (s_idx,) in ob_last:
                    wcp.append(ob_last[(s_idx,)])
                pl.op("scalar", lambda s, bk=bk, s_idx=s_idx: s.copy(
                    MSTG[:, s_idx].rearrange("p c f -> p (c f)"),
                    PS[bk][:]),
                    waits=wcp, incs=[pl.inc("sc")])
                copy_sc[(slab, half)] = pl.val("sc")
                wob = [("sc", copy_sc[(slab, half)])]
                if (s_idx,) in tstg_free:
                    wob.append(tstg_free[(s_idx,)])
                ob = dma(f"ob{s_idx}",
                         lambda s, s_idx=s_idx: s.dma_start_transpose(
                             TSTG[:, s_idx], MSTG[:, s_idx]),
                         waits=wob, eng="scalar")
                ob_last[(s_idx,)] = ob
                for ti in range(4):
                    t = 4 * half + ti
                    n = nst[0]
                    slot = n % 4
                    stsem = f"st{slot}"
                    oslot = OSB[:, slot // 2, (slot % 2) * 128:
                                (slot % 2) * 128 + 128]
                    wev = [ob, xps_loaded[slab]]
                    if (slot, "last") in store_led:
                        wev.append(store_led[(slot, "last")])
                    pl.op("vector", lambda v, t=t, slab=slab, oslot=oslot,
                          s_idx=s_idx, ti=ti: v.scalar_tensor_tensor(
                              oslot, TSTG[:, s_idx, ti, :], tc(t, 8),
                              XPS[slab % 2][:, t, :], OP.mult, OP.add),
                          waits=wev, incs=[pl.inc("v")])
                    vev = pl.val("v")
                    st = dma(stsem, lambda s, t=t, slab=slab, oslot=oslot:
                             s.dma_start(
                                 out=out_hbm[t * P:(t + 1) * P,
                                             slab * DK:(slab + 1) * DK],
                                 in_=oslot), waits=[("v", vev)],
                             eng="scalar")
                    store_led[(slot, "last")] = st
                    nst[0] += 1
                tstg_free[(s_idx,)] = ("v", pl.val("v"))
            stt_v[slab] = ("v", pl.val("v"))

        pl.op("sync", lambda s: s.nop(),
              waits=[(f"st{i}", pl.val(f"st{i}")) for i in range(4)])

        if hasattr(pl, "_tap_cut"):
            pl.ops = pl.ops[:pl._tap_cut]

        @block.sync
        def _(sync):
            pl.emit("sync", sync, handles)

        @block.vector
        def _(vector):
            pl.emit("vector", vector, handles)

        @block.scalar
        def _(scalar):
            pl.emit("scalar", scalar, handles)

        @block.gpsimd
        def _(gpsimd):
            pl.emit("gpsimd", gpsimd, handles)

        @block.tensor
        def _(tensor):
            pl.emit("tensor", tensor, handles)

    return nc


_NC_CACHE = []


def kernel(L0_attn_x, L0_attn_Wq, L0_attn_Wk, L0_attn_Wv, L0_attn_Wo,
           L0_ffn_w1, L0_ffn_w2):
    x = np.ascontiguousarray(np.asarray(L0_attn_x, dtype=np.float32)
                             .reshape(N_CORES * TOK, D))
    wq = np.ascontiguousarray(np.asarray(L0_attn_Wq, dtype=np.float32))
    wk = np.ascontiguousarray(np.asarray(L0_attn_Wk, dtype=np.float32))
    wv = np.ascontiguousarray(np.asarray(L0_attn_Wv, dtype=np.float32))
    wo = np.ascontiguousarray(np.asarray(L0_attn_Wo, dtype=np.float32))
    w1 = np.ascontiguousarray(np.asarray(L0_ffn_w1, dtype=np.float32))
    w2 = np.ascontiguousarray(np.asarray(L0_ffn_w2, dtype=np.float32))

    if not _NC_CACHE:
        _NC_CACHE.append(build_program())
    nc = _NC_CACHE[0]

    in_maps = []
    for c in range(N_CORES):
        in_maps.append({
            "x_in": x[c * TOK:(c + 1) * TOK],
            "wq_in": wq, "wk_in": wk, "wv_in": wv, "wo_in": wo,
            "w1_in": w1, "w2_in": w2,
        })
    res = run_bass_kernel_spmd(nc, in_maps, core_ids=list(range(N_CORES)))
    out = np.concatenate([np.asarray(r["out"]) for r in res.results], axis=0)
    return out.reshape(1, N_CORES * TOK, D).astype(np.float32)


# version 12
# speedup vs baseline: 1.3753x; 1.0899x over previous
import numpy as np
from contextlib import ExitStack

import concourse.bass as bass
import concourse.mybir as mybir
from concourse.bass_utils import run_bass_kernel_spmd

F32 = mybir.dt.float32
BF16 = mybir.dt.bfloat16
AF = mybir.ActivationFunctionType
OP = mybir.AluOpType
AX = mybir.AxisListType

N_CORES = 8
P = 128
D = 2048
INNER = 8192
TOK = 1024
NT = TOK // P
H = 16
DK = 128
EPS_LN = 1e-5
EPS_Q = 1e-5
EPS_W = 1e-5
C32 = float(1.5 * 2.0 ** 23)
C64 = float(1.5 * 2.0 ** 52)
MS1 = 256
NSPAN1 = INNER // MS1
NSLAB2 = D // P

SEMS = ["xA", "xB", "wA", "wB", "mA", "mB", "s1A", "s1B",
        "s2A", "s2B", "s2C",
        "xpA", "xpB", "txA", "txB", "toA", "toB", "t1A", "t1B", "t2A", "t2B",
        "w1A", "w1B", "w2p0", "w2p1", "w2p2", "w2p3", "sp",
        "qq0", "qq1", "kk0", "kk1", "sr0", "sr1", "sr2", "sr3",
        "ob0", "ob1", "ob2", "ob3",
        "st0", "st1", "st2", "st3", "v", "vv", "sc", "g", "t"]


class Plan:
    def __init__(self):
        self.ops = []
        self.sems = {}

    def sem(self, name):
        self.sems.setdefault(name, 0)

    def inc(self, name, n=1):
        self.sems[name] = self.sems.get(name, 0) + n
        return (name, n)

    def val(self, name):
        return self.sems.get(name, 0)

    def op(self, engine, fn, waits=(), incs=()):
        self.ops.append((engine, fn,
                         [w for w in waits if w is not None and w[1] > 0],
                         list(incs)))

    def emit(self, engine_name, eng, handles):
        for e, fn, waits, incs in self.ops:
            if e != engine_name:
                continue
            for (s, v) in waits:
                eng.wait_ge(handles[s], v)
            ins = fn(eng)
            for (s, amt) in incs:
                ins.then_inc(handles[s], amt)


def build_program(tap=None):
    nc = bass.Bass("TRN2", target_bir_lowering=False,
                   detect_race_conditions=False,
                   dynamic_dma_scratch_size=2048)

    x_in = nc.dram_tensor("x_in", [TOK, D], F32, kind="ExternalInput").ap()
    w_attn = {
        "q": nc.dram_tensor("wq_in", [D, D], F32, kind="ExternalInput").ap(),
        "k": nc.dram_tensor("wk_in", [D, D], F32, kind="ExternalInput").ap(),
        "v": nc.dram_tensor("wv_in", [D, D], F32, kind="ExternalInput").ap(),
        "o": nc.dram_tensor("wo_in", [D, D], F32, kind="ExternalInput").ap(),
    }
    w1_in = nc.dram_tensor("w1_in", [INNER, D], F32, kind="ExternalInput").ap()
    w2_in = nc.dram_tensor("w2_in", [D, INNER], F32, kind="ExternalInput").ap()
    out_hbm = nc.dram_tensor("out", [TOK, D], F32, kind="ExternalOutput").ap()
    xp_hbm = nc.dram_tensor("xp_spill", [NSLAB2, NT, P, DK], BF16,
                            kind="Internal").ap()

    pl = Plan()
    for s in SEMS:
        pl.sem(s)

    with ExitStack() as ctx:
        def sbt(name, shape, dt=BF16):
            return ctx.enter_context(nc.sbuf_tensor(name, shape, dt))

        AR14 = sbt("AR14", [P, 4, NT, D])
        AR1 = AR14[:, 0]
        AR2 = AR14[:, 1]
        AR3 = AR14[:, 2]
        AR4 = AR14[:, 3]
        AR8 = sbt("AR8", [P, 28672])
        F1 = sbt("F1", [P, D], F32)
        F2 = sbt("F2", [P, D], F32)
        F3 = sbt("F3", [P, D], F32)
        F5 = sbt("F5", [P, NT, H, H])
        OSB = sbt("OSB", [P, 2, 256], F32)
        COLS = sbt("COLS", [P, 16], F32)
        TC = sbt("TC", [P, NT, 12], F32)
        MEANP = sbt("MEANP", [P, 128], F32)
        JNK = sbt("JNK", [P, 256], BF16)
        ONESF = sbt("ONESF", [P, 128], F32)

        PS = [ctx.enter_context(nc.psum_tensor(f"ps{i}", [P, 512], F32))
              for i in range(8)]

        handles = {s: ctx.enter_context(nc.semaphore(s)) for s in pl.sems}
        block = ctx.enter_context(nc.Block())

        cC, cProbe, cFlag, cTmp = 0, 1, 2, 3
        cM1, cSw1, cM1o127 = 4, 5, 6
        cM2, cSw2, cM2o127 = 7, 8, 9
        cT2, cCp1, cCm1, cZero = 10, 11, 12, 13
        cOne, cNegC = 14, 15

        def col(i):
            return COLS[:, i:i + 1]

        def tc(t, i):
            return TC[:, t, i:i + 1]

        XNT = AR1[:].rearrange("p t (c f) -> p t c f", c=16)
        XT1 = XNT
        q_bf = AR2
        k_bf = AR3
        o_bf = AR3

        def wqtr(i):
            return AR4[:, 4 * (i % 2):4 * (i % 2) + 4, :].rearrange(
                "p t f -> p (t f)").rearrange("p (c g) -> p c g", c=16)

        def ar8(lo, hi):
            return AR8[:, lo:hi]

        xn_buf = [ar8(0, 2048), ar8(2048, 4096)]
        tmp_buf = [ar8(4096, 6144), ar8(6144, 8192)]
        junk1k = ar8(8192, 9216)
        junkP0 = ar8(9216, 11264)
        gtmp = [ar8(0, 2048), ar8(2048, 4096)]
        VBF = ar8(12288, 28672).rearrange("p (t f) -> p t f", t=NT)

        def xp_t(t):
            return ar8(t * 2048, (t + 1) * 2048)

        XPV = AR8[:, 0:16384].rearrange("p (t f) -> p t f", t=NT)
        xint_buf = [ar8(16384, 18432), ar8(18432, 20480)]
        junkP5 = ar8(20480, 22528)
        W1Q = ar8(16384, 20480).rearrange("p (s f) -> p s f", s=2)
        W1T = [ar8(20480, 24576).rearrange("p (c f) -> p c f", c=16),
               ar8(24576, 28672).rearrange("p (c f) -> p c f", c=16)]

        def h2_tile(t):
            if t < 2:
                return AR2[:, 4 * t:4 * (t + 1), :].rearrange(
                    "p a b -> p (a b)")
            if t < 4:
                return AR3[:, 4 * (t - 2):4 * (t - 1), :].rearrange(
                    "p a b -> p (a b)")
            if t < 6:
                return AR4[:, 4 * (t - 4):4 * (t - 3), :].rearrange(
                    "p a b -> p (a b)")
            return ar8(8192 * (t - 6), 8192 * (t - 5))

        def xt2_tile(t):
            if t < 2:
                base = AR1[:, 4 * t:4 * (t + 1), :].rearrange(
                    "p a b -> p (a b)")
            else:
                base = h2_tile(t - 2)
            return base.rearrange("p (c f) -> p c f", c=64)

        W2T = [ar8(0, 8192).rearrange("p (c f) -> p c f", c=64),
               ar8(8192, 16384).rearrange("p (c f) -> p c f", c=64)]
        PQ = [ar8(16384, 18432), ar8(18432, 20480)]
        XPS = [ar8(20480, 21504).rearrange("p (t f) -> p t f", t=NT),
               ar8(21504, 22528).rearrange("p (t f) -> p t f", t=NT)]
        XT2V = AR14[:].rearrange("p a t d -> p (a t d)").rearrange(
            "p (T c f) -> p T c f", T=NT, c=64)
        MSTG = ar8(22528, 24576).rearrange("p (s c f) -> p s c f", s=4, c=4)
        TSTG = ar8(24576, 26624).rearrange("p (s c f) -> p s c f", s=4, c=4)

        def gap(eng="vector"):
            pl.op(eng, lambda e: e.memset(col(cT2), 0.0))

        def vchain(fn, extra_waits=(), incs=()):
            pl.op("vector", fn,
                  waits=[("vv", pl.val("vv"))] + list(extra_waits),
                  incs=[pl.inc("vv")])
            if incs:
                pl.op("vector", lambda v: v.memset(col(cT2), 0.0),
                      waits=[("vv", pl.val("vv"))], incs=list(incs))

        def dma(sem, fn, waits=(), eng="sync"):
            pl.op(eng, fn, waits=waits, incs=[pl.inc(sem, 16)])
            return (sem, pl.val(sem))

        def tappoint(name, rows_src):
            if tap != name:
                return
            for t, (ap, w) in enumerate(rows_src):
                W = min(ap.free_size(), 2048)
                pl.op("vector", lambda v, ap=ap, W=W: v.tensor_copy(
                    F3[:, 0:W], ap),
                    waits=list(w) + [("st0", pl.val("st0"))],
                    incs=[pl.inc("v")])
                vv_ = pl.val("v")
                dma("st0", lambda s, t=t, W=W: s.dma_start(
                    out=out_hbm[t * P:(t + 1) * P, 0:W], in_=F3[:, 0:W]),
                    waits=[("v", vv_)])
            pl.op("sync", lambda s: s.nop(),
                  waits=[("st0", pl.val("st0"))])
            pl._tap_cut = len(pl.ops)

        vchain(lambda v: v.memset(ONESF[:], 1.0))
        vchain(lambda v: v.memset(col(cProbe), 0.25))
        vchain(lambda v: v.tensor_scalar(
            col(cProbe), col(cProbe), C32, C32, OP.add, OP.subtract))
        vchain(lambda v: v.tensor_scalar(
            col(cFlag), col(cProbe), 0.1, None, OP.is_gt))
        vchain(lambda v: v.tensor_scalar(
            col(cC), col(cFlag), C64, None, OP.mult))
        vchain(lambda v: v.tensor_scalar(
            col(cTmp), col(cFlag), -C32, C32, OP.mult, OP.add))
        vchain(lambda v: v.tensor_tensor(
            col(cC), col(cC), col(cTmp), OP.add))
        vchain(lambda v: v.tensor_scalar(
            col(cCp1), col(cC), 1.0, None, OP.add))
        vchain(lambda v: v.tensor_scalar(
            col(cCm1), col(cC), -1.0, None, OP.add))
        vchain(lambda v: v.memset(col(cZero), 0.0))
        vchain(lambda v: v.memset(col(cOne), 1.0))
        vchain(lambda v: v.tensor_scalar(
            col(cNegC), col(cC), -1.0, None, OP.mult))
        vchain(lambda v: v.memset(JNK[:], 0.0), incs=[pl.inc("v")])
        vC = pl.val("v")

        def xstage(t):
            return F1 if t % 2 == 0 else F2

        def xsem(t):
            return "xA" if t % 2 == 0 else "xB"

        def txsem(t):
            return "txA" if t % 2 == 0 else "txB"

        v_xn = {}
        tr_xnt = {}
        x_ld = {}
        for t in range(NT):
            w = [("v", v_xn[t - 2])] if (t - 2) in v_xn else []
            x_ld[t] = dma(xsem(t), lambda s, t=t: s.dma_start(
                out=xstage(t)[:], in_=x_in[t * P:(t + 1) * P, :]), waits=w,
                eng="gpsimd")
            vchain(lambda v, t=t: v.tensor_reduce(
                tc(t, 0), xstage(t)[:], AX.X, OP.add),
                extra_waits=[x_ld[t], ("v", vC)])
            vchain(lambda v, t=t: v.scalar_tensor_tensor(
                junkP0, xstage(t)[:], 1.0, xstage(t)[:], OP.mult, OP.mult,
                accum_out=tc(t, 3)))
            vchain(lambda v, t=t: v.tensor_scalar(
                tc(t, 0), tc(t, 0), 1.0 / D, None, OP.mult))
            vchain(lambda v, t=t: v.tensor_scalar(
                tc(t, 3), tc(t, 3), 1.0 / D, None, OP.mult))
            vchain(lambda v, t=t: v.tensor_tensor(
                tc(t, 2), tc(t, 0), tc(t, 0), OP.mult))
            vchain(lambda v, t=t: v.tensor_tensor(
                tc(t, 3), tc(t, 3), tc(t, 2), OP.subtract))
            vchain(lambda v, t=t: v.tensor_scalar(
                tc(t, 3), tc(t, 3), EPS_LN, None, OP.add))
            vchain(lambda v, t=t: v.reciprocal(tc(t, 3), tc(t, 3)),
                   incs=[pl.inc("v")])
            vr = pl.val("v")
            pl.op("scalar", lambda s, t=t: s.activation(
                tc(t, 1), tc(t, 3), AF.Sqrt), waits=[("v", vr)],
                incs=[pl.inc("sc")])
            scr = pl.val("sc")
            vchain(lambda v, t=t: v.tensor_tensor(
                tc(t, 2), tc(t, 0), tc(t, 1), OP.mult),
                extra_waits=[("sc", scr)])
            wtr = [tr_xnt[t - 2]] if (t - 2) in tr_xnt else []
            vchain(lambda v, t=t: v.tensor_scalar(
                xn_buf[t % 2], xstage(t)[:], tc(t, 1), tc(t, 2),
                OP.mult, OP.subtract), extra_waits=wtr, incs=[pl.inc("v")])
            v_xn[t] = pl.val("v")
            tr_xnt[t] = dma(txsem(t), lambda s, t=t: s.dma_start_transpose(
                XNT[:, t], xn_buf[t % 2].rearrange("p (c f) -> p c f", c=16)),
                waits=[("v", v_xn[t])], eng="scalar")

        tappoint("xnt", [(XNT[:, t].rearrange("p c f -> p (c f)"),
                          [tr_xnt[t]]) for t in range(NT)])

        mean_state = {"w1": 0, "w2": 0}
        mean_sc = {}

        junk_f5 = F5[:].rearrange("p t h k -> p (t h k)")[:, 0:1024]

        def emit_mean_piece(tag, whbm, stage_buf, halves_per_row, first_wait):
            i = mean_state[tag]
            mean_state[tag] += 1
            row0 = i // halves_per_row
            cof = (i % halves_per_row) * 1024
            half = stage_buf[:, (i % 2) * 1024:(i % 2) * 1024 + 1024]
            junk = junk1k if tag == "w1" else junk_f5
            sem = "mA" if i % 2 == 0 else "mB"
            wv = []
            if (tag, i - 2) in mean_sc:
                wv.append(("sc", mean_sc[(tag, i - 2)]))
            elif i < 2 and first_wait is not None:
                wv.append(first_wait)
            ld = dma(sem, lambda s, whbm=whbm, row0=row0, cof=cof, half=half:
                     s.dma_start(out=half,
                                 in_=whbm[row0 * P:(row0 + 1) * P,
                                          cof:cof + 1024]), waits=wv)
            pl.op("scalar", lambda s, half=half, i=i, junk=junk: s.activation(
                junk, half, AF.Abs, accum_out=MEANP[:, i:i + 1]),
                waits=[ld], incs=[pl.inc("sc")])
            mean_sc[(tag, i)] = pl.val("sc")

        def finalize_mean(tag, cM, cSw, cMo127, extra_mm_waits=()):
            vchain(lambda v, cM=cM: v.tensor_reduce(
                col(cM), MEANP[:, 0:128], AX.X, OP.add),
                extra_waits=[("sc", mean_sc[(tag, 127)])],
                incs=[pl.inc("v")])
            vv_ = pl.val("v")
            pl.op("tensor", lambda e, cM=cM: e.matmul(
                PS[0][:, 0:1], ONESF[:], col(cM), start=True, stop=True),
                waits=[("v", vv_)] + list(extra_mm_waits),
                incs=[pl.inc("t")])
            tv = pl.val("t")
            vchain(lambda v, cM=cM: v.tensor_copy(
                col(cM), PS[0][:, 0:1]), extra_waits=[("t", tv)])
            vchain(lambda v, cM=cM: v.tensor_scalar(
                col(cM), col(cM), 1.0 / (INNER * D), None, OP.mult))
            vchain(lambda v, cM=cM: v.tensor_scalar(
                col(cM), col(cM), EPS_W, None, OP.max))
            vchain(lambda v, cM=cM, cSw=cSw: v.reciprocal(
                col(cSw), col(cM)))
            vchain(lambda v, cM=cM, cMo127=cMo127: v.tensor_scalar(
                col(cMo127), col(cM), 1.0 / 127.0, None, OP.mult))
            vchain(lambda v: v.memset(JNK[:], 0.0), incs=[pl.inc("v")])
            return pl.val("v")

        cvt_seq = []
        qtr_mm_done = {}
        sc_evac = {}
        t_mm = {}
        bank_free = {}
        g_scores = {}

        def attn_weight_phase(wname, stat_tr, dst, wi,
                              extra_evac_wait=None, bg=None):
            whbm = w_attn[wname]
            for qtr in range(4):
                gq = wi * 4 + qtr
                buf = wqtr(gq % 2)
                for i8 in range(8):
                    slot = i8 % 2
                    fsl = F3[:, slot * 1024:(slot + 1) * 1024]
                    sem = "wA" if slot == 0 else "wB"
                    wv = []
                    if len(cvt_seq) >= 2:
                        wv.append(("sc", cvt_seq[-2]))
                    if i8 == 0 and (gq - 2) in qtr_mm_done:
                        wv.append(("t", qtr_mm_done[gq - 2]))
                    src = whbm[i8 * 256:(i8 + 1) * 256,
                               qtr * 512:(qtr + 1) * 512]
                    ld = dma(sem, lambda s, src=src, fsl=fsl: s.dma_start(
                        out=fsl.rearrange("p (a g) -> p a g", a=2),
                        in_=src.rearrange("(a p) g -> p a g", p=128)),
                        waits=wv)
                    pl.op("scalar", lambda s, buf=buf, i8=i8, fsl=fsl:
                          s.copy(buf[:, 2 * i8:2 * i8 + 2, :].rearrange(
                              "p a g -> p (a g)"), fsl),
                          waits=[ld], incs=[pl.inc("sc")])
                    cvt_seq.append(pl.val("sc"))
                cvt_all = pl.val("sc")
                if bg is not None:
                    bg()
                for t in range(NT):
                    wts = [("sc", cvt_all), stat_tr[t]]
                    if t in bank_free:
                        wts.append(bank_free[t])
                    for c in range(16):
                        pl.op("tensor", lambda e, t=t, c=c, buf=buf:
                              e.matmul(PS[t][:], XNT[:, t, c, :],
                                       buf[:, c, :],
                                       start=(c == 0), stop=(c == 15)),
                              waits=wts if c == 0 else [],
                              incs=[pl.inc("t")] if c == 15 else [])
                    t_mm[(wname, qtr, t)] = pl.val("t")
                    wev = [("t", t_mm[(wname, qtr, t)])]
                    if extra_evac_wait is not None:
                        wev.extend(extra_evac_wait(t))
                    pl.op("scalar", lambda s, t=t, qtr=qtr, dst=dst:
                          s.copy(dst[:, t, qtr * 512:(qtr + 1) * 512],
                                 PS[t][:]),
                          waits=wev, incs=[pl.inc("sc")])
                    bank_free[t] = ("sc", pl.val("sc"))
                    if qtr == 3:
                        sc_evac[(wname, t)] = pl.val("sc")
                qtr_mm_done[gq] = pl.val("t")

        def mean_w1_bg():
            for _ in range(12):
                if mean_state["w1"] < 128:
                    emit_mean_piece("w1", w1_in, F2, 2, ("v", v_xn[NT - 1]))

        bg_count = [0]

        def bg_hook():
            bg_count[0] += 1
            if bg_count[0] >= 2:
                mean_w1_bg()

        attn_weight_phase("q", tr_xnt, q_bf, 0, bg=bg_hook)
        tappoint("q", [(q_bf[:, t, :], [("sc", sc_evac[("q", t)])])
                       for t in range(NT)])

        attn_weight_phase("k", tr_xnt, k_bf, 1, bg=bg_hook)
        tappoint("k", [(k_bf[:, t, :], [("sc", sc_evac[("k", t)])])
                       for t in range(NT)])

        attn_weight_phase("v", tr_xnt, VBF, 2, bg=bg_hook)

        v_odone = {}
        tr_ot = {}
        t_vdone = t_mm[("v", 3, NT - 1)]
        QKZ = [(ar8(0, 2048), ar8(2048, 4096)),
               (ar8(4096, 6144), ar8(6144, 8192))]

        def qkview(t, which):
            return QKZ[t % 2][which].rearrange("p (h f) -> p h f", h=16)

        qk_tr = {}
        for t in range(NT):
            qk_tr[(t, 0)] = ("qq" + str(t % 2), 16 * (t // 2 + 1))
            qk_tr[(t, 1)] = ("kk" + str(t % 2), 16 * (t // 2 + 1))

        gmm = {}
        smm = {}
        scp = {}
        f5cp = {}
        otmp = ar8(9216, 11264).rearrange("p (h f) -> p h f", h=16)
        F1V = F1[0:16, :].rearrange("p (k tok) -> p k tok", k=16)
        ID16 = OSB[0:16, 0, 0:16]
        psv = {}
        for t in range(NT):
            psv[t] = PS[t][0:16, :].rearrange("p (k g) -> p k g", k=16)

        pl.op("gpsimd", lambda g: g.memset(ID16, 0.0))
        pl.op("gpsimd", lambda g: g.affine_select(
            out=ID16, in_=ID16, compare_op=OP.not_equal, fill=1.0,
            base=0, pattern=[[-1, 16]], channel_multiplier=1),
            incs=[pl.inc("g")])
        id_done = ("g", pl.val("g"))

        def emit_o(t):
            o3 = o_bf[:, t, :].rearrange("p (h f) -> p h f", h=16)

            def omul(kh, dst, t=t):
                return lambda v, kh=kh, dst=dst, t=t: v.tensor_tensor(
                    dst,
                    F5[:, t, kh, :].unsqueeze(2).broadcast_to([P, H, DK]),
                    VBF[:, t, kh * DK:(kh + 1) * DK].unsqueeze(1)
                    .broadcast_to([P, H, DK]),
                    OP.mult)

            def oadd(t=t):
                return lambda v, t=t: v.tensor_tensor(
                    o_bf[:, t, :].rearrange("p (h f) -> p h f", h=16),
                    o_bf[:, t, :].rearrange("p (h f) -> p h f", h=16),
                    otmp, OP.add)

            w0 = [("sc", sc_evac[("v", t)]), ("v", f5cp[t])]
            pl.op("vector", omul(0, o3), waits=w0)
            for kh in range(1, H):
                pl.op("vector", omul(kh, otmp))
                pl.op("vector", oadd(),
                      incs=[pl.inc("v")] if kh == H - 1 else [])
            v_odone[t] = ("v", pl.val("v"))

        for t in range(NT):
            for g in range(4):
                for tau in range(32):
                    w = []
                    if tau == 0:
                        if g == 0:
                            w = [qk_tr[(t, 0)], qk_tr[(t, 1)], bank_free[t]]
                        else:
                            w = [("v", scp[(t, g - 1)])]
                    pl.op("tensor", lambda e, t=t, g=g, tau=tau: e.matmul(
                        psv[t][:, :, tau],
                        qkview(t, 0)[:, :, g * 32 + tau],
                        qkview(t, 1)[:, :, g * 32 + tau],
                        start=True, stop=True),
                        waits=w,
                        incs=[pl.inc("t")] if tau == 31 else [])
                gmm[(t, g)] = pl.val("t")
                w = [("t", gmm[(t, g)])]
                pl.op("vector", lambda v, t=t, g=g: v.tensor_copy(
                    F1V[:, :, g * 32:(g + 1) * 32],
                    psv[t]),
                    waits=w, incs=[pl.inc("v")])
                scp[(t, g)] = pl.val("v")
            for kh in range(H):
                w = []
                if kh == 0:
                    w = [("v", scp[(t, 3)])]
                    if t == 0:
                        w.append(id_done)
                pl.op("tensor", lambda e, t=t, kh=kh: e.transpose(
                    PS[t][:, kh * 16:(kh + 1) * 16],
                    F1[0:16, kh * 128:(kh + 1) * 128],
                    ID16),
                    waits=w,
                    incs=[pl.inc("t")] if kh == H - 1 else [])
            smm[t] = ("t", pl.val("t"))
            pl.op("vector", lambda v, t=t: v.tensor_copy(
                F5[:, t, :, :], PS[t][:, 0:256].rearrange(
                    "p (a b) -> p a b", a=16)),
                waits=[smm[t]], incs=[pl.inc("v")])
            f5cp[t] = pl.val("v")
            if t >= 1:
                emit_o(t - 1)
        emit_o(NT - 1)
        for t in range(NT):
            bank_free[t] = ("v", f5cp[t])

        def emit_trot():
            if tr_ot:
                return

            def qkdma(t):
                for which, src in ((0, q_bf), (1, k_bf)):
                    w = [("sc", sc_evac[("q" if which == 0 else "k", t)])]
                    if t >= 2:
                        w.append(smm[t - 2])
                    got = dma(qk_tr[(t, 0)][0] if which == 0
                              else qk_tr[(t, 1)][0],
                              lambda s, t=t, which=which, src=src:
                              s.dma_start_transpose(
                                  qkview(t, which),
                                  src[:, t, :].rearrange(
                                      "p (c f) -> p c f", c=16)),
                              waits=w)
                    assert got == qk_tr[(t, which)], (got, t, which)

            qkdma(0)
            qkdma(1)
            for t in range(NT):
                w = [v_odone[t], ("t", t_vdone)]
                if t >= 2:
                    w.append(tr_ot[t - 2])
                tr_ot[t] = dma("toA" if t % 2 == 0 else "toB",
                               lambda s, t=t: s.dma_start_transpose(
                                   XNT[:, t], o_bf[:, t, :].rearrange(
                                       "p (c f) -> p c f", c=16)),
                               waits=w)
                if t + 2 < NT:
                    qkdma(t + 2)

        tappoint("o", [(o_bf[:, t, :], [v_odone[t]]) for t in range(NT)])

        def wo_gate(t):
            w = []
            if t < 4:
                w.append(smm[NT - 1])
            if t in (4, 5):
                w.append(v_odone[NT - 1])
            if t not in (2, 3):
                w.append(("sc", mean_sc[("w1", 127)]))
            return w
        attn_weight_phase("o", tr_ot, XPV, 3, extra_evac_wait=wo_gate,
                          bg=emit_trot)

        v_w1s = finalize_mean("w1", cM1, cSw1, cM1o127,
                              extra_mm_waits=[("sc", sc_evac[("o", 0)])])

        vq1_done = {}
        tr_xt1 = {}
        for t in range(NT):
            xps = xp_t(t)
            fstage = F1 if t % 2 == 0 else F2
            wld = []
            if (t - 2) in vq1_done:
                wld.append(("v", vq1_done[t - 2]))
            elif t == 0:
                wld.append(("v", v_xn[NT - 2]))
            else:
                wld.append(("sc", mean_sc[("w1", 127)]))
            ld = dma(xsem(t), lambda s, t=t: s.dma_start(
                out=xstage(t)[:], in_=x_in[t * P:(t + 1) * P, :]), waits=wld)
            pl.op("vector", lambda v, xps=xps, fstage=fstage:
                  v.tensor_tensor(xps, xps, fstage[:], OP.add),
                  waits=[ld, ("sc", sc_evac[("o", t)])],
                  incs=[pl.inc("v")])
            vadd = pl.val("v")
            dma("sp", lambda s, t=t, xps=xps: s.dma_start(
                out=xp_hbm.transpose([2, 0, 1, 3])[:, :, t, :],
                in_=xps.rearrange("p (c f) -> p c f", c=16)),
                waits=[("v", vadd), ("sp", pl.val("sp"))])
            vchain(lambda v, xps=xps, t=t: v.tensor_reduce(
                tc(t, 0), xps, AX.X, OP.add), extra_waits=[("v", vadd)])
            vchain(lambda v, xps=xps, t=t: v.scalar_tensor_tensor(
                junkP5, xps, 1.0, xps, OP.mult, OP.mult, accum_out=tc(t, 3)))
            vchain(lambda v, t=t: v.tensor_scalar(
                tc(t, 0), tc(t, 0), 1.0 / D, None, OP.mult))
            vchain(lambda v, t=t: v.tensor_scalar(
                tc(t, 3), tc(t, 3), 1.0 / D, None, OP.mult))
            vchain(lambda v, t=t: v.tensor_tensor(
                tc(t, 2), tc(t, 0), tc(t, 0), OP.mult))
            vchain(lambda v, t=t: v.tensor_tensor(
                tc(t, 3), tc(t, 3), tc(t, 2), OP.subtract))
            vchain(lambda v, t=t: v.tensor_scalar(
                tc(t, 3), tc(t, 3), EPS_LN, None, OP.add))
            vchain(lambda v, t=t: v.reciprocal(tc(t, 3), tc(t, 3)),
                   incs=[pl.inc("v")])
            vr = pl.val("v")
            pl.op("scalar", lambda s, t=t: s.activation(
                tc(t, 1), tc(t, 3), AF.Sqrt), waits=[("v", vr)],
                incs=[pl.inc("sc")])
            scr = pl.val("sc")
            vchain(lambda v, t=t: v.tensor_tensor(
                tc(t, 2), tc(t, 0), tc(t, 1), OP.mult),
                extra_waits=[("sc", scr)])
            vchain(lambda v, xps=xps, fstage=fstage, t=t:
                   v.tensor_scalar(fstage[:], xps, tc(t, 1), tc(t, 2),
                                   OP.mult, OP.subtract))
            vchain(lambda v, fstage=fstage, t=t:
                   v.scalar_tensor_tensor(junkP5, fstage[:], 1.0, fstage[:],
                                          OP.mult, OP.mult,
                                          accum_out=tc(t, 3)))
            vchain(lambda v, fstage=fstage, t=t: v.tensor_reduce(
                tc(t, 4), fstage[:], AX.X, OP.max,
                apply_absolute_value=True))
            vchain(lambda v, t=t: v.tensor_scalar(
                tc(t, 3), tc(t, 3), 1e-24, None, OP.max))
            vchain(lambda v, t=t: v.reciprocal(tc(t, 3), tc(t, 3)),
                   incs=[pl.inc("v")])
            vr2 = pl.val("v")
            pl.op("scalar", lambda s, t=t: s.activation(
                tc(t, 5), tc(t, 3), AF.Sqrt, scale=float(D)),
                waits=[("v", vr2)], incs=[pl.inc("sc")])
            scf = pl.val("sc")
            vchain(lambda v, t=t: v.tensor_tensor(
                tc(t, 6), tc(t, 4), tc(t, 5), OP.mult),
                extra_waits=[("sc", scf), ("v", v_w1s)])
            vchain(lambda v, t=t: v.tensor_scalar(
                tc(t, 6), tc(t, 6), EPS_Q, None, OP.max))
            vchain(lambda v, t=t: v.reciprocal(tc(t, 7), tc(t, 6)))
            vchain(lambda v, t=t: v.tensor_tensor(
                tc(t, 8), tc(t, 6), col(cM1o127), OP.mult))
            vchain(lambda v, t=t: v.tensor_scalar(
                tc(t, 7), tc(t, 7), 127.0, None, OP.mult))
            vchain(lambda v, t=t: v.tensor_tensor(
                tc(t, 7), tc(t, 7), tc(t, 5), OP.mult))
            vchain(lambda v, fstage=fstage, t=t: v.tensor_scalar(
                fstage[:], fstage[:], tc(t, 7), col(cC), OP.mult, OP.add))
            wtr = [tr_xt1[t - 2]] if (t - 2) in tr_xt1 else []
            vchain(lambda v, fstage=fstage, t=t: v.tensor_scalar(
                xint_buf[t % 2], fstage[:], col(cC), None, OP.subtract),
                extra_waits=wtr, incs=[pl.inc("v")])
            vq1_done[t] = pl.val("v")
            tr_xt1[t] = dma("t1A" if t % 2 == 0 else "t1B",
                            lambda s, t=t: s.dma_start_transpose(
                                XT1[:, t], xint_buf[t % 2].rearrange(
                                    "p (c f) -> p c f", c=16)),
                            waits=[("v", vq1_done[t]),
                                   ("t", t_mm[("o", 3, t)])], eng="scalar")
        sp_all = ("sp", pl.val("sp"))

        tappoint("xp", [(xp_t(t), [("v", vq1_done[t])]) for t in range(NT)])
        tappoint("xt1", [(XT1[:, t].rearrange("p c f -> p (c f)"),
                          [tr_xt1[t]]) for t in range(NT)])
        tappoint("tc", [(TC[:].rearrange("p a b -> p (a b)"),
                         [("v", vq1_done[NT - 1])])])

        h2_gate = {}
        for t in range(NT):
            if t < 2:
                h2_gate[t] = [qk_tr[(NT - 1, 0)]]
            elif t < 4:
                h2_gate[t] = [tr_ot[NT - 1], tr_ot[NT - 2]]
            elif t < 6:
                h2_gate[t] = [("t", qtr_mm_done[15])]
            else:
                h2_gate[t] = [sp_all, ("v", vq1_done[NT - 1])]

        xt1_all = [tr_xt1[NT - 1], tr_xt1[NT - 2]]
        g_tern1 = {}
        tr_w1 = {}
        t_mm1 = {}
        ev1 = {}
        gelu2_sc = {}
        for span in range(NSPAN1):
            for j in range(2):
                slab = 2 * span + j
                stg = F3 if j == 0 else F1
                sem = "s1A" if j == 0 else "s1B"
                wv = []
                if slab >= 2:
                    wv.append(("g", g_tern1[slab - 2]))
                elif j == 0:
                    wv.append(("sc", cvt_seq[-1]))
                    wv.append(("v", v_w1s))
                else:
                    wv.append(("v", vq1_done[NT - 2]))
                ld = dma(sem, lambda s, slab=slab, stg=stg: s.dma_start(
                    out=stg[:], in_=w1_in[slab * P:(slab + 1) * P, :]),
                    waits=wv)
                wq_gate = [ld]
                if span == 0:
                    wq_gate += xt1_all
                if span >= 1:
                    wq_gate.append(tr_w1[2 * (span - 1) + j])
                pl.op("gpsimd", lambda g, stg=stg: g.tensor_scalar(
                    stg[:], stg[:], col(cSw1), col(cC), OP.mult, OP.add),
                    waits=wq_gate)
                gap("gpsimd")
                pl.op("gpsimd", lambda g, stg=stg: g.tensor_scalar(
                    stg[:], stg[:], col(cCp1), col(cCm1), OP.min, OP.max))
                gap("gpsimd")
                pl.op("gpsimd", lambda g, stg=stg, j=j: g.tensor_scalar(
                    W1Q[:, j, :], stg[:], col(cOne), col(cNegC),
                    OP.mult, OP.add),
                    incs=[pl.inc("g")])
                g_tern1[slab] = pl.val("g")
                wtr = [("g", g_tern1[slab])]
                if span >= 2:
                    wtr.append(("t", t_mm1[(span - 2, 1)]))
                if span == 0:
                    wtr.append(("v", vq1_done[NT - 1]))
                if span >= 1:
                    wtr.append(tr_w1[2 * (span - 1) + j])
                tr_w1[slab] = dma(
                    "w1A" if j == 0 else "w1B",
                    lambda s, span=span, j=j: s.dma_start_transpose(
                        W1T[span % 2][:, :, j * 128:(j + 1) * 128],
                        W1Q[:, j, :].rearrange("p (c f) -> p c f", c=16)),
                    waits=wtr, eng="scalar")
            for _ in range(4):
                if mean_state["w2"] < 128:
                    emit_mean_piece("w2", w2_in, F2, 8,
                                    ("v", vq1_done[NT - 1]))
            for grp in range(2):
                wts = [tr_w1[2 * span], tr_w1[2 * span + 1]]
                if (span - 1, grp) in ev1:
                    wts.append(("sc", ev1[(span - 1, grp)]))
                for ti in range(4):
                    t = grp * 4 + ti
                    w0 = list(wts) if ti == 0 else []
                    if span == 0:
                        w0.append(tr_xt1[t])
                    for c in range(16):
                        pl.op("tensor", lambda e, t=t, c=c, span=span:
                              e.matmul(PS[t][:, 0:MS1], XT1[:, t, c, :],
                                       W1T[span % 2][:, c, :],
                                       start=(c == 0), stop=(c == 15)),
                              waits=w0 if c == 0 else [],
                              incs=[pl.inc("t")] if (ti == 3 and c == 15)
                              else [])
                t_mm1[(span, grp)] = pl.val("t")
                for pair in range(2):
                    ts_ = [grp * 4 + 2 * pair, grp * 4 + 2 * pair + 1]
                    for t in ts_:
                        wev = [("t", t_mm1[(span, grp)])]
                        if span == 0:
                            wev += h2_gate[t]
                        pl.op("scalar", lambda s, t=t: s.activation(
                            OSB[:, t % 2, :], PS[t][:, 0:MS1], AF.Gelu,
                            scale=tc(t, 8)), waits=wev)
                    for t in ts_:
                        pl.op("scalar", lambda s, t=t, span=span:
                              s.activation(
                                  h2_tile(t)[:, span * MS1:(span + 1) * MS1],
                                  OSB[:, t % 2, :], AF.Gelu),
                              incs=[pl.inc("sc")])
                        gelu2_sc[(span, t)] = pl.val("sc")
                ev1[(span, grp)] = pl.val("sc")
            for t in range(NT):
                h2s = h2_tile(t)[:, span * MS1:(span + 1) * MS1]
                acc = tc(t, 10) if span == 0 else tc(t, 11)
                pl.op("vector", lambda v, h2s=h2s, acc=acc:
                      v.scalar_tensor_tensor(JNK[:], h2s, 1.0, h2s,
                                             OP.mult, OP.mult,
                                             accum_out=acc),
                      waits=[("sc", gelu2_sc[(span, t)])])
            for t in range(NT):
                h2s = h2_tile(t)[:, span * MS1:(span + 1) * MS1]
                dstm = tc(t, 9) if span == 0 else tc(t, 2)
                pl.op("vector", lambda v, h2s=h2s, dstm=dstm:
                      v.tensor_reduce(dstm, h2s, AX.X, OP.max,
                                      apply_absolute_value=True))
            if span > 0:
                for t in range(NT):
                    pl.op("vector", lambda v, t=t: v.tensor_tensor(
                        tc(t, 10), tc(t, 10), tc(t, 11), OP.add))
                for t in range(NT):
                    pl.op("vector", lambda v, t=t: v.tensor_tensor(
                        tc(t, 9), tc(t, 9), tc(t, 2), OP.max))
        pl.op("vector", lambda v: v.memset(col(cT2), 0.0),
              incs=[pl.inc("v")])
        v_h2stats = pl.val("v")

        tappoint("h2", [(h2_tile(t)[:, 0:2048],
                         [("sc", gelu2_sc[(NSPAN1 - 1, t)])])
                        for t in range(NT)])

        v_w2s = finalize_mean("w2", cM2, cSw2, cM2o127,
                              extra_mm_waits=[("sc", ev1[(NSPAN1 - 1, 0)])])

        q2_done = {}
        tr_xt2 = {}
        mm1_done = ("t", t_mm1[(NSPAN1 - 1, 1)])
        for t in range(NT):
            w0_ = [("v", v_h2stats), ("v", v_w2s)]
            vchain(lambda v, t=t: v.tensor_scalar(
                tc(t, 3), tc(t, 10), 1e-24, None, OP.max), extra_waits=w0_)
            vchain(lambda v, t=t: v.reciprocal(tc(t, 3), tc(t, 3)),
                   incs=[pl.inc("v")])
            vr = pl.val("v")
            pl.op("scalar", lambda s, t=t: s.activation(
                tc(t, 5), tc(t, 3), AF.Sqrt, scale=float(INNER)),
                waits=[("v", vr)], incs=[pl.inc("sc")])
            scq = pl.val("sc")
            vchain(lambda v, t=t: v.tensor_tensor(
                tc(t, 6), tc(t, 9), tc(t, 5), OP.mult),
                extra_waits=[("sc", scq)])
            vchain(lambda v, t=t: v.tensor_scalar(
                tc(t, 6), tc(t, 6), EPS_Q, None, OP.max))
            vchain(lambda v, t=t: v.reciprocal(tc(t, 7), tc(t, 6)))
            vchain(lambda v, t=t: v.tensor_tensor(
                tc(t, 8), tc(t, 6), col(cM2o127), OP.mult))
            vchain(lambda v, t=t: v.tensor_scalar(
                tc(t, 7), tc(t, 7), 127.0, None, OP.mult))
            vchain(lambda v, t=t: v.tensor_tensor(
                tc(t, 7), tc(t, 7), tc(t, 5), OP.mult),
                incs=[pl.inc("v")])
            v_s2t = pl.val("v")
            eng = "vector" if t % 2 == 0 else "gpsimd"
            esem = "v" if t % 2 == 0 else "g"
            fstage = F1 if t % 2 == 0 else F3
            wv0 = [("v", v_s2t)]
            if t % 2 == 0:
                wv0.append(("g", g_tern1[2 * NSPAN1 - 1]))
            else:
                wv0.append(("g", g_tern1[2 * NSPAN1 - 2]))
            for ch in range(4):
                h2c = h2_tile(t)[:, ch * D:(ch + 1) * D]
                pl.op(eng, lambda e, h2c=h2c, fstage=fstage, t=t:
                      e.tensor_scalar(fstage[:], h2c, tc(t, 7), col(cC),
                                      OP.mult, OP.add),
                      waits=wv0 if ch == 0 else [])
                gap(eng)
                pl.op(eng, lambda e, h2c=h2c, fstage=fstage:
                      e.tensor_scalar(h2c, fstage[:], col(cOne), col(cNegC),
                                      OP.mult, OP.add),
                      incs=[pl.inc(esem)] if ch == 3 else [])
            q2_done[t] = (esem, pl.val(esem))
            wv = [q2_done[t]]
            if t < 2:
                wv.append(mm1_done)
            else:
                wv.append(tr_xt2[t - 2])
            tr_xt2[t] = dma("t2A" if t % 2 == 0 else "t2B",
                            lambda s, t=t: s.dma_start_transpose(
                                xt2_tile(t), h2_tile(t).rearrange(
                                    "p (c f) -> p c f", c=64)), waits=wv,
                            eng="scalar")
        xt2_all = [tr_xt2[NT - 1], tr_xt2[NT - 2]]

        tern2_seq = {}
        cvt2_sc = {}
        tr_w2p = {}
        t_mm2 = {}
        copy_sc = {}
        tstg_free = {}
        ob_last = {}
        stt_v = {}
        nst = [0]
        store_led = {}
        xps_loaded = {}
        gelu_all = ("sc", gelu2_sc[(NSPAN1 - 1, NT - 1)])

        def mm2_prep(slab):
            wxp = [sp_all]
            if slab < 2:
                wxp.append(mm1_done)
            if slab >= 2:
                wxp.append(stt_v[slab - 2])
            xps_loaded[slab] = dma(
                "xpA" if slab % 2 == 0 else "xpB",
                lambda s, slab=slab: s.dma_start(
                    out=XPS[slab % 2][:],
                    in_=xp_hbm[slab].transpose([1, 0, 2])), waits=wxp)
            for p4 in range(4):
                piece = slab * 4 + p4
                stg = [F1, F3, F2][piece % 3]
                eng = "vector" if piece % 2 == 0 else "gpsimd"
                esem = "v" if eng == "vector" else "g"
                wv = []
                if piece >= 3:
                    wv.append(("sc", cvt2_sc[piece - 3]))
                elif piece < 2:
                    wv.append(q2_done[6 if piece % 2 == 0 else 7])
                else:
                    wv.append(("sc", mean_sc[("w2", 127)]))
                ld = dma(["s2A", "s2B", "s2C"][piece % 3],
                         lambda s, slab=slab, p4=p4, stg=stg: s.dma_start(
                             out=stg[:],
                             in_=w2_in[slab * P:(slab + 1) * P,
                                       p4 * D:(p4 + 1) * D]), waits=wv)
                wc = [ld]
                if piece < 2:
                    wc.append(tr_w1[2 * NSPAN1 - 1])
                    wc.append(tr_w1[2 * NSPAN1 - 2])
                if piece >= 2:
                    wc.append(tr_w2p[piece - 2])
                pl.op("scalar", lambda s, stg=stg, piece=piece: s.copy(
                    PQ[piece % 2], stg[:]),
                    waits=wc, incs=[pl.inc("sc")])
                cvt2_sc[piece] = pl.val("sc")
                wtr = []
                if slab < 2 and p4 < 2:
                    wtr += xt2_all
                if slab >= 2:
                    wtr.append(("t", t_mm2[(slab - 2, 1)]))
                if slab >= 1:
                    wtr.append(tr_w2p[(slab - 1) * 4 + p4])
                tr_w2p[piece] = dma(
                    f"w2p{p4}",
                    lambda s, slab=slab, p4=p4, piece=piece:
                    s.dma_start_transpose(
                        W2T[slab % 2][:, p4 * 16:(p4 + 1) * 16, :],
                        PQ[piece % 2].rearrange("p (c f) -> p c f", c=16)),
                    waits=wtr, eng="scalar")
                q4 = ar8(8192 * (slab % 2) + p4 * 2048,
                         8192 * (slab % 2) + (p4 + 1) * 2048)
                wt0 = [tr_w2p[piece]]
                if eng == "gpsimd" and slab == 0:
                    wt0.append(("v", v_w2s))
                pl.op(eng, lambda e, q4=q4: e.tensor_scalar(
                    q4, q4, col(cSw2), col(cC), OP.mult, OP.add),
                    waits=wt0)
                gap(eng)
                pl.op(eng, lambda e, q4=q4: e.tensor_scalar(
                    q4, q4, col(cCp1), col(cCm1), OP.min, OP.max))
                gap(eng)
                pl.op(eng, lambda e, q4=q4: e.tensor_scalar(
                    q4, q4, col(cOne), col(cNegC),
                    OP.mult, OP.add),
                    incs=[pl.inc(esem)])
                tern2_seq[piece] = (esem, pl.val(esem))

        def mm2_mme(slab):
            for half in range(2):
                bk = 2 * (slab % 2) + half
                wts = [xt2_all[0], xt2_all[1]]
                if slab < 2:
                    wts.append(gelu_all)
                    wts.append(("v", v_w2s))
                if (slab - 2, half) in copy_sc:
                    wts.append(("sc", copy_sc[(slab - 2, half)]))
                for c in range(64):
                    w = []
                    if c == 0:
                        w = wts + [tern2_seq[slab * 4]]
                    elif c % 16 == 0:
                        w = [tern2_seq[slab * 4 + c // 16]]
                    pl.op("tensor", lambda e, c=c, slab=slab, half=half,
                          bk=bk: e.matmul(
                              PS[bk][:],
                              W2T[slab % 2][:, c, :],
                              XT2V[:, 4 * half:4 * half + 4, c, :],
                              start=(c == 0), stop=(c == 63)),
                          waits=w,
                          incs=[pl.inc("t")] if c == 63 else [])
                t_mm2[(slab, half)] = pl.val("t")
                s_idx = (2 * slab + half) % 4
                wcp = [("t", t_mm2[(slab, half)])]
                if (s_idx,) in ob_last:
                    wcp.append(ob_last[(s_idx,)])
                pl.op("scalar", lambda s, bk=bk, s_idx=s_idx: s.copy(
                    MSTG[:, s_idx].rearrange("p c f -> p (c f)"),
                    PS[bk][:]),
                    waits=wcp, incs=[pl.inc("sc")])
                copy_sc[(slab, half)] = pl.val("sc")
                wob = [("sc", copy_sc[(slab, half)])]
                if (s_idx,) in tstg_free:
                    wob.append(tstg_free[(s_idx,)])
                ob = dma(f"ob{s_idx}",
                         lambda s, s_idx=s_idx: s.dma_start_transpose(
                             TSTG[:, s_idx], MSTG[:, s_idx]),
                         waits=wob, eng="scalar")
                ob_last[(s_idx,)] = ob
                for ti in range(4):
                    t = 4 * half + ti
                    n = nst[0]
                    slot = n % 4
                    stsem = f"st{slot}"
                    oslot = OSB[:, slot // 2, (slot % 2) * 128:
                                (slot % 2) * 128 + 128]
                    wev = [ob, xps_loaded[slab]]
                    if (slot, "last") in store_led:
                        wev.append(store_led[(slot, "last")])
                    pl.op("vector", lambda v, t=t, slab=slab, oslot=oslot,
                          s_idx=s_idx, ti=ti: v.scalar_tensor_tensor(
                              oslot, TSTG[:, s_idx, ti, :], tc(t, 8),
                              XPS[slab % 2][:, t, :], OP.mult, OP.add),
                          waits=wev, incs=[pl.inc("v")])
                    vev = pl.val("v")
                    st = dma(stsem, lambda s, t=t, slab=slab, oslot=oslot:
                             s.dma_start(
                                 out=out_hbm[t * P:(t + 1) * P,
                                             slab * DK:(slab + 1) * DK],
                                 in_=oslot), waits=[("v", vev)],
                             eng="scalar")
                    store_led[(slot, "last")] = st
                    nst[0] += 1
                tstg_free[(s_idx,)] = ("v", pl.val("v"))
            stt_v[slab] = ("v", pl.val("v"))

        for slab in range(NSLAB2 + 1):
            if slab < NSLAB2:
                mm2_prep(slab)
            if slab >= 1:
                mm2_mme(slab - 1)

        pl.op("sync", lambda s: s.nop(),
              waits=[(f"st{i}", pl.val(f"st{i}")) for i in range(4)])

        if hasattr(pl, "_tap_cut"):
            pl.ops = pl.ops[:pl._tap_cut]

        @block.sync
        def _(sync):
            pl.emit("sync", sync, handles)

        @block.vector
        def _(vector):
            pl.emit("vector", vector, handles)

        @block.scalar
        def _(scalar):
            pl.emit("scalar", scalar, handles)

        @block.gpsimd
        def _(gpsimd):
            pl.emit("gpsimd", gpsimd, handles)

        @block.tensor
        def _(tensor):
            pl.emit("tensor", tensor, handles)

    return nc


_NC_CACHE = []


def kernel(L0_attn_x, L0_attn_Wq, L0_attn_Wk, L0_attn_Wv, L0_attn_Wo,
           L0_ffn_w1, L0_ffn_w2):
    x = np.ascontiguousarray(np.asarray(L0_attn_x, dtype=np.float32)
                             .reshape(N_CORES * TOK, D))
    wq = np.ascontiguousarray(np.asarray(L0_attn_Wq, dtype=np.float32))
    wk = np.ascontiguousarray(np.asarray(L0_attn_Wk, dtype=np.float32))
    wv = np.ascontiguousarray(np.asarray(L0_attn_Wv, dtype=np.float32))
    wo = np.ascontiguousarray(np.asarray(L0_attn_Wo, dtype=np.float32))
    w1 = np.ascontiguousarray(np.asarray(L0_ffn_w1, dtype=np.float32))
    w2 = np.ascontiguousarray(np.asarray(L0_ffn_w2, dtype=np.float32))

    if not _NC_CACHE:
        _NC_CACHE.append(build_program())
    nc = _NC_CACHE[0]

    in_maps = []
    for c in range(N_CORES):
        in_maps.append({
            "x_in": x[c * TOK:(c + 1) * TOK],
            "wq_in": wq, "wk_in": wk, "wv_in": wv, "wo_in": wo,
            "w1_in": w1, "w2_in": w2,
        })
    res = run_bass_kernel_spmd(nc, in_maps, core_ids=list(range(N_CORES)))
    out = np.concatenate([np.asarray(r["out"]) for r in res.results], axis=0)
    return out.reshape(1, N_CORES * TOK, D).astype(np.float32)
